# version 13
# baseline (speedup 1.0000x reference)
import numpy as np
import ml_dtypes
import concourse.bass as bass
import concourse.mybir as mybir
from concourse.bass_utils import run_bass_kernel_spmd

F32 = mybir.dt.float32
BF16 = mybir.dt.bfloat16
AF = mybir.ActivationFunctionType
ALU = mybir.AluOpType
AX = mybir.AxisListType

B, S, D = 2, 8192, 1024
H, DH = 16, 64
DKV = 512
NQ = 2048
NCORE = 8
NG = 8
NCH = 4
NK = 64
QC = 512
EPS = 1e-5
DEBUG_TAPS = False
VST = 193
VPITCH = NK * VST


def _rope_tables():
    ROPE_BASE = 10000.0
    inv_freq = 1.0 / ROPE_BASE ** (np.arange(0, DH, 2, dtype=np.float32) / DH)
    t = np.arange(S, dtype=np.float32)
    freqs = np.outer(t, inv_freq)
    cos = np.concatenate([np.cos(freqs), np.cos(freqs)], -1)[:, ::2]
    sin = np.concatenate([np.sin(freqs), np.sin(freqs)], -1)[:, ::2]
    return cos, sin


def _host_prepare(inp):
    bf = ml_dtypes.bfloat16
    perm = np.zeros(D, dtype=np.int64)
    for h0 in range(H):
        perm[64 * h0:64 * h0 + 32] = np.arange(0, 64, 2) + 64 * h0
        perm[64 * h0 + 32:64 * h0 + 64] = np.arange(1, 64, 2) + 64 * h0

    x = np.ascontiguousarray(np.asarray(inp["L0_attn_x"], dtype=np.float32))

    def fm(v):
        return np.ascontiguousarray(np.asarray(v, np.float32).reshape(8, 128).T)

    cos, sin = _rope_tables()
    trig_c = np.ascontiguousarray(np.tile(cos.T, (4, 1)).astype(bf))
    trig_s = np.ascontiguousarray(np.tile(sin.T, (4, 1)).astype(bf))

    ident = np.eye(128, dtype=np.float32)
    bmask = np.zeros((128, 128), dtype=np.float32)
    bmask[0, :] = 1.0
    bmask[64, 0:64] = 1.0
    bmask[32, 64:128] = 1.0
    mrot = np.zeros((128, 128), dtype=np.float32)
    for blk in range(2):
        o = 64 * blk
        for i in range(32):
            mrot[o + 32 + i, o + i] = -1.0
            mrot[o + i, o + 32 + i] = 1.0
    emat = np.zeros((8, 64), dtype=np.float32)
    mmat = np.zeros((8, 64), dtype=np.float32)
    for e in range(8):
        for r in range(8):
            emat[e, 8 * e + r] = 2.0
            mmat[r, 8 * e + r] = 1.0

    shared = {
        "w_dq_t": np.ascontiguousarray(np.asarray(inp["L0_attn_w_down_q"]).T),
        "w_dkv_t": np.ascontiguousarray(np.asarray(inp["L0_attn_w_down_kv"]).T),
        "w_uq_t": np.ascontiguousarray(np.asarray(inp["L0_attn_w_up_q"])[perm].T),
        "w_uk_t": np.ascontiguousarray(np.asarray(inp["L0_attn_w_up_k"])[perm].T),
        "w_uv_t": np.ascontiguousarray(np.asarray(inp["L0_attn_w_up_v"]).T),
        "w_out_t": np.ascontiguousarray(np.asarray(inp["L0_attn_w_out"]).T),
        "base_w_t": np.ascontiguousarray(np.asarray(inp["L0_ffn_base_w"]).T),
        "lora_A": np.ascontiguousarray(np.asarray(inp["L0_ffn_lora_A"])),
        "lora_Bp": np.ascontiguousarray(
            np.asarray(inp["L0_ffn_lora_B"]).reshape(64, D)),
        "rout_t": np.ascontiguousarray(np.asarray(inp["L0_ffn_router_w"]).T),
        "ln_g_fm": fm(inp["L0_attn_ln_scale"]),
        "ln_b_fm": fm(inp["L0_attn_ln_bias"]),
        "base_b_fm": fm(inp["L0_ffn_base_b"]),
        "rout_b": np.ascontiguousarray(
            np.asarray(inp["L0_ffn_router_b"], np.float32).reshape(1, 8)),
        "trig_c": trig_c, "trig_s": trig_s,
        "ident_f": ident, "ident_bf": ident.astype(bf),
        "bmask": bmask, "mrot": mrot.astype(bf),
        "emat": emat.astype(bf), "mmat": mmat.astype(bf),
    }
    in_maps = []
    for c in range(NCORE):
        bb = c // 4
        q0 = (c % 4) * NQ
        m = dict(shared)
        m["xb"] = np.ascontiguousarray(np.roll(x[bb], -q0, axis=0))
        m["xq"] = np.ascontiguousarray(x[bb, q0:q0 + NQ])
        m["trig_c"] = np.ascontiguousarray(np.roll(trig_c, -q0, axis=1))
        m["trig_s"] = np.ascontiguousarray(np.roll(trig_s, -q0, axis=1))
        m["trig_cq"] = np.ascontiguousarray(trig_c[:, q0:q0 + NQ])
        m["trig_sq"] = np.ascontiguousarray(trig_s[:, q0:q0 + NQ])
        in_maps.append(m)
    return in_maps


class Sched:

    ENGINES = ("sync", "gpsimd", "vector", "scalar", "tensor")

    def __init__(self):
        self.ops = {e: [] for e in self.ENGINES}
        self.cnt = {}
        self.events = {}

    ENGSEM = {"vector": "V", "scalar": "A"}

    def emit(self, engine, fn, waits=(), inc=None, ev=None):
        rw = list(waits)
        es = self.ENGSEM.get(engine)
        if es is not None:
            if inc is None:
                inc = (es, 1)
            if inc[0] == es:
                rw.append((es, self.cnt.get(es, 0)))
        if inc is not None:
            sem, k = inc
            self.cnt[sem] = self.cnt.get(sem, 0) + k
            if ev is not None:
                self.events[ev] = (sem, self.cnt[sem])
        self.ops[engine].append((fn, tuple(rw), inc))

    def dma(self, queue, out_ap, in_ap, waits=(), ev=None, sem=None):
        assert sem is not None
        self.emit(queue, lambda e, o=out_ap, i=in_ap: e.dma_start(out=o, in_=i),
                  waits=waits, inc=(sem, 16), ev=ev)

    def replay(self, engine_name, eng, semobjs):
        for fn, waits, inc in self.ops[engine_name]:
            best = {}
            for w in waits:
                if isinstance(w, tuple):
                    sem, val = w[0][1:] if w[0].startswith("#") else w[0], w[1]
                else:
                    sem, val = self.events[w]
                if val > best.get(sem, -1):
                    best[sem] = val
            for sem, val in best.items():
                eng.wait_ge(semobjs[sem], val)
            instr = fn(eng)
            if inc is not None:
                sem, k = inc
                instr.then_inc(semobjs[sem], k)


class Res:
    def __init__(self):
        self.last = {}

    def acquire(self, key):
        ev = self.last.get(key)
        return [ev] if ev is not None else []

    def release(self, key, ev):
        self.last[key] = ev


def build_nc():
    nc = bass.Bass("TRN2", target_bir_lowering=False)
    dt = nc.dram_tensor

    din = {}
    for name, shape, dty in [
            ("xb", [S, D], F32), ("xq", [NQ, D], F32),
            ("w_dq_t", [D, DKV], F32), ("w_dkv_t", [D, DKV], F32),
            ("w_uq_t", [DKV, D], F32), ("w_uk_t", [DKV, D], F32),
            ("w_uv_t", [DKV, D], F32), ("w_out_t", [D, D], F32),
            ("base_w_t", [D, D], F32), ("lora_A", [D, 8], F32),
            ("lora_Bp", [64, D], F32), ("rout_t", [D, 8], F32),
            ("ln_g_fm", [128, 8], F32), ("ln_b_fm", [128, 8], F32),
            ("base_b_fm", [128, 8], F32), ("rout_b", [1, 8], F32),
            ("trig_c", [128, S], BF16), ("trig_s", [128, S], BF16),
            ("trig_cq", [128, NQ], BF16), ("trig_sq", [128, NQ], BF16),
            ("ident_f", [128, 128], F32), ("ident_bf", [128, 128], BF16),
            ("bmask", [128, 128], F32), ("mrot", [128, 128], BF16),
            ("emat", [8, 64], BF16), ("mmat", [8, 64], BF16)]:
        din[name] = dt(name, shape, dty, kind="ExternalInput").ap()

    out_d = dt("out", [NQ, D], F32, kind="ExternalOutput").ap()
    _dbg = "ExternalOutput" if DEBUG_TAPS else "Internal"
    ckv_h = dt("ckv_h", [DKV, S], BF16, kind=_dbg).ap()
    xnt_h = dt("xnt_h", [D, NQ], BF16, kind=_dbg).ap()
    attn_h = dt("attn_h", [D, NQ], BF16, kind=_dbg).ap()
    g_h = dt("g_h", [D, NQ], F32, kind=_dbg).ap()
    kt_h = dt("kt_h", [2 * 128, S], BF16).ap()
    vs_h = dt("vs_h", [2 * 128, NK * 128], BF16).ap()
    if DEBUG_TAPS:
        dbg_kt = dt("dbg_kt", [128, S], BF16, kind="ExternalOutput").ap()
        dbg_qt = dt("dbg_qt", [128, NQ], BF16, kind="ExternalOutput").ap()
        dbg_vs = dt("dbg_vs", [128, VPITCH], BF16, kind="ExternalOutput").ap()
        dbg_cq = dt("dbg_cq", [128, 8192], BF16, kind="ExternalOutput").ap()

    from contextlib import ExitStack
    ctx = ExitStack()
    ent = ctx.enter_context

    sbt = {}

    def T(name, shape, dty):
        sbt[name] = ent(nc.sbuf_tensor(name, shape, dty))
        return sbt[name]

    st_nm1 = T("st_nm1", [128, 80], F32)
    st_r1 = T("st_r1", [128, 80], F32)
    st_s = T("st_s", [128, 80], F32)
    st_q = T("st_q", [128, 80], F32)
    st_nm2 = T("st_nm2", [128, 80], F32)
    st_r2 = T("st_r2", [128, 80], F32)
    st_v = T("st_v", [128, 80], F32)
    ident_f = T("identf", [128, 128], F32)
    ident_bf = T("identb", [128, 128], BF16)
    bmask = T("bmaskt", [128, 128], F32)
    mrot = T("mrott", [128, 128], BF16)
    emat = T("ematt", [8, 64], BF16)
    mmat = T("mmatt", [8, 64], BF16)
    vecs = T("vecs", [128, 64], F32)
    probs = T("probs", [128, 3072], BF16)
    rec_sb = T("rec_sb", [128, 1024], F32)
    bc_sb = T("bc_sb", [128, 512], F32)
    ao_sb = T("ao_sb", [128, 1024], BF16)
    av_sb = T("av_sb", [128, 1024], F32)
    ckv_in = T("ckv_in", [128, 6144], BF16)
    ab_buf = T("ab_buf", [128, 2048], BF16)
    trig_r = T("trig_r", [128, 4096], BF16)
    wu_d = T("wu_d", [128, 2 * 1536], BF16)
    kt_res = T("kt_res", [128, NK * 128], BF16)
    vs_r = T("vs_r", [128, 16 * VST], BF16)
    kts = T("kts", [128, 2 * 512], BF16)
    vss = T("vss", [128, 2 * 512], BF16)
    xnts_t = T("xnts_t", [128, 2048], BF16)
    qt_d = T("qt_d", [128, 2 * NQ], BF16)
    sqd = T("sqd", [128, 1024], BF16)
    stage = T("stage", [128, 1024], F32)
    stage2 = T("stage2", [128, 1024], F32)
    cst = T("cst", [128, 4], F32)
    beta_bf = T("beta_bf", [128, 8], BF16)

    AR = 26624
    arena = T("arena", [128, AR], F32)
    ar = arena[:, :]
    arb = ar.bitcast(BF16)
    wbeta = arb[:, 26624:34816]

    def x_ring(i):
        return ar[:, 1024 * (i % 8):1024 * (i % 8) + 1024]

    wstage = stage[:, :]

    def wdqp(k):
        return arb[:, 16384 + 512 * k:16384 + 512 * (k + 1)]

    def wdkvp(k):
        return arb[:, 20480 + 512 * k:20480 + 512 * (k + 1)]

    def xnh_ring(i):
        return arb[:, 24576 + 1024 * (i % 2):24576 + 1024 * (i % 2) + 1024]

    def xnb_ring(i):
        return arb[:, 47104 + 1024 * (i % 4):47104 + 1024 * (i % 4) + 1024]

    def xhT_buf(cc):
        o = 26624 + 4096 * (cc % 2)
        return arb[:, o:o + 4096]

    def ckv_sb(cc):
        o = 34816 + 2048 * (cc % 2)
        return arb[:, o:o + 2048]

    cq_keep = arb[:, 38912:47104]

    def xnts(i):
        return xnts_t[:, 1024 * (i % 2):1024 * (i % 2) + 1024]

    ffn_basew = arb[:, 0:8192]

    def ffn_xnt(k):
        return arb[:, 8192 + 512 * k:8192 + 512 * (k + 1)]

    ffn_loraA = arb[:, 12288:12352]
    ffn_loraB = arb[0:64, 12352:13376]
    ffn_routw = arb[:, 13376:13440]
    ffn_logits = ar[:, 6720:6848]
    ffn_gat = arb[:, 13696:13824]
    ffn_gt = arb[0:8, 13824:15872]
    ffn_midt = arb[0:8, 15872:17920]
    ffn_grep = arb[0:64, 17920:18432]
    ffn_gm = arb[0:64, 18432:20480]
    ffn_lsb = ar[:, 10240:10752]

    def ffn_u(i):
        return ar[:, 10752 + 512 * (i % 2):10752 + 512 * (i % 2) + 512]

    ffn_sig = ar[:, 11776:12288]

    def ffn_gout(i):
        return ar[:, 12288 + 512 * (i % 2):12288 + 512 * (i % 2) + 512]

    wstage_f = stage[:, :]

    def tl_attn_r(i):
        return arb[:, 4096 * (i % 2):4096 * (i % 2) + 4096]

    tl_wout = arb[:, 8192:16384]
    def tl_gr(i):
        return ar[:, 8192 + 4096 * (i % 2):8192 + 4096 * (i % 2) + 4096]
    tl_wstage = stage[:, :]

    def tl_sT(i):
        return ar[:, 16384 + 4096 * (i % 2):16384 + 4096 * (i % 2) + 4096]

    tl_x = ar[:, 24576:25600]
    tl_s1 = ar[:, 25600:26624]
    tl_s2 = stage2[:, :]

    ps_t = ent(nc.psum_tensor("ps", [128, 4096], F32))
    ps = ps_t[:, :]
    psb = ps.bitcast(BF16)

    def bank(b3):
        return ps[:, 512 * b3:512 * (b3 + 1)]

    def bank_bf(b3):
        return psb[:, 1024 * b3:1024 * (b3 + 1)]

    semnames = (["V", "A", "P", "QK", "EX", "AVT", "ph0", "stg", "tlx",
                 "outd", "dbg"] +
                [f"xld{i}" for i in range(8)] +
                [f"trg{i}" for i in range(4)] +
                [f"kin{i}" for i in range(12)] +
                [f"fxd{i}" for i in range(8)] +
                [f"tla{i}" for i in range(2)] +
                [f"tlg{i}" for i in range(2)] +
                [f"sb{i}" for i in range(2)] +
                [f"ks{i}" for i in range(16)] +
                [f"vs{i}" for i in range(16)] +
                ["stg2", "kto0", "kto1", "vso0", "vso1", "xt0", "xt1"] +
                [f"xlq{i}" for i in range(8)] +
                [f"gd{i}" for i in range(2)] +
                [f"ao{i}" for i in range(2)])
    semobjs = {n: ent(nc.semaphore(n)) for n in semnames}

    sc = Sched()
    banks = Res()
    rings = Res()
    sc.emit("vector", lambda e: e.memset(cst[:, 0:1], EPS), inc=("V", 1),
            ev="cst_ready")

    def statA_ap(sl):
        return vs_r[:, VST * sl:VST * sl + 65]

    def statB_ap(sl):
        return vs_r[:, VST * sl + 65:VST * (sl + 1)]

    for dst, src in [(ident_f[:, :], din["ident_f"]),
                     (ident_bf[:, :], din["ident_bf"]),
                     (bmask[:, :], din["bmask"]), (mrot[:, :], din["mrot"]),
                     (emat[:, :], din["emat"]), (mmat[:, :], din["mmat"]),
                     (vecs[:, 0:8], din["ln_g_fm"]),
                     (vecs[:, 8:16], din["ln_b_fm"]),
                     (vecs[:, 16:24], din["base_b_fm"]),
                     (vecs[0:1, 32:40], din["rout_b"])]:
        sc.dma("sync", dst, src, ev="ph0last", sem="ph0")

    def conv(name, dram, nkt, cols, dst_fn):
        for k in range(nkt):
            w = rings.acquire(("stage",)) + (["ph0last"] if k == 0 else [])
            sc.dma("sync", wstage[:, 0:cols], dram[128 * k:128 * (k + 1), :],
                   waits=w, ev=f"wl_{name}_{k}", sem="stg")
            fn = (lambda dk, c: lambda e: e.tensor_copy(dk, wstage[:, 0:c]))(
                dst_fn(k), cols)
            sc.emit("vector", fn, waits=[f"wl_{name}_{k}"], inc=("V", 1),
                    ev=f"wc_{name}_{k}")
            rings.release(("stage",), f"wc_{name}_{k}")

    conv("dq", din["w_dq_t"], 8, 512, lambda k: wbeta[:, 512 * k:512 * (k + 1)])
    conv("dkv", din["w_dkv_t"], 8, 512,
         lambda k: wbeta[:, 4096 + 512 * k:4096 + 512 * (k + 1)])
    sc.emit("vector", lambda e: e.tensor_copy(beta_bf[:, :], vecs[:, 8:16]),
            waits=["ph0last"], inc=("V", 1), ev="beta_bf")

    for which, base, vcol in (("q", 0, 24), ("kv", 4096, 28)):
        for m in range(4):
            for k in range(8):
                wt = []
                if k == 0:
                    wt = banks.acquire(4)
                    if m == 0:
                        wt = wt + [f"wc_d{which}_7", "beta_bf"]
                fn = (lambda ba, mm, kk: lambda e: e.matmul(
                    bank(4)[:, mm:mm + 1],
                    wbeta[:, ba + 512 * kk + 128 * mm:
                          ba + 512 * kk + 128 * (mm + 1)],
                    beta_bf[:, kk:kk + 1], start=(kk == 0),
                    stop=(kk == 7)))(base, m, k)
                ev = f"biasmm_{which}" if (m == 3 and k == 7) else None
                sc.emit("tensor", fn, waits=wt, inc=("P", 1) if ev else None,
                        ev=ev)
        fn = (lambda vc: lambda e: e.tensor_copy(
            vecs[:, vc:vc + 4], bank(4)[:, 0:4]))(vcol)
        sc.emit("vector", fn, waits=[f"biasmm_{which}"], inc=("V", 1),
                ev=f"biasev_{which}")
        banks.release(4, f"biasev_{which}")

    for which, base, dstf in (("q", 0, wdqp), ("kv", 4096, wdkvp)):
        for k in range(8):
            fn = (lambda ba, kk, df: lambda e: e.tensor_scalar(
                df(kk), wbeta[:, ba + 512 * kk:ba + 512 * (kk + 1)],
                vecs[:, kk:kk + 1], None, ALU.mult))(base, k, dstf)
            sc.emit("vector", fn,
                    waits=[f"biasev_{which}"] if k == 0 else (),
                    inc=("V", 1), ev=f"wsc_{which}_{k}")

    sc.emit("tensor", lambda e: e.matmul(
        bank(4)[:, 0:8], bmask[0:1, 0:128], vecs[0:1, 32:40],
        start=True, stop=True),
        waits=["ph0last"] + banks.acquire(4), inc=("P", 1), ev="rbmm")
    sc.emit("vector", lambda e: e.tensor_copy(vecs[:, 40:48], bank(4)[:, 0:8]),
            waits=["rbmm"], inc=("V", 1), ev="rbev")
    banks.release(4, "rbev")

    sc.emit("vector", lambda e: e.memset(vs_r[:, :], 0.0), inc=("V", 1))
    on64 = bass.AP(sbt["vs_r"], 64, [[16 * VST, 128], [VST, 16]])
    on97 = bass.AP(sbt["vs_r"], 97, [[16 * VST, 128], [VST, 16]])
    sc.emit("vector", (lambda a: lambda e: e.memset(a, 1.0))(on64),
            inc=("V", 1))
    sc.emit("vector", (lambda a: lambda e: e.memset(a, 1.0))(on97),
            inc=("V", 1), ev="vsr_init")

    turn = [0]
    slot_free = {}

    def take_turn(free_ev):
        s = turn[0] % 3
        turn[0] += 1
        w = [slot_free[s]] if s in slot_free else []
        slot_free[s] = free_ev
        return s, w

    slot_free[2] = "rbev"

    last_ex_ev = [None]

    def ph1_gen():
        def phA(cc):
            t0 = 4 * cc
            for j in range(4):
                ti = t0 + j
                qd = "gpsimd" if ti % 2 == 0 else "sync"
                sc.dma(qd, x_ring(ti), din["xb"][128 * ti:128 * (ti + 1), :],
                       waits=rings.acquire(("x", ti % 8)) +
                       rings.acquire(("x2", ti % 8)), ev=f"xld_{ti}",
                       sem=f"xld{ti % 8}")

        def phB(cc):
            is_qc = cc < 4
            t0 = 4 * cc
            for j in range(4):
                ti = t0 + j
                sc.emit("vector", (lambda t: lambda e: e.reduce_sum(
                    st_s[:, t:t + 1], x_ring(t), axis=AX.X))(ti),
                    waits=[f"xld_{ti}"], inc=("V", 1), ev=f"xsum_{ti}")
                sc.emit("scalar", (lambda t: lambda e: e.activation(
                    sqd[:, 0:1024], x_ring(t), AF.Square,
                    accum_out=st_q[:, t:t + 1]))(ti),
                    waits=[f"xld_{ti}"], inc=("A", 1), ev=f"sq1_{ti}")
            c4 = slice(4 * cc, 4 * cc + 4)
            sc.emit("vector", (lambda c: lambda e: e.tensor_scalar(
                st_nm1[:, c], st_s[:, c], -1.0 / 1024, None,
                ALU.mult))(c4), inc=("V", 1))
            sc.emit("vector", (lambda c: lambda e: e.tensor_tensor(
                st_v[:, c], st_nm1[:, c], st_nm1[:, c], ALU.mult))(c4),
                inc=("V", 1))
            sc.emit("vector", (lambda c: lambda e: e.scalar_tensor_tensor(
                st_v[:, c], st_q[:, c], 1.0 / 1024, st_v[:, c], ALU.mult,
                ALU.subtract))(c4), waits=[f"sq1_{t0 + 3}"], inc=("V", 1),
                ev=f"var1_{cc}")
            sc.emit("scalar", (lambda c: lambda e: e.activation(
                st_r1[:, c], st_v[:, c], AF.Ln, bias=cst[:, 0:1]))(c4),
                waits=[f"var1_{cc}"], inc=("A", 1))
            sc.emit("scalar", (lambda c: lambda e: e.activation(
                st_r1[:, c], st_r1[:, c], AF.Exp, scale=-0.5))(c4),
                inc=("A", 1), ev=f"rstd1_{cc}")
            sc.emit("vector", (lambda c: lambda e: e.tensor_tensor(
                st_nm2[:, c], st_r1[:, c], st_r1[:, c], ALU.mult))(c4),
                waits=[f"rstd1_{cc}"], inc=("V", 1))
            sc.emit("vector", (lambda c: lambda e: e.tensor_tensor(
                st_nm2[:, c], st_nm2[:, c], st_v[:, c], ALU.mult))(c4),
                inc=("V", 1), ev=f"var2_{cc}")
            sc.emit("scalar", (lambda c: lambda e: e.activation(
                st_r2[:, c], st_nm2[:, c], AF.Ln, bias=cst[:, 0:1]))(c4),
                waits=[f"var2_{cc}"], inc=("A", 1))
            sc.emit("scalar", (lambda c: lambda e: e.activation(
                st_r2[:, c], st_r2[:, c], AF.Exp, scale=-0.5))(c4),
                inc=("A", 1), ev=f"rstd2_{cc}")
            sc.emit("vector", (lambda c: lambda e: e.tensor_tensor(
                st_r2[:, c], st_r2[:, c], st_r1[:, c], ALU.mult))(c4),
                waits=[f"rstd2_{cc}"], inc=("V", 1))
            sc.emit("vector", (lambda c: lambda e: e.tensor_tensor(
                st_nm2[:, c], st_nm1[:, c], st_r2[:, c], ALU.mult))(c4),
                inc=("V", 1), ev=f"aab_{cc}")
            if is_qc:
                sc.emit("vector", (lambda c: lambda e: e.tensor_tensor(
                    st_v[:, c], st_nm1[:, c], st_r1[:, c], ALU.mult))(c4),
                    inc=("V", 1), ev=f"rb1_{cc}")

        def phD(cc, dpair):
            is_qc = cc < 4
            t0 = 4 * cc
            s, sw = take_turn(f"trE_{t0 + 2 * dpair + 1}")
            for j2 in range(2):
                ti = t0 + 2 * dpair + j2
                w = [f"aab_{cc}"] + rings.acquire(("xnh", ti % 2))
                sc.emit("scalar", (lambda t: lambda e: e.activation(
                    xnh_ring(t), x_ring(t), AF.Identity,
                    scale=st_r2[:, t:t + 1],
                    bias=st_nm2[:, t:t + 1]))(ti),
                    waits=w, inc=("A", 1), ev=f"xnh_{ti}")
                if is_qc:
                    w3 = [f"rb1_{cc}"] + rings.acquire(("xnb", ti % 4))
                    sc.emit("scalar", (lambda t: lambda e: e.activation(
                        xnb_ring(t), x_ring(t), AF.Identity,
                        scale=st_r1[:, t:t + 1],
                        bias=st_v[:, t:t + 1]))(ti), waits=w3,
                        inc=("A", 1), ev=f"xnb_{ti}")
                rings.release(("x", ti % 8),
                              f"xnb_{ti}" if is_qc else f"xnh_{ti}")
                rings.last[("x2", ti % 8)] = f"xsum_{ti}"
                bb = 2 * s + j2
                for kk in range(8):
                    wt = []
                    if kk == 0:
                        wt = [f"xnh_{ti}"] + (list(sw) if j2 == 0 else [])
                    fn = (lambda t, k2, b2: lambda e: e.transpose(
                        bank_bf(b2)[:, 128 * k2:128 * (k2 + 1)],
                        xnh_ring(t)[:, 128 * k2:128 * (k2 + 1)],
                        ident_bf[:, :]))(ti, kk, bb)
                    sc.emit("tensor", fn, waits=wt,
                            inc=("P", 1) if kk == 7 else None,
                            ev=f"trT_{ti}" if kk == 7 else None)
            for j2 in range(2):
                ti = t0 + 2 * dpair + j2
                j = 2 * dpair + j2
                bb = 2 * s + j2
                dv = xhT_buf(cc).rearrange("p (k t) -> p k t", t=512)[
                    :, :, 128 * j:128 * (j + 1)]
                fn = (lambda d, b2: lambda e: e.tensor_copy(
                    d, bank_bf(b2).rearrange("p (k t) -> p k t",
                                             t=128)))(dv, bb)
                w2 = [f"trT_{ti}"] + \
                    (rings.acquire(("xhT", cc % 2)) if j == 0 else [])
                sc.emit("vector", fn, waits=w2, inc=("V", 1),
                        ev=f"trE_{ti}")
                rings.release(("xnh", ti % 2), f"trE_{ti}")

        def phE(cc, epair, pname):
            t0 = 4 * cc
            wfn = wdqp if pname == "cq" else wdkvp
            s, sw = take_turn(f"{pname}E_{cc}_{2 * epair + 1}")
            for m2 in range(2):
                m = 2 * epair + m2
                for k in range(8):
                    wt = []
                    if k == 0:
                        wt = (list(sw) if m2 == 0 else []) + \
                            [f"trE_{t0 + 3}"]
                        if cc == 0 and m == 0:
                            wt = wt + (["wsc_q_7"] if pname == "cq"
                                       else ["wsc_kv_7"])
                    fn = (lambda mm, kk, wf, c, b2: lambda e: e.matmul(
                        bank(b2), wf(kk)[:, 128 * mm:128 * (mm + 1)],
                        xhT_buf(c)[:, 512 * kk:512 * (kk + 1)],
                        start=(kk == 0), stop=(kk == 7)))(
                            m, k, wfn, cc, 2 * s + m2)
                    ev = f"{pname}M_{cc}_{m}" if k == 7 else None
                    sc.emit("tensor", fn, waits=wt,
                            inc=("P", 1) if ev else None, ev=ev)
            for m2 in range(2):
                m = 2 * epair + m2
                if pname == "ckv":
                    w4 = [f"ckvM_{cc}_{m}"]
                    if m == 0:
                        w4 += rings.acquire(("ckvsb", cc % 2))
                        if cc >= 2:
                            w4.append(f"vprj_0_{cc - 2}_1")
                    fn = (lambda c, mm, b2: lambda e: e.tensor_scalar(
                        ckv_sb(c)[:, 512 * mm:512 * (mm + 1)], bank(b2),
                        vecs[:, 28 + mm:29 + mm], None, ALU.add))(
                            cc, m, 2 * s + m2)
                    sc.emit("vector", fn, waits=w4, inc=("V", 1),
                            ev=f"ckvE_{cc}_{m}")
                    sc.dma("gpsimd" if cc % 2 == 0 else "sync",
                           ckv_h[128 * m:128 * (m + 1),
                                 512 * cc:512 * (cc + 1)],
                           ckv_sb(cc)[:, 512 * m:512 * (m + 1)],
                           waits=[f"ckvE_{cc}_{m}"], ev=f"ckvD_{cc}_{m}",
                           sem=f"sb{cc % 2}")
                    if m == 3:
                        rings.release(("ckvsb", cc % 2), f"ckvD_{cc}_3")
                else:
                    fn = (lambda c3, mm, b2: lambda e: e.tensor_scalar(
                        cq_keep[:, 2048 * mm + 512 * c3:
                                2048 * mm + 512 * (c3 + 1)],
                        bank(b2), vecs[:, 24 + mm:25 + mm], None,
                        ALU.add))(cc, m, 2 * s + m2)
                    sc.emit("vector", fn, waits=[f"cqM_{cc}_{m}"],
                            inc=("V", 1), ev=f"cqE_{cc}_{m}")

        def phF(cc, fpair):
            t0 = 4 * cc
            s, sw = take_turn(f"xnTe_{t0 + 2 * fpair + 1}")
            for j2 in range(2):
                ti = t0 + 2 * fpair + j2
                bb = 2 * s + j2
                for kk in range(8):
                    wt = []
                    if kk == 0:
                        wt = [f"xnb_{ti}"] + \
                            (list(sw) if j2 == 0 else [])
                    fn = (lambda t, k2, b2: lambda e: e.transpose(
                        bank_bf(b2)[:, 128 * k2:128 * (k2 + 1)],
                        xnb_ring(t)[:, 128 * k2:128 * (k2 + 1)],
                        ident_bf[:, :]))(ti, kk, bb)
                    sc.emit("tensor", fn, waits=wt,
                            inc=("P", 1) if kk == 7 else None,
                            ev=f"xnT_{ti}" if kk == 7 else None)
            for j2 in range(2):
                ti = t0 + 2 * fpair + j2
                bb = 2 * s + j2
                w5 = [f"xnT_{ti}"] + rings.acquire(("xnts", ti % 2))
                fn = (lambda t, b2: lambda e: e.tensor_copy(
                    xnts(t), bank_bf(b2)))(ti, bb)
                sc.emit("vector", fn, waits=w5, inc=("V", 1),
                        ev=f"xnTe_{ti}")
                rings.release(("xnb", ti % 4), f"xnTe_{ti}")
                dstv = xnt_h.rearrange("(k p) t -> p k t", p=128)[
                    :, :, 128 * ti:128 * (ti + 1)]
                srcv = xnts(ti).rearrange("p (k t) -> p k t", t=128)
                sc.dma("gpsimd", dstv, srcv, waits=[f"xnTe_{ti}"],
                       ev=f"xnTd_{ti}", sem=f"xt{ti % 2}")
                rings.release(("xnts", ti % 2), f"xnTd_{ti}")

        NCHK = 16
        for i in range(NCHK + 3):
            cc = i - 3
            if 0 <= cc < NCHK:
                for ep in range(2):
                    phE(cc, ep, "ckv")
                    yield
                if cc < 4:
                    for ep in range(2):
                        phE(cc, ep, "cq")
                        yield
                rings.release(("xhT", cc % 2),
                              f"cqM_{cc}_3" if cc < 4 else f"ckvM_{cc}_3")
                yield ("E", cc)
                if cc < 4:
                    for fp in range(2):
                        phF(cc, fp)
                        yield
            if 2 <= i < NCHK + 2:
                for dp in range(2):
                    phD(i - 2, dp)
                    yield
            if 1 <= i < NCHK + 1:
                phB(i - 1)
            yield
            if i < NCHK:
                phA(i)
            yield
        sc.emit("vector", lambda e: e.memset(rec_sb[:, 0:1], 0.0),
                waits=["xnh_63"], inc=("V", 1), ev="xfree")

    def ffn_gen():
        for k in range(8):
            w = rings.acquire(("stage",)) + (["xfree"] if k == 0 else [])
            sc.dma("gpsimd", wstage_f[:, 0:1024],
                   din["base_w_t"][128 * k:128 * (k + 1), :], waits=w,
                   ev=f"fwl_{k}", sem="stg2")
            fn = (lambda kk: lambda e: e.tensor_copy(
                ffn_basew[:, 1024 * kk:1024 * (kk + 1)],
                wstage_f[:, 0:1024]))(k)
            sc.emit("vector", fn, waits=[f"fwl_{k}"], inc=("V", 1),
                    ev=f"fwc_{k}")
            rings.release(("stage",), f"fwc_{k}")
            if k % 2 == 1:
                yield
        srcA = din["lora_A"].rearrange("(k p) r -> p k r", p=128)
        sc.dma("gpsimd", wstage_f[:, 0:64].rearrange("p (k r) -> p k r", r=8),
               srcA, waits=rings.acquire(("stage",)), ev="flA", sem="stg2")
        sc.emit("vector", lambda e: e.tensor_copy(ffn_loraA[:, :],
                                                  wstage_f[:, 0:64]),
                waits=["flA"], inc=("V", 1), ev="fcA")
        rings.release(("stage",), "fcA")
        sc.dma("gpsimd", wstage_f[0:64, 0:1024], din["lora_Bp"],
               waits=rings.acquire(("stage",)), ev="flB", sem="stg2")
        sc.emit("vector", lambda e: e.tensor_copy(ffn_loraB[:, :],
                                                  wstage_f[0:64, 0:1024]),
                waits=["flB"], inc=("V", 1), ev="fcB")
        rings.release(("stage",), "fcB")
        srcR = din["rout_t"].rearrange("(k p) r -> p k r", p=128)
        sc.dma("gpsimd", wstage_f[:, 0:64].rearrange("p (k r) -> p k r", r=8),
               srcR, waits=rings.acquire(("stage",)), ev="flR", sem="stg2")
        sc.emit("vector", lambda e: e.tensor_copy(ffn_routw[:, :],
                                                  wstage_f[:, 0:64]),
                waits=["flR"], inc=("V", 1), ev="fcR")
        rings.release(("stage",), "fcR")
        yield
        for n in range(4):
            for k in range(8):
                w = rings.acquire(("fxnt", k))
                if n == 0:
                    w = w + ["xfree", "xnTd_14", "xnTd_15"]
                sc.dma("gpsimd", ffn_xnt(k),
                       xnt_h[128 * k:128 * (k + 1), 512 * n:512 * (n + 1)],
                       waits=w, ev=f"fx_{n}_{k}",
                       sem=f"fxd{k}")
            yield
            for j in range(4):
                m = 4 * n + j
                s, sw = take_turn(f"logE_{m}")
                for k in range(8):
                    wt = []
                    if j == 0:
                        wt += [f"fx_{n}_{k}"]
                    if k == 0:
                        wt += list(sw)
                        if m == 0:
                            wt += ["fcR", "rbev"]
                    fn = (lambda jj, kk, ss: lambda e: e.matmul(
                        bank(2 * ss)[:, 0:8],
                        ffn_xnt(kk)[:, 128 * jj:128 * (jj + 1)],
                        ffn_routw[:, 8 * kk:8 * (kk + 1)],
                        start=(kk == 0), stop=(kk == 7)))(j, k, s)
                    sc.emit("tensor", fn, waits=wt,
                            inc=("P", 1) if k == 7 else None,
                            ev=f"logM_{m}" if k == 7 else None)
                fn = (lambda mm, ss: lambda e: e.tensor_tensor(
                    ffn_logits[:, 8 * mm:8 * (mm + 1)], bank(2 * ss)[:, 0:8],
                    vecs[:, 40:48], ALU.add))(m, s)
                sc.emit("vector", fn, waits=[f"logM_{m}"], inc=("V", 1),
                        ev=f"logE_{m}")
                fn = (lambda mm: lambda e: e.reduce_max(
                    st_v[:, mm:mm + 1], ffn_logits[:, 8 * mm:8 * (mm + 1)],
                    axis=AX.X))(m)
                sc.emit("vector", fn, inc=("V", 1))
                fn = (lambda mm: lambda e: e.tensor_scalar(
                    st_s[:, mm:mm + 1], st_v[:, mm:mm + 1], -1.0, None,
                    ALU.mult))(m)
                sc.emit("vector", fn, inc=("V", 1), ev=f"gmx_{m}")
                fn = (lambda mm: lambda e: e.activation(
                    stage2[:, 16:24], ffn_logits[:, 8 * mm:8 * (mm + 1)],
                    AF.Exp, bias=st_s[:, mm:mm + 1],
                    accum_out=st_q[:, mm:mm + 1]))(m)
                sc.emit("scalar", fn, waits=[f"gmx_{m}"], inc=("A", 1),
                        ev=f"gex_{m}")
                fn = (lambda mm: lambda e: e.reciprocal(
                    st_v[:, mm:mm + 1], st_q[:, mm:mm + 1]))(m)
                sc.emit("vector", fn, waits=[f"gex_{m}"], inc=("V", 1))
                fn = (lambda mm: lambda e: e.tensor_scalar(
                    ffn_gat[:, 8 * mm:8 * (mm + 1)], stage2[:, 16:24],
                    st_v[:, mm:mm + 1], None, ALU.mult))(m)
                sc.emit("vector", fn, inc=("V", 1), ev=f"gat_{m}")
                yield
            s, sw = take_turn(f"midE_{n}")
            for k in range(8):
                wt = (list(sw) + ["fcA"]) if k == 0 else []
                fn = (lambda kk, ss: lambda e: e.matmul(
                    bank(2 * ss)[0:8, :], ffn_loraA[:, 8 * kk:8 * (kk + 1)],
                    ffn_xnt(kk)[:, :], start=(kk == 0), stop=(kk == 7)))(k, s)
                sc.emit("tensor", fn, waits=wt,
                        inc=("P", 1) if k == 7 else None,
                        ev=f"midM_{n}" if k == 7 else None)
                if k == 3:
                    yield "cont"
            fn = (lambda nn, ss: lambda e: e.tensor_copy(
                ffn_midt[:, 512 * nn:512 * (nn + 1)],
                bank(2 * ss)[0:8, :]))(n, s)
            sc.emit("vector", fn, waits=[f"midM_{n}"], inc=("V", 1),
                    ev=f"midE_{n}")
            for k in range(8):
                rings.release(("fxnt", k), f"midE_{n}")
            yield
        s, sw = take_turn("gtE")
        for m in range(16):
            wt = []
            if m == 0:
                wt = list(sw) + [f"gat_{mm}" for mm in range(16)]
            b3 = 2 * s + (m // 8)
            fn = (lambda mm, bb: lambda e: e.transpose(
                bank_bf(bb)[0:8, 128 * (mm % 8):128 * ((mm % 8) + 1)],
                ffn_gat[:, 8 * mm:8 * (mm + 1)], ident_bf[:, :]))(m, b3)
            sc.emit("tensor", fn, waits=wt, inc=("P", 1) if m == 15 else None,
                    ev="gtT" if m == 15 else None)
        fn = (lambda ss: lambda e: e.tensor_copy(
            ffn_gt[:, 0:1024], bank_bf(2 * ss)[0:8, :]))(s)
        sc.emit("vector", fn, waits=["gtT"], inc=("V", 1))
        fn = (lambda ss: lambda e: e.tensor_copy(
            ffn_gt[:, 1024:2048], bank_bf(2 * ss + 1)[0:8, :]))(s)
        sc.emit("vector", fn, inc=("V", 1), ev="gtE")
        yield
        for n in range(4):
            s, sw = take_turn(f"gm_{n}")
            fn = (lambda nn, ss: lambda e: e.matmul(
                bank(2 * ss)[0:64, :], emat[:, :],
                ffn_gt[:, 512 * nn:512 * (nn + 1)], start=True,
                stop=True))(n, s)
            sc.emit("tensor", fn, waits=list(sw) + ["gtE"], inc=("P", 1),
                    ev=f"grepM_{n}")
            fn = (lambda nn, ss: lambda e: e.matmul(
                bank(2 * ss + 1)[0:64, :], mmat[:, :],
                ffn_midt[:, 512 * nn:512 * (nn + 1)], start=True,
                stop=True))(n, s)
            sc.emit("tensor", fn, waits=[f"midE_{n}"], inc=("P", 1),
                    ev=f"mrepM_{n}")
            fn = (lambda nn, ss: lambda e: e.tensor_copy(
                ffn_grep[:, :], bank(2 * ss)[0:64, :]))(n, s)
            sc.emit("vector", fn, waits=[f"grepM_{n}"], inc=("V", 1),
                    ev=f"grepE_{n}")
            fn = (lambda nn, ss: lambda e: e.tensor_tensor(
                ffn_gm[:, 512 * nn:512 * (nn + 1)], bank(2 * ss + 1)[0:64, :],
                ffn_grep[:, :], ALU.mult))(n, s)
            sc.emit("vector", fn, waits=[f"mrepM_{n}", f"grepE_{n}"],
                    inc=("V", 1), ev=f"gm_{n}")
            yield
        for n in range(4):
            for k in range(8):
                sc.dma("gpsimd", ffn_xnt(k),
                       xnt_h[128 * k:128 * (k + 1), 512 * n:512 * (n + 1)],
                       waits=rings.acquire(("fxnt", k)), ev=f"fx2_{n}_{k}",
                       sem=f"fxd{k}")
            yield
            for m in range(8):
                ii = n * 8 + m
                s, sw = take_turn(f"u_{n}_{m}")
                wt = list(sw) + [f"gm_{n}"]
                if n == 0 and m == 0:
                    wt += ["fcB"]
                fn = (lambda nn, mm, ss: lambda e: e.matmul(
                    bank(2 * ss + 1), ffn_loraB[:, 128 * mm:128 * (mm + 1)],
                    ffn_gm[:, 512 * nn:512 * (nn + 1)], start=True,
                    stop=True))(n, m, s)
                sc.emit("tensor", fn, waits=wt, inc=("P", 1),
                        ev=f"lraM_{n}_{m}")
                fn = (lambda ss: lambda e: e.tensor_copy(
                    ffn_lsb[:, :], bank(2 * ss + 1)))(s)
                sc.emit("vector", fn, waits=[f"lraM_{n}_{m}"], inc=("V", 1),
                        ev=f"lraE_{n}_{m}")
                for k in range(8):
                    wt = []
                    if m == 0:
                        wt += [f"fx2_{n}_{k}"]
                    if k == 0:
                        wt += ["fwc_7"]
                    fn = (lambda mm, kk, ss: lambda e: e.matmul(
                        bank(2 * ss), ffn_basew[:, 1024 * kk + 128 * mm:
                                                1024 * kk + 128 * (mm + 1)],
                        ffn_xnt(kk)[:, :], start=(kk == 0),
                        stop=(kk == 7)))(m, k, s)
                    sc.emit("tensor", fn, waits=wt,
                            inc=("P", 1) if k == 7 else None,
                            ev=f"basM_{n}_{m}" if k == 7 else None)
                    if k == 3:
                        yield "cont"
                fn = (lambda mm, ii2, ss: lambda e: e.scalar_tensor_tensor(
                    ffn_u(ii2), bank(2 * ss), vecs[:, 16 + mm:17 + mm],
                    ffn_lsb[:, :], ALU.add, ALU.add))(m, ii, s)
                w6 = [f"basM_{n}_{m}", f"lraE_{n}_{m}"] + \
                    rings.acquire(("fu", ii % 2))
                sc.emit("vector", fn, waits=w6, inc=("V", 1), ev=f"u_{n}_{m}")
                w7 = [f"u_{n}_{m}"]
                if ii >= 1:
                    w7.append(f"sg1_{(ii - 1) // 8}_{(ii - 1) % 8}")
                fn = (lambda ii2: lambda e: e.activation(
                    ffn_sig[:, :], ffn_u(ii2), AF.Exp, scale=-1.0))(ii)
                sc.emit("scalar", fn, waits=w7, inc=("A", 1), ev=f"es_{n}_{m}")
                fn = (lambda: lambda e: e.tensor_scalar(
                    stage2[:, 0:512], ffn_sig[:, :], 1.0, None, ALU.add))()
                sc.emit("vector", fn, waits=[f"es_{n}_{m}"], inc=("V", 1),
                        ev=f"sg1_{n}_{m}")
                fn = (lambda: lambda e: e.reciprocal(stage2[:, 512:1024],
                                                     stage2[:, 0:512]))()
                sc.emit("vector", fn, inc=("V", 1))
                fn = (lambda ii2: lambda e: e.tensor_tensor(
                    ffn_gout(ii2), ffn_u(ii2), stage2[:, 512:1024],
                    ALU.mult))(ii)
                w8 = rings.acquire(("gout", ii % 2))
                sc.emit("vector", fn, waits=w8, inc=("V", 1), ev=f"g_{n}_{m}")
                rings.release(("fu", ii % 2), f"g_{n}_{m}")
                sc.dma("gpsimd",
                       g_h[128 * m:128 * (m + 1), 512 * n:512 * (n + 1)],
                       ffn_gout(ii), waits=[f"g_{n}_{m}"], ev=f"gD_{n}_{m}",
                       sem=f"gd{ii % 2}")
                rings.release(("gout", ii % 2), f"gD_{n}_{m}")
                yield
            for k in range(8):
                rings.release(("fxnt", k), f"basM_{n}_{7}")

    def build_gen(g):
        par = g % 2
        if g >= 1:
            for _ in range(40):
                yield
        wuw = []
        if g >= 2:
            wuw = [f"qprj_{g - 2}_3"]
        for wi, wname in enumerate(("w_uk_t", "w_uv_t", "w_uq_t")):
            rearr = din[wname].rearrange("(k p) c -> p k c", p=128)
            srcw = rearr[:, :, 128 * g:128 * (g + 1)]
            w = rings.acquire(("stage",)) + (wuw if wi == 0 else [])
            dstw = stage[:, 0:512].rearrange("p (k c) -> p k c", c=128)
            sc.dma("gpsimd", dstw, srcw, waits=w, ev=f"wul_{g}_{wi}",
                   sem="stg2")
            fn = (lambda wi2: lambda e: e.tensor_copy(
                wu_d[:, 1536 * par + 512 * wi2:1536 * par + 512 * (wi2 + 1)],
                stage[:, 0:512]))(wi)
            sc.emit("vector", fn, waits=[f"wul_{g}_{wi}"], inc=("V", 1),
                    ev=f"wuc_{g}_{wi}")
            rings.release(("stage",), f"wuc_{g}_{wi}")
        yield

        def wu_k(kk):
            return wu_d[:, 1536 * par + 128 * kk:1536 * par + 128 * (kk + 1)]

        def wu_v(kk):
            o = 1536 * par + 512
            return wu_d[:, o + 128 * kk:o + 128 * (kk + 1)]

        def wu_q(kk):
            o = 1536 * par + 1024
            return wu_d[:, o + 128 * kk:o + 128 * (kk + 1)]

        def qbuild():
            for qc2 in range(4):
                tsl2 = qc2 % 4
                w = rings.acquire(("trig", tsl2))
                sc.dma("gpsimd", trig_r[:, 1024 * tsl2:1024 * tsl2 + 512],
                           din["trig_cq"][:, 512 * qc2:512 * (qc2 + 1)], waits=w,
                           ev=f"tqc_{g}_{qc2}", sem=f"trg{tsl2}")
                sc.dma("gpsimd", trig_r[:, 1024 * tsl2 + 512:1024 * (tsl2 + 1)],
                           din["trig_sq"][:, 512 * qc2:512 * (qc2 + 1)],
                           ev=f"tqs_{g}_{qc2}", sem=f"trg{tsl2}")
                s, sw = take_turn(f"qab_{g}_{qc2}")
                for k in range(4):
                        wt = []
                        if k == 0:
                            wt = list(sw)
                            if qc2 == 0:
                                    wt = wt + rings.acquire(("qtd", par)) + \
                                        [f"wuc_{g}_2"]
                            if qc2 == 0 and g <= 1:
                                    wt = wt + ["cqE_3_3"]
                        fn = (lambda c2, kk, ss: lambda e: e.matmul(
                            bank(2 * ss), wu_q(kk),
                            cq_keep[:, 2048 * kk + 512 * c2:2048 * kk + 512 * (c2 + 1)],
                            start=(kk == 0), stop=(kk == 3)))(qc2, k, s)
                        sc.emit("tensor", fn, waits=wt,
                                    inc=("P", 1) if k == 3 else None,
                                    ev=f"qprj_{g}_{qc2}" if k == 3 else None)
                asl = qc2 % 2
                wA = rings.acquire(("ab", asl)) + [f"qprj_{g}_{qc2}",
                                                                   f"tqc_{g}_{qc2}",
                                                                   f"tqs_{g}_{qc2}"]
                fn = (lambda a2, t3, ss: lambda e: e.tensor_tensor(
                        ab_buf[:, 1024 * a2:1024 * a2 + 512], bank(2 * ss),
                        trig_r[:, 1024 * t3:1024 * t3 + 512], ALU.mult))(asl, tsl2, s)
                sc.emit("vector", fn, waits=wA, inc=("V", 1))
                fn = (lambda a2, t3, ss: lambda e: e.tensor_tensor(
                        ab_buf[:, 1024 * a2 + 512:1024 * (a2 + 1)], bank(2 * ss),
                        trig_r[:, 1024 * t3 + 512:1024 * (t3 + 1)],
                        ALU.mult))(asl, tsl2, s)
                sc.emit("vector", fn, inc=("V", 1), ev=f"qab_{g}_{qc2}")
                rings.release(("trig", tsl2), f"qab_{g}_{qc2}")
                s2, sw2 = take_turn(f"qt_{g}_{qc2}")
                fn = (lambda a2, ss: lambda e: e.matmul(
                        bank(2 * ss), mrot[:, :],
                        ab_buf[:, 1024 * a2 + 512:1024 * (a2 + 1)],
                        start=True, stop=True))(asl, s2)
                sc.emit("tensor", fn, waits=list(sw2) + [f"qab_{g}_{qc2}"],
                            inc=("P", 1), ev=f"qrot_{g}_{qc2}")
                fn = (lambda c2, a2, ss: lambda e: e.tensor_tensor(
                        qt_d[:, NQ * par + 512 * c2:NQ * par + 512 * (c2 + 1)],
                        bank(2 * ss), ab_buf[:, 1024 * a2:1024 * a2 + 512],
                        ALU.add))(qc2, asl, s2)
                sc.emit("vector", fn, waits=[f"qrot_{g}_{qc2}"], inc=("V", 1),
                            ev=f"qt_{g}_{qc2}")
                rings.release(("ab", asl), f"qt_{g}_{qc2}")
                yield


        if g != 0:
            for _ in qbuild():
                yield

        war = []
        if g >= 2:
            war = [f"qk_{g - 2}_{NCH * NK - 1}", f"avs_{g - 2}_{NCH - 1}"]

        def chunk_dmas(ch):
            for k in (() if g == 0 else range(4)):
                sl = (4 * ch + k) % 12
                w = rings.acquire(("ckvin", sl))
                if g == 1:
                    w = w + [f"ckvD_{ch}_3"]
                sc.dma("gpsimd", ckv_in[:, 512 * sl:512 * (sl + 1)],
                       ckv_h[128 * k:128 * (k + 1), 512 * ch:512 * (ch + 1)],
                       waits=w, ev=f"kin_{g}_{ch}_{k}", sem=f"kin{sl}")
            tsl2 = ch % 4
            w = rings.acquire(("trig", tsl2))
            sc.dma("gpsimd", trig_r[:, 1024 * tsl2:1024 * tsl2 + 512],
                   din["trig_c"][:, 512 * ch:512 * (ch + 1)], waits=w,
                   ev=f"tcd_{g}_{ch}", sem=f"trg{tsl2}")
            sc.dma("gpsimd", trig_r[:, 1024 * tsl2 + 512:1024 * (tsl2 + 1)],
                   din["trig_s"][:, 512 * ch:512 * (ch + 1)],
                   ev=f"tsd_{g}_{ch}", sem=f"trg{tsl2}")

        chunk_dmas(0)
        yield
        chunk_dmas(1)
        yield
        for ch in range(16):
            tsl = ch % 4
            if ch + 2 < 16:
                chunk_dmas(ch + 2)
            yield
            s, sw = take_turn(f"kt_{g}_{ch}")

            def ckv_rhs(kk, lo, sz, c2=ch):
                if g == 0:
                    return ckv_sb(c2)[:, 512 * kk + lo:512 * kk + lo + sz]
                sl2 = (4 * c2 + kk) % 12
                return ckv_in[:, 512 * sl2 + lo:512 * sl2 + lo + sz]

            for k in range(4):
                wt = []
                if k == 0:
                    wt = list(sw)
                    wt += ([f"ckvE_{ch}_3"] if g == 0 else
                           [f"kin_{g}_{ch}_{k2}" for k2 in range(4)])
                    if ch == 0:
                        wt = wt + [f"wuc_{g}_2"]
                fn = (lambda kk, ss, rhs: lambda e: e.matmul(
                    bank(2 * ss), wu_k(kk), rhs,
                    start=(kk == 0), stop=(kk == 3)))(k, s, ckv_rhs(k, 0, 512))
                sc.emit("tensor", fn, waits=wt,
                        inc=("P", 1) if k == 3 else None,
                        ev=f"kprj_{g}_{ch}" if k == 3 else None)
                if k == 1:
                    yield "cont"
            asl = ch % 2
            wA = rings.acquire(("ab", asl)) + \
                [f"kprj_{g}_{ch}", f"tcd_{g}_{ch}", f"tsd_{g}_{ch}"]
            fn = (lambda a2, t3, ss: lambda e: e.tensor_tensor(
                ab_buf[:, 1024 * a2:1024 * a2 + 512], bank(2 * ss),
                trig_r[:, 1024 * t3:1024 * t3 + 512], ALU.mult))(asl, tsl, s)
            sc.emit("vector", fn, waits=wA, inc=("V", 1))
            fn = (lambda a2, t3, ss: lambda e: e.tensor_tensor(
                ab_buf[:, 1024 * a2 + 512:1024 * (a2 + 1)], bank(2 * ss),
                trig_r[:, 1024 * t3 + 512:1024 * (t3 + 1)],
                ALU.mult))(asl, tsl, s)
            sc.emit("vector", fn, inc=("V", 1), ev=f"kab_{g}_{ch}")
            rings.release(("trig", tsl), f"kab_{g}_{ch}")
            fn = (lambda a2, ss: lambda e: e.matmul(
                bank(2 * ss + 1), mrot[:, :],
                ab_buf[:, 1024 * a2 + 512:1024 * (a2 + 1)],
                start=True, stop=True))(asl, s)
            sc.emit("tensor", fn, waits=[f"kab_{g}_{ch}"], inc=("P", 1),
                    ev=f"krot_{g}_{ch}")
            ksl = ch % 2
            wkt = rings.acquire(("kts", ksl)) + [f"krot_{g}_{ch}"]
            fn = (lambda a2, ss, kl: lambda e: e.tensor_tensor(
                kts[:, 512 * kl:512 * (kl + 1)], bank(2 * ss + 1),
                ab_buf[:, 1024 * a2:1024 * a2 + 512], ALU.add))(asl, s, ksl)
            sc.emit("vector", fn, waits=wkt, inc=("V", 1), ev=f"kt_{g}_{ch}")
            rings.release(("ab", asl), f"kt_{g}_{ch}")
            w = [f"kt_{g}_{ch}"] + (war if ch == 0 else [])
            sc.dma("gpsimd",
                   kt_h[128 * par:128 * (par + 1), 512 * ch:512 * (ch + 1)],
                   kts[:, 512 * ksl:512 * (ksl + 1)], waits=w,
                   ev=f"kto_{g}_{ch}", sem=f"kto{ksl}")
            rings.release(("kts", ksl), f"kto_{g}_{ch}")
            yield
            vsl = ch % 2
            wv0 = rings.acquire(("vss", vsl))
            for half in range(2):
                s, sw = take_turn(f"vse_{g}_{ch}_{half}")
                for mj2 in range(2):
                    mj = 2 * half + mj2
                    kt2 = 4 * ch + mj
                    for k in range(4):
                        wt = []
                        if k == 0:
                            wt = list(sw) if mj2 == 0 else []
                            if half == 0 and mj2 == 0:
                                wt = wt + wv0
                        fn = (lambda kk, bb, rhs: lambda e: e.matmul(
                            ps[:, 512 * bb:512 * bb + 128], rhs,
                            wu_v(kk), start=(kk == 0), stop=(kk == 3)))(
                                k, 2 * s + mj2, ckv_rhs(k, 128 * mj, 128))
                        sc.emit("tensor", fn, waits=wt,
                                inc=("P", 1) if (k == 3 and mj2 == 1) else None,
                                ev=f"vprj_{g}_{ch}_{half}"
                                if (k == 3 and mj2 == 1) else None)
                for mj2 in range(2):
                    mj = 2 * half + mj2
                    fn = (lambda mj3, bb, vl: lambda e: e.tensor_copy(
                        vss[:, 512 * vl + 128 * mj3:512 * vl + 128 * (mj3 + 1)],
                        ps[:, 512 * bb:512 * bb + 128]))(mj, 2 * s + mj2, vsl)
                    sc.emit("vector", fn,
                            waits=[f"vprj_{g}_{ch}_{half}"] if mj2 == 0 else [],
                            inc=("V", 1),
                            ev=f"vse_{g}_{ch}_{half}" if mj2 == 1 else None)
                yield
            VH = NK * 128
            srcA = bass.AP(sbt["vss"], 512 * vsl, [[1024, 128], [128, 4], [1, 64]])
            dstA = bass.AP(vs_h.tensor, (128 * par) * VH + 128 * (4 * ch),
                           [[VH, 128], [128, 4], [1, 64]])
            sc.dma("gpsimd", dstA, srcA, waits=[f"vse_{g}_{ch}_1"],
                   ev=f"vso_{g}_{ch}_A", sem=f"vso{vsl}")
            srcB = bass.AP(sbt["vss"], 512 * vsl + 64, [[1024, 128], [128, 4], [1, 64]])
            dstB = bass.AP(vs_h.tensor,
                           (128 * par) * VH + 128 * (4 * ch) + 64,
                           [[VH, 128], [128, 4], [1, 64]])
            sc.dma("gpsimd", dstB, srcB, waits=[f"vse_{g}_{ch}_1"],
                   ev=f"vso_{g}_{ch}_B", sem=f"vso{vsl}")
            rings.release(("vss", vsl), f"vso_{g}_{ch}_B")
        if g == 0:
            for _ in qbuild():
                yield
    PF = 12

    def emit_stream(g, idx, first_waits):
        par = g % 2
        kk = idx % NK
        sl = idx % 16
        VH = NK * 128
        w = rings.acquire(("vs", sl)) + list(first_waits) + \
            [f"vso_{g}_{kk // 4}_B"]
        if g == 0 and idx < 16:
            w.append("vsr_init")
        dstv = bass.AP(sbt["vs_r"], VST * sl, [[16 * VST, 128], [129, 2], [1, 64]])
        srcv = bass.AP(vs_h.tensor, (128 * par) * VH + 128 * kk,
                       [[VH, 128], [64, 2], [1, 64]])
        sc.dma("sync", dstv, srcv, waits=w, ev=f"vsd_{g}_{idx}",
               sem=f"vs{sl}")

    def emit_ktload(g, kk, first_waits=()):
        par = g % 2
        w = list(first_waits) + [f"kto_{g}_{kk // 4}"]
        if g >= 1:
            w.append(f"qk_{g - 1}_{3 * NK + kk}")
        if kk >= 16:
            w.append(f"qk_{g}_{kk - 16}")
        sc.dma("sync", kt_res[:, 128 * kk:128 * (kk + 1)],
               kt_h[128 * par:128 * (par + 1), 128 * kk:128 * (kk + 1)],
               waits=w, ev=f"ktr_{g}_{kk}", sem=f"ks{kk % 16}")

    ph1g = ph1_gen()
    b0 = build_gen(0)

    def drive_b0(n=1):
        for _ in range(n):
            try:
                next(b0)
            except StopIteration:
                return False
        return True

    def ph1_until(target):
        for tok in ph1g:
            if tok == ("E", target):
                return

    drive_b0(1)
    drive_b0(2)
    for ch in range(16):
        ph1_until(ch)
        drive_b0(5)
    drive_b0(4)
    for _ in ph1g:
        pass
    assert not drive_b0()

    pulls = []
    pull_i = [0]
    cont = [None]
    defq = []

    def pull():
        gen = cont[0]
        cont[0] = None
        if gen is None and defq:
            gen = defq[0]
        if gen is None and pulls:
            gen = pulls[pull_i[0] % len(pulls)]
        if gen is not None:
            try:
                if next(gen) == "cont":
                    cont[0] = gen
            except StopIteration:
                if gen in pulls:
                    pulls.remove(gen)
                if defq and gen is defq[0]:
                    defq.pop(0)
        pull_i[0] += 1

    for g in range(NG):
        TT = NCH * NK
        if g == 1:
            pulls.append(ffn_gen())
        if g + 1 < NG:
            pulls.append(build_gen(g + 1))

        bld = [f"qt_{g}_3"]
        qk_meta = []

        def emit_qk(idx, g=g, qk_meta=qk_meta, bld=bld):
            c, k = divmod(idx, NK)
            s, sw = take_turn(f"ex_{g}_{idx}")
            qk_meta.append((s, idx))
            wt = list(sw)
            if idx < NK:
                wt.append(f"ktr_{g}_{k}")
            if idx == 0:
                wt = wt + bld
            fn = (lambda cc2, ss, kk2, par2: lambda e: e.matmul(
                bank(2 * ss), kt_res[0:64, 128 * kk2:128 * (kk2 + 1)],
                qt_d[0:64, NQ * par2 + 512 * cc2:NQ * par2 + 512 * (cc2 + 1)],
                start=True, stop=True))(c, s, k, g % 2)
            sc.emit("tensor", fn, waits=wt)
            fn = (lambda cc2, ss, kk2, par2: lambda e: e.matmul(
                bank(2 * ss + 1), kt_res[64:128, 128 * kk2:128 * (kk2 + 1)],
                qt_d[64:128, NQ * par2 + 512 * cc2:NQ * par2 + 512 * (cc2 + 1)],
                start=True, stop=True))(c, s, k, g % 2)
            sc.emit("tensor", fn, inc=("QK", 1), ev=f"qk_{g}_{idx}")

        def emit_exp(idx, g=g, qk_meta=qk_meta):
            s = qk_meta[idx][0]
            p3 = idx % 3
            wt = [f"qk_{g}_{idx}"]
            fn = (lambda ss, pp: lambda e: e.activation(
                probs[:, 1024 * pp:1024 * (pp + 1)],
                ps[:, 1024 * ss:1024 * (ss + 1)], AF.Exp, scale=0.125))(s, p3)
            sc.emit("scalar", fn, waits=wt, inc=("EX", 1), ev=f"ex_{g}_{idx}")
            last_ex_ev[0] = f"ex_{g}_{idx}"

        def emit_av(idx, g=g):
            c, k = divmod(idx, NK)
            p3 = idx % 3
            sl = idx % 16
            wt = [f"ex_{g}_{idx}", f"vsd_{g}_{idx}"]
            if k == 0:
                wt = wt + banks.acquire(6) + banks.acquire(7)
            fn = (lambda sll, pp, kk2: lambda e: e.matmul(
                bank(6)[0:65, :], statA_ap(sll),
                probs[:, 1024 * pp:1024 * pp + 512],
                start=(kk2 == 0), stop=(kk2 == NK - 1)))(sl, p3, k)
            sc.emit("tensor", fn, waits=wt)
            fn = (lambda sll, pp, kk2: lambda e: e.matmul(
                bank(7), statB_ap(sll),
                probs[:, 1024 * pp + 512:1024 * (pp + 1)],
                start=(kk2 == 0), stop=(kk2 == NK - 1)))(sl, p3, k)
            sc.emit("tensor", fn, inc=("AVT", 1))
            ev = f"avs_{g}_{c}" if k == NK - 1 else None
            if ev:
                sc.events[ev] = ("AVT", sc.cnt["AVT"])
            rings.release(("vs", sl), ("#AVT", sc.cnt["AVT"]))
            if k == NK - 1:
                ec = f"avs_{g}_{c}"
                sc.emit("vector", (lambda: lambda e: e.tensor_copy(
                    av_sb[0:65, 0:512], bank(6)[0:65, :]))(), waits=[ec],
                    inc=("V", 1))
                sc.emit("vector", (lambda: lambda e: e.tensor_copy(
                    av_sb[:, 512:1024], bank(7)))(), inc=("V", 1),
                    ev=f"avc_{g}_{c}")
                banks.release(6, f"avc_{g}_{c}")
                banks.release(7, f"avc_{g}_{c}")
                defq.append(epi_gen(c, g))

        def epi_gen(c, g=g):
            sc.emit("scalar", (lambda: lambda e: e.activation(
                rec_sb[64:65, 0:512], av_sb[64:65, 0:512], AF.Ln))(),
                waits=[f"avc_{g}_{c}"], inc=("A", 1))
            sc.emit("scalar", (lambda: lambda e: e.activation(
                rec_sb[64:65, 512:1024], rec_sb[64:65, 0:512], AF.Exp,
                scale=-1.0))(), inc=("A", 1), ev=f"rcpA_{g}_{c}")
            yield
            sc.emit("scalar", (lambda: lambda e: e.activation(
                rec_sb[32:33, 0:512], av_sb[32:33, 512:1024], AF.Ln))(),
                inc=("A", 1))
            sc.emit("scalar", (lambda: lambda e: e.activation(
                rec_sb[32:33, 512:1024], rec_sb[32:33, 0:512], AF.Exp,
                scale=-1.0))(), inc=("A", 1), ev=f"rcp_{g}_{c}")
            yield
            s, sw = take_turn(f"bce_{g}_{c}")
            sc.emit("tensor", (lambda ss: lambda e: e.matmul(
                bank(2 * ss), bmask[64:65, 0:128], rec_sb[64:65, 512:1024],
                start=True, stop=True))(s), waits=list(sw) + [f"rcpA_{g}_{c}"])
            sc.emit("tensor", (lambda ss: lambda e: e.matmul(
                bank(2 * ss + 1), bmask[32:33, 0:128],
                rec_sb[32:33, 512:1024], start=True, stop=True))(s),
                waits=[f"rcp_{g}_{c}"], inc=("P", 1), ev=f"bcm_{g}_{c}")
            sc.emit("vector", (lambda ss: lambda e: e.tensor_copy(
                bc_sb[0:64, :], bank(2 * ss)[0:64, :]))(s),
                waits=[f"bcm_{g}_{c}"], inc=("V", 1))
            sc.emit("vector", (lambda ss: lambda e: e.tensor_copy(
                bc_sb[64:128, :], bank(2 * ss + 1)[64:128, :]))(s),
                inc=("V", 1), ev=f"bce_{g}_{c}")
            yield
            aosl = c % 2
            w7 = rings.acquire(("ao", aosl))
            sc.emit("vector", (lambda a2: lambda e: e.tensor_tensor(
                ao_sb[0:64, 512 * a2:512 * (a2 + 1)], av_sb[0:64, 0:512],
                bc_sb[0:64, :], ALU.mult))(aosl),
                waits=[f"bce_{g}_{c}"] + w7, inc=("V", 1))
            sc.emit("vector", (lambda a2: lambda e: e.tensor_tensor(
                ao_sb[64:128, 512 * a2:512 * (a2 + 1)],
                av_sb[64:128, 512:1024],
                bc_sb[64:128, :], ALU.mult))(aosl),
                inc=("V", 1), ev=f"ep_{g}_{c}")
            sc.dma("gpsimd",
                   attn_h[128 * g:128 * (g + 1), 512 * c:512 * (c + 1)],
                   ao_sb[:, 512 * aosl:512 * (aosl + 1)],
                   waits=[f"ep_{g}_{c}"], ev=f"aod_{g}_{c}", sem=f"ao{aosl}")
            rings.release(("ao", aosl), f"aod_{g}_{c}")

        for kk in range(16):
            emit_ktload(g, kk)
        for pidx in range(PF):
            emit_stream(g, pidx, [])
        emit_qk(0)
        emit_qk(1)
        for idx in range(TT):
            emit_exp(idx)
            pull()
            if idx < NK - 16:
                emit_ktload(g, idx + 16)
            if idx + PF < TT:
                emit_stream(g, idx + PF, [])
            if idx + 2 < TT:
                emit_qk(idx + 2)
            emit_av(idx)
        rings.release(("qtd", g % 2), f"qk_{g}_{TT - 1}")

    while pulls or defq or cont[0] is not None:
        pull()
    for s in range(3):
        if s in slot_free:
            banks.release(2 * s, slot_free[s])
            banks.release(2 * s + 1, slot_free[s])

    tail_bar = [f"aod_{NG - 1}_{NCH - 2}", f"aod_{NG - 1}_{NCH - 1}",
                f"ep_{NG - 1}_{NCH - 1}", f"avs_{NG - 1}_{NCH - 1}",
                last_ex_ev[0]]
    for k in range(8):
        w = rings.acquire(("stage",)) + (list(tail_bar) if k == 0 else [])
        sc.dma("sync", tl_wstage[:, 0:1024],
               din["w_out_t"][128 * k:128 * (k + 1), :], waits=w,
               ev=f"tlWl_{k}", sem="stg")
        fn = (lambda kk: lambda e: e.tensor_copy(
            tl_wout[:, 1024 * kk:1024 * (kk + 1)], tl_wstage[:, 0:1024]))(k)
        sc.emit("vector", fn, waits=[f"tlWl_{k}"], inc=("V", 1), ev=f"tlW_{k}")
        rings.release(("stage",), f"tlW_{k}")
    for n in range(NCH):
        for gg in range(NG):
            w = rings.acquire(("tla", n % 2)) if gg == 0 else []
            if gg == 0 and n == 0:
                w = w + list(tail_bar)
            sc.dma("sync", tl_attn_r(n)[:, 512 * gg:512 * (gg + 1)],
                   attn_h[128 * gg:128 * (gg + 1), 512 * n:512 * (n + 1)],
                   waits=w, ev=f"tlA_{n}_{gg}", sem=f"tla{n % 2}")
        for k in range(8):
            w = rings.acquire(("tlg", n % 2))
            if k == 0 and n == 0:
                w = w + list(tail_bar) + ["gD_3_6", "gD_3_7"]
            sc.dma("gpsimd",
                   tl_gr(n)[:, 512 * k:512 * (k + 1)],
                   g_h[128 * k:128 * (k + 1), 512 * n:512 * (n + 1)],
                   waits=w, ev=f"tlG_{n}_{k}", sem=f"tlg{n % 2}")
        for m in range(8):
            for k in range(8):
                wt = []
                if k == 0:
                    wt = banks.acquire(m % 2)
                    if m <= 1:
                        wt = wt + ["tlW_7"] + \
                            [f"tlA_{n}_{gg}" for gg in range(NG)]
                fn = (lambda nn, mm, kk: lambda e: e.matmul(
                    bank(mm % 2), tl_wout[:, 1024 * kk + 128 * mm:
                                          1024 * kk + 128 * (mm + 1)],
                    tl_attn_r(nn)[:, 512 * kk:512 * (kk + 1)],
                    start=(kk == 0), stop=(kk == 7)))(n, m, k)
                sc.emit("tensor", fn, waits=wt,
                        inc=("P", 1) if k == 7 else None,
                        ev=f"wo_{n}_{m}" if k == 7 else None)
            w8 = [f"wo_{n}_{m}"]
            if m == 0:
                w8 = w8 + rings.acquire(("sT", n % 2))
            fn = (lambda nn, mm: lambda e: e.tensor_copy(
                tl_sT(nn)[:, 512 * mm:512 * (mm + 1)], bank(mm % 2)))(n, m)
            sc.emit("vector", fn, waits=w8, inc=("V", 1), ev=f"woE_{n}_{m}")
            banks.release(m % 2, f"woE_{n}_{m}")
        for j in range(4):
            tt = 4 * n + j
            for kk in range(8):
                wt = []
                if kk == 0:
                    wt = banks.acquire(2) + banks.acquire(3) + \
                        [f"woE_{n}_{mm}" for mm in range(8)]
                fn = (lambda nn, jj, k2: lambda e: e.transpose(
                    ps[:, 1024 + 128 * k2:1024 + 128 * (k2 + 1)],
                    tl_sT(nn)[:, 512 * k2 + 128 * jj:
                              512 * k2 + 128 * (jj + 1)],
                    ident_f[:, :]))(n, j, kk)
                sc.emit("tensor", fn, waits=wt,
                        inc=("P", 1) if kk == 7 else None,
                        ev=f"wT_{tt}" if kk == 7 else None)
            for kk in range(8):
                wt = []
                if kk == 0:
                    wt = banks.acquire(4) + banks.acquire(5) + \
                        [f"tlG_{n}_{k2}" for k2 in range(8)]
                fn = (lambda nn, jj, k2: lambda e: e.transpose(
                    ps[:, 2048 + 128 * k2:2048 + 128 * (k2 + 1)],
                    tl_gr(nn)[:, 512 * k2 + 128 * jj:
                              512 * k2 + 128 * (jj + 1)],
                    ident_f[:, :]))(n, j, kk)
                sc.emit("tensor", fn, waits=wt,
                        inc=("P", 1) if kk == 7 else None,
                        ev=f"gT_{tt}" if kk == 7 else None)
            w9 = [f"out_{tt - 1}"] if tt >= 1 else list(tail_bar)
            sc.dma("gpsimd", tl_x[:, :],
                   din["xq"][128 * tt:128 * (tt + 1), :], waits=w9,
                   ev=f"tlx_{tt}", sem="tlx")
            fn = (lambda t2: lambda e: e.tensor_scalar(
                tl_s1[:, :], tl_x[:, :], st_nm1[:, t2:t2 + 1],
                st_r1[:, t2:t2 + 1], ALU.add, ALU.mult))(tt)
            sc.emit("vector", fn, waits=[f"tlx_{tt}"], inc=("V", 1))
            fn = (lambda: lambda e: e.tensor_tensor(
                tl_s2[:, :], ps[:, 1024:2048], tl_s1[:, :], ALU.add))()
            sc.emit("vector", fn, waits=[f"wT_{tt}"], inc=("V", 1),
                    ev=f"sA_{tt}")
            banks.release(2, f"sA_{tt}")
            banks.release(3, f"sA_{tt}")
            fn = (lambda: lambda e: e.tensor_tensor(
                tl_s1[:, :], ps[:, 2048:3072], tl_s2[:, :], ALU.mult))()
            sc.emit("vector", fn, waits=[f"gT_{tt}"], inc=("V", 1),
                    ev=f"sB_{tt}")
            banks.release(4, f"sB_{tt}")
            banks.release(5, f"sB_{tt}")
            fn = (lambda: lambda e: e.tensor_tensor(
                tl_s2[:, :], tl_s1[:, :], tl_x[:, :], ALU.add))()
            sc.emit("vector", fn, inc=("V", 1), ev=f"cmb_{tt}")
            sc.dma("gpsimd", out_d[128 * tt:128 * (tt + 1), :], tl_s2[:, :],
                   waits=[f"cmb_{tt}"], ev=f"out_{tt}", sem="outd")
        rings.release(("sT", n % 2), f"wT_{4 * n + 3}")
        rings.release(("tlg", n % 2), f"gT_{4 * n + 3}")
        rings.release(("tla", n % 2), f"wo_{n}_7")

    sc.emit("gpsimd", lambda e: None, waits=["out_15"])

    with nc.Block() as block:
        @block.sync
        def _(eng):
            sc.replay("sync", eng, semobjs)

        @block.gpsimd
        def _(eng):
            sc.replay("gpsimd", eng, semobjs)

        @block.vector
        def _(eng):
            sc.replay("vector", eng, semobjs)

        @block.scalar
        def _(eng):
            sc.replay("scalar", eng, semobjs)

        @block.tensor
        def _(eng):
            sc.replay("tensor", eng, semobjs)

    ctx.close()
    return nc


_NC_CACHE = []


def kernel(**inputs):
    in_maps = _host_prepare(inputs)
    if not _NC_CACHE:
        _NC_CACHE.append(build_nc())
    nc = _NC_CACHE[0]
    res = run_bass_kernel_spmd(nc, in_maps, core_ids=list(range(NCORE)))
    out = np.empty((B, S, D), dtype=np.float32)
    for c in range(NCORE):
        bb = c // 4
        q0 = (c % 4) * NQ
        out[bb, q0:q0 + NQ] = res.results[c]["out"]
    return out


# version 14
# speedup vs baseline: 1.0123x; 1.0123x over previous
import numpy as np
import ml_dtypes
import concourse.bass as bass
import concourse.mybir as mybir
from concourse.bass_utils import run_bass_kernel_spmd

F32 = mybir.dt.float32
BF16 = mybir.dt.bfloat16
AF = mybir.ActivationFunctionType
ALU = mybir.AluOpType
AX = mybir.AxisListType

B, S, D = 2, 8192, 1024
H, DH = 16, 64
DKV = 512
NQ = 2048
NCORE = 8
NG = 8
NCH = 4
NK = 64
QC = 512
EPS = 1e-5
DEBUG_TAPS = False
VST = 193
VPITCH = NK * VST


def _rope_tables():
    ROPE_BASE = 10000.0
    inv_freq = 1.0 / ROPE_BASE ** (np.arange(0, DH, 2, dtype=np.float32) / DH)
    t = np.arange(S, dtype=np.float32)
    freqs = np.outer(t, inv_freq)
    cos = np.concatenate([np.cos(freqs), np.cos(freqs)], -1)[:, ::2]
    sin = np.concatenate([np.sin(freqs), np.sin(freqs)], -1)[:, ::2]
    return cos, sin


def _host_prepare(inp):
    bf = ml_dtypes.bfloat16
    perm = np.zeros(D, dtype=np.int64)
    for h0 in range(H):
        perm[64 * h0:64 * h0 + 32] = np.arange(0, 64, 2) + 64 * h0
        perm[64 * h0 + 32:64 * h0 + 64] = np.arange(1, 64, 2) + 64 * h0

    x = np.ascontiguousarray(np.asarray(inp["L0_attn_x"], dtype=np.float32))

    def fm(v):
        return np.ascontiguousarray(np.asarray(v, np.float32).reshape(8, 128).T)

    cos, sin = _rope_tables()
    trig_c = np.ascontiguousarray(np.tile(cos.T, (4, 1)).astype(bf))
    trig_s = np.ascontiguousarray(np.tile(sin.T, (4, 1)).astype(bf))

    ident = np.eye(128, dtype=np.float32)
    bmask = np.zeros((128, 128), dtype=np.float32)
    bmask[0, :] = 1.0
    bmask[64, 0:64] = 1.0
    bmask[32, 64:128] = 1.0
    mrot = np.zeros((128, 128), dtype=np.float32)
    for blk in range(2):
        o = 64 * blk
        for i in range(32):
            mrot[o + 32 + i, o + i] = -1.0
            mrot[o + i, o + 32 + i] = 1.0
    emat = np.zeros((8, 64), dtype=np.float32)
    mmat = np.zeros((8, 64), dtype=np.float32)
    for e in range(8):
        for r in range(8):
            emat[e, 8 * e + r] = 2.0
            mmat[r, 8 * e + r] = 1.0

    shared = {
        "w_dq_t": np.ascontiguousarray(np.asarray(inp["L0_attn_w_down_q"]).T),
        "w_dkv_t": np.ascontiguousarray(np.asarray(inp["L0_attn_w_down_kv"]).T),
        "w_uq_t": np.ascontiguousarray(np.asarray(inp["L0_attn_w_up_q"])[perm].T),
        "w_uk_t": np.ascontiguousarray(np.asarray(inp["L0_attn_w_up_k"])[perm].T),
        "w_uv_t": np.ascontiguousarray(np.asarray(inp["L0_attn_w_up_v"]).T),
        "w_out_t": np.ascontiguousarray(np.asarray(inp["L0_attn_w_out"]).T),
        "base_w_t": np.ascontiguousarray(np.asarray(inp["L0_ffn_base_w"]).T),
        "lora_A": np.ascontiguousarray(np.asarray(inp["L0_ffn_lora_A"])),
        "lora_Bp": np.ascontiguousarray(
            np.asarray(inp["L0_ffn_lora_B"]).reshape(64, D)),
        "rout_t": np.ascontiguousarray(np.asarray(inp["L0_ffn_router_w"]).T),
        "ln_g_fm": fm(inp["L0_attn_ln_scale"]),
        "ln_b_fm": fm(inp["L0_attn_ln_bias"]),
        "base_b_fm": fm(inp["L0_ffn_base_b"]),
        "rout_b": np.ascontiguousarray(
            np.asarray(inp["L0_ffn_router_b"], np.float32).reshape(1, 8)),
        "trig_c": trig_c, "trig_s": trig_s,
        "ident_f": ident, "ident_bf": ident.astype(bf),
        "bmask": bmask, "mrot": mrot.astype(bf),
        "emat": emat.astype(bf), "mmat": mmat.astype(bf),
    }
    in_maps = []
    for c in range(NCORE):
        bb = c // 4
        q0 = (c % 4) * NQ
        m = dict(shared)
        m["xb"] = np.ascontiguousarray(np.roll(x[bb], -q0, axis=0))
        m["xq"] = np.ascontiguousarray(x[bb, q0:q0 + NQ])
        m["trig_c"] = np.ascontiguousarray(np.roll(trig_c, -q0, axis=1))
        m["trig_s"] = np.ascontiguousarray(np.roll(trig_s, -q0, axis=1))
        m["trig_cq"] = np.ascontiguousarray(trig_c[:, q0:q0 + NQ])
        m["trig_sq"] = np.ascontiguousarray(trig_s[:, q0:q0 + NQ])
        in_maps.append(m)
    return in_maps


class Sched:

    ENGINES = ("sync", "gpsimd", "vector", "scalar", "tensor")

    def __init__(self):
        self.ops = {e: [] for e in self.ENGINES}
        self.cnt = {}
        self.events = {}

    ENGSEM = {"vector": "V", "scalar": "A"}

    def emit(self, engine, fn, waits=(), inc=None, ev=None):
        rw = list(waits)
        es = self.ENGSEM.get(engine)
        if es is not None:
            if inc is None:
                inc = (es, 1)
            if inc[0] == es:
                rw.append((es, self.cnt.get(es, 0)))
        if inc is not None:
            sem, k = inc
            self.cnt[sem] = self.cnt.get(sem, 0) + k
            if ev is not None:
                self.events[ev] = (sem, self.cnt[sem])
        self.ops[engine].append((fn, tuple(rw), inc))

    def dma(self, queue, out_ap, in_ap, waits=(), ev=None, sem=None):
        assert sem is not None
        self.emit(queue, lambda e, o=out_ap, i=in_ap: e.dma_start(out=o, in_=i),
                  waits=waits, inc=(sem, 16), ev=ev)

    def replay(self, engine_name, eng, semobjs):
        for fn, waits, inc in self.ops[engine_name]:
            best = {}
            for w in waits:
                if isinstance(w, tuple):
                    sem, val = w[0][1:] if w[0].startswith("#") else w[0], w[1]
                else:
                    sem, val = self.events[w]
                if val > best.get(sem, -1):
                    best[sem] = val
            for sem, val in best.items():
                eng.wait_ge(semobjs[sem], val)
            instr = fn(eng)
            if inc is not None:
                sem, k = inc
                instr.then_inc(semobjs[sem], k)


class Res:
    def __init__(self):
        self.last = {}

    def acquire(self, key):
        ev = self.last.get(key)
        return [ev] if ev is not None else []

    def release(self, key, ev):
        self.last[key] = ev


def build_nc():
    nc = bass.Bass("TRN2", target_bir_lowering=False)
    dt = nc.dram_tensor

    din = {}
    for name, shape, dty in [
            ("xb", [S, D], F32), ("xq", [NQ, D], F32),
            ("w_dq_t", [D, DKV], F32), ("w_dkv_t", [D, DKV], F32),
            ("w_uq_t", [DKV, D], F32), ("w_uk_t", [DKV, D], F32),
            ("w_uv_t", [DKV, D], F32), ("w_out_t", [D, D], F32),
            ("base_w_t", [D, D], F32), ("lora_A", [D, 8], F32),
            ("lora_Bp", [64, D], F32), ("rout_t", [D, 8], F32),
            ("ln_g_fm", [128, 8], F32), ("ln_b_fm", [128, 8], F32),
            ("base_b_fm", [128, 8], F32), ("rout_b", [1, 8], F32),
            ("trig_c", [128, S], BF16), ("trig_s", [128, S], BF16),
            ("trig_cq", [128, NQ], BF16), ("trig_sq", [128, NQ], BF16),
            ("ident_f", [128, 128], F32), ("ident_bf", [128, 128], BF16),
            ("bmask", [128, 128], F32), ("mrot", [128, 128], BF16),
            ("emat", [8, 64], BF16), ("mmat", [8, 64], BF16)]:
        din[name] = dt(name, shape, dty, kind="ExternalInput").ap()

    out_d = dt("out", [NQ, D], F32, kind="ExternalOutput").ap()
    _dbg = "ExternalOutput" if DEBUG_TAPS else "Internal"
    ckv_h = dt("ckv_h", [DKV, S], BF16, kind=_dbg).ap()
    xnt_h = dt("xnt_h", [D, NQ], BF16, kind=_dbg).ap()
    attn_h = dt("attn_h", [D, NQ], BF16, kind=_dbg).ap()
    g_h = dt("g_h", [D, NQ], F32, kind=_dbg).ap()
    kt_h = dt("kt_h", [2 * 128, S], BF16).ap()
    vs_h = dt("vs_h", [2 * 128, NK * 128], BF16).ap()
    if DEBUG_TAPS:
        dbg_kt = dt("dbg_kt", [128, S], BF16, kind="ExternalOutput").ap()
        dbg_qt = dt("dbg_qt", [128, NQ], BF16, kind="ExternalOutput").ap()
        dbg_vs = dt("dbg_vs", [128, VPITCH], BF16, kind="ExternalOutput").ap()
        dbg_cq = dt("dbg_cq", [128, 8192], BF16, kind="ExternalOutput").ap()

    from contextlib import ExitStack
    ctx = ExitStack()
    ent = ctx.enter_context

    sbt = {}

    def T(name, shape, dty):
        sbt[name] = ent(nc.sbuf_tensor(name, shape, dty))
        return sbt[name]

    st_nm1 = T("st_nm1", [128, 80], F32)
    st_r1 = T("st_r1", [128, 80], F32)
    st_s = T("st_s", [128, 80], F32)
    st_q = T("st_q", [128, 80], F32)
    st_nm2 = T("st_nm2", [128, 80], F32)
    st_r2 = T("st_r2", [128, 80], F32)
    st_v = T("st_v", [128, 80], F32)
    ident_f = T("identf", [128, 128], F32)
    ident_bf = T("identb", [128, 128], BF16)
    bmask = T("bmaskt", [128, 128], F32)
    mrot = T("mrott", [128, 128], BF16)
    emat = T("ematt", [8, 64], BF16)
    mmat = T("mmatt", [8, 64], BF16)
    vecs = T("vecs", [128, 64], F32)
    probs = T("probs", [128, 3072], BF16)
    rec_sb = T("rec_sb", [128, 1024], F32)
    bc_sb = T("bc_sb", [128, 512], F32)
    ao_sb = T("ao_sb", [128, 1024], BF16)
    av_sb = T("av_sb", [128, 1024], F32)
    ckv_in = T("ckv_in", [128, 6144], BF16)
    ab_buf = T("ab_buf", [128, 2048], BF16)
    trig_r = T("trig_r", [128, 4096], BF16)
    wu_d = T("wu_d", [128, 2 * 1536], BF16)
    kt_res = T("kt_res", [128, NK * 128], BF16)
    vs_r = T("vs_r", [128, 16 * VST], BF16)
    kts = T("kts", [128, 2 * 512], BF16)
    vss = T("vss", [128, 2 * 512], BF16)
    xnts_t = T("xnts_t", [128, 2048], BF16)
    qt_d = T("qt_d", [128, 2 * NQ], BF16)
    sqd = T("sqd", [128, 1024], BF16)
    stage = T("stage", [128, 1024], F32)
    stage2 = T("stage2", [128, 1024], F32)
    cst = T("cst", [128, 4], F32)
    beta_bf = T("beta_bf", [128, 8], BF16)

    AR = 26624
    arena = T("arena", [128, AR], F32)
    ar = arena[:, :]
    arb = ar.bitcast(BF16)
    wbeta = arb[:, 26624:34816]

    def x_ring(i):
        return ar[:, 1024 * (i % 8):1024 * (i % 8) + 1024]

    wstage = stage[:, :]

    def wdqp(k):
        return arb[:, 16384 + 512 * k:16384 + 512 * (k + 1)]

    def wdkvp(k):
        return arb[:, 20480 + 512 * k:20480 + 512 * (k + 1)]

    def xnh_ring(i):
        return arb[:, 24576 + 1024 * (i % 2):24576 + 1024 * (i % 2) + 1024]

    def xnb_ring(i):
        return arb[:, 47104 + 1024 * (i % 4):47104 + 1024 * (i % 4) + 1024]

    def xhT_buf(cc):
        o = 26624 + 4096 * (cc % 2)
        return arb[:, o:o + 4096]

    def ckv_sb(cc):
        o = 34816 + 2048 * (cc % 2)
        return arb[:, o:o + 2048]

    cq_keep = arb[:, 38912:47104]

    def xnts(i):
        return xnts_t[:, 1024 * (i % 2):1024 * (i % 2) + 1024]

    ffn_basew = arb[:, 0:8192]

    def ffn_xnt(k):
        return arb[:, 8192 + 512 * k:8192 + 512 * (k + 1)]

    ffn_loraA = arb[:, 12288:12352]
    ffn_loraB = arb[0:64, 12352:13376]
    ffn_routw = arb[:, 13376:13440]
    ffn_logits = ar[:, 6720:6848]
    ffn_gat = arb[:, 13696:13824]
    ffn_gt = arb[0:8, 13824:15872]
    ffn_midt = arb[0:8, 15872:17920]
    ffn_grep = arb[0:64, 17920:18432]
    ffn_gm = arb[0:64, 18432:20480]
    ffn_lsb = ar[:, 10240:10752]

    def ffn_u(i):
        return ar[:, 10752 + 512 * (i % 2):10752 + 512 * (i % 2) + 512]

    ffn_sig = ar[:, 11776:12288]

    def ffn_gout(i):
        return ar[:, 12288 + 512 * (i % 2):12288 + 512 * (i % 2) + 512]

    wstage_f = stage[:, :]

    def tl_attn_r(i):
        return arb[:, 4096 * (i % 2):4096 * (i % 2) + 4096]

    tl_wout = arb[:, 8192:16384]
    def tl_gr(i):
        return ar[:, 8192 + 4096 * (i % 2):8192 + 4096 * (i % 2) + 4096]
    tl_wstage = stage[:, :]

    def tl_sT(i):
        return ar[:, 16384 + 4096 * (i % 2):16384 + 4096 * (i % 2) + 4096]

    tl_x = ar[:, 24576:25600]
    tl_s1 = ar[:, 25600:26624]
    tl_s2 = stage2[:, :]
    ckv_f = ckv_in[:, :].bitcast(F32)

    def tlx_buf(tt):
        return tl_x if tt % 2 == 0 else ckv_f[:, 1024:2048]

    def tls2_buf(tt):
        return tl_s2 if tt % 2 == 0 else ckv_f[:, 2048:3072]

    def gT_ps(tt):
        return ps[:, 2048:3072] if tt % 2 == 0 else ps[:, 3072:4096]

    ps_t = ent(nc.psum_tensor("ps", [128, 4096], F32))
    ps = ps_t[:, :]
    psb = ps.bitcast(BF16)

    def bank(b3):
        return ps[:, 512 * b3:512 * (b3 + 1)]

    def bank_bf(b3):
        return psb[:, 1024 * b3:1024 * (b3 + 1)]

    semnames = (["V", "A", "P", "QK", "EX", "AVT", "ph0", "stg", "tlx",
                 "outd", "dbg"] +
                [f"xld{i}" for i in range(8)] +
                [f"trg{i}" for i in range(4)] +
                [f"kin{i}" for i in range(12)] +
                [f"fxd{i}" for i in range(8)] +
                [f"tla{i}" for i in range(2)] +
                [f"tlg{i}" for i in range(2)] +
                [f"sb{i}" for i in range(2)] +
                [f"ks{i}" for i in range(16)] +
                [f"vs{i}" for i in range(16)] +
                ["stg2", "kto0", "kto1", "vso0", "vso1", "xt0", "xt1"] +
                [f"xlq{i}" for i in range(8)] +
                [f"gd{i}" for i in range(2)] +
                [f"ao{i}" for i in range(2)])
    semobjs = {n: ent(nc.semaphore(n)) for n in semnames}

    sc = Sched()
    banks = Res()
    rings = Res()
    sc.emit("vector", lambda e: e.memset(cst[:, 0:1], EPS), inc=("V", 1),
            ev="cst_ready")

    def statA_ap(sl):
        return vs_r[:, VST * sl:VST * sl + 65]

    def statB_ap(sl):
        return vs_r[:, VST * sl + 65:VST * (sl + 1)]

    for dst, src in [(ident_f[:, :], din["ident_f"]),
                     (ident_bf[:, :], din["ident_bf"]),
                     (bmask[:, :], din["bmask"]), (mrot[:, :], din["mrot"]),
                     (emat[:, :], din["emat"]), (mmat[:, :], din["mmat"]),
                     (vecs[:, 0:8], din["ln_g_fm"]),
                     (vecs[:, 8:16], din["ln_b_fm"]),
                     (vecs[:, 16:24], din["base_b_fm"]),
                     (vecs[0:1, 32:40], din["rout_b"])]:
        sc.dma("sync", dst, src, ev="ph0last", sem="ph0")

    for ti in range(8):
        qd = "gpsimd" if ti % 2 == 0 else "sync"
        sc.dma(qd, x_ring(ti), din["xb"][128 * ti:128 * (ti + 1), :],
               ev=f"xld_{ti}", sem=f"xld{ti % 8}")

    conv_i = [0]

    def conv(name, dram, nkt, cols, dst_fn):
        for k in range(nkt):
            par = conv_i[0] % 2
            conv_i[0] += 1
            buf = wstage if par == 0 else stage2[:, :]
            key = ("stage",) if par == 0 else ("stg2c",)
            w = rings.acquire(key) + (["ph0last"] if conv_i[0] <= 2 else [])
            sc.dma("sync", buf[:, 0:cols], dram[128 * k:128 * (k + 1), :],
                   waits=w, ev=f"wl_{name}_{k}", sem="stg" if par == 0 else "xlq3")
            fn = (lambda dk, c, b: lambda e: e.tensor_copy(dk, b[:, 0:c]))(
                dst_fn(k), cols, buf)
            sc.emit("vector", fn, waits=[f"wl_{name}_{k}"], inc=("V", 1),
                    ev=f"wc_{name}_{k}")
            rings.release(key, f"wc_{name}_{k}")

    conv("dq", din["w_dq_t"], 8, 512, lambda k: wbeta[:, 512 * k:512 * (k + 1)])
    conv("dkv", din["w_dkv_t"], 8, 512,
         lambda k: wbeta[:, 4096 + 512 * k:4096 + 512 * (k + 1)])
    sc.emit("vector", lambda e: e.tensor_copy(beta_bf[:, :], vecs[:, 8:16]),
            waits=["ph0last"], inc=("V", 1), ev="beta_bf")

    for which, base, vcol in (("q", 0, 24), ("kv", 4096, 28)):
        for m in range(4):
            for k in range(8):
                wt = []
                if k == 0:
                    wt = banks.acquire(4)
                    if m == 0:
                        wt = wt + [f"wc_d{which}_7", "beta_bf"]
                fn = (lambda ba, mm, kk: lambda e: e.matmul(
                    bank(4)[:, mm:mm + 1],
                    wbeta[:, ba + 512 * kk + 128 * mm:
                          ba + 512 * kk + 128 * (mm + 1)],
                    beta_bf[:, kk:kk + 1], start=(kk == 0),
                    stop=(kk == 7)))(base, m, k)
                ev = f"biasmm_{which}" if (m == 3 and k == 7) else None
                sc.emit("tensor", fn, waits=wt, inc=("P", 1) if ev else None,
                        ev=ev)
        fn = (lambda vc: lambda e: e.tensor_copy(
            vecs[:, vc:vc + 4], bank(4)[:, 0:4]))(vcol)
        sc.emit("vector", fn, waits=[f"biasmm_{which}"], inc=("V", 1),
                ev=f"biasev_{which}")
        banks.release(4, f"biasev_{which}")

    for which, base, dstf in (("q", 0, wdqp), ("kv", 4096, wdkvp)):
        for k in range(8):
            fn = (lambda ba, kk, df: lambda e: e.tensor_scalar(
                df(kk), wbeta[:, ba + 512 * kk:ba + 512 * (kk + 1)],
                vecs[:, kk:kk + 1], None, ALU.mult))(base, k, dstf)
            sc.emit("vector", fn,
                    waits=[f"biasev_{which}"] if k == 0 else (),
                    inc=("V", 1), ev=f"wsc_{which}_{k}")

    sc.emit("tensor", lambda e: e.matmul(
        bank(4)[:, 0:8], bmask[0:1, 0:128], vecs[0:1, 32:40],
        start=True, stop=True),
        waits=["ph0last"] + banks.acquire(4), inc=("P", 1), ev="rbmm")
    sc.emit("vector", lambda e: e.tensor_copy(vecs[:, 40:48], bank(4)[:, 0:8]),
            waits=["rbmm"], inc=("V", 1), ev="rbev")
    banks.release(4, "rbev")

    sc.emit("vector", lambda e: e.memset(vs_r[:, :], 0.0), inc=("V", 1))
    on64 = bass.AP(sbt["vs_r"], 64, [[16 * VST, 128], [VST, 16]])
    on97 = bass.AP(sbt["vs_r"], 97, [[16 * VST, 128], [VST, 16]])
    sc.emit("vector", (lambda a: lambda e: e.memset(a, 1.0))(on64),
            inc=("V", 1))
    sc.emit("vector", (lambda a: lambda e: e.memset(a, 1.0))(on97),
            inc=("V", 1), ev="vsr_init")

    turn = [0]
    slot_free = {}

    def take_turn(free_ev):
        s = turn[0] % 3
        turn[0] += 1
        w = [slot_free[s]] if s in slot_free else []
        slot_free[s] = free_ev
        return s, w

    slot_free[2] = "rbev"

    last_ex_ev = [None]

    def ph1_gen():
        def phA(cc):
            if cc < 2:
                return
            t0 = 4 * cc
            for j in range(4):
                ti = t0 + j
                qd = "gpsimd" if ti % 2 == 0 else "sync"
                sc.dma(qd, x_ring(ti), din["xb"][128 * ti:128 * (ti + 1), :],
                       waits=rings.acquire(("x", ti % 8)) +
                       rings.acquire(("x2", ti % 8)), ev=f"xld_{ti}",
                       sem=f"xld{ti % 8}")

        def phB(cc):
            is_qc = cc < 4
            t0 = 4 * cc
            for j in range(4):
                ti = t0 + j
                sc.emit("vector", (lambda t: lambda e: e.reduce_sum(
                    st_s[:, t:t + 1], x_ring(t), axis=AX.X))(ti),
                    waits=[f"xld_{ti}"], inc=("V", 1), ev=f"xsum_{ti}")
                sc.emit("scalar", (lambda t: lambda e: e.activation(
                    sqd[:, 0:1024], x_ring(t), AF.Square,
                    accum_out=st_q[:, t:t + 1]))(ti),
                    waits=[f"xld_{ti}"], inc=("A", 1), ev=f"sq1_{ti}")
            c4 = slice(4 * cc, 4 * cc + 4)
            sc.emit("vector", (lambda c: lambda e: e.tensor_scalar(
                st_nm1[:, c], st_s[:, c], -1.0 / 1024, None,
                ALU.mult))(c4), inc=("V", 1))
            sc.emit("vector", (lambda c: lambda e: e.tensor_tensor(
                st_v[:, c], st_nm1[:, c], st_nm1[:, c], ALU.mult))(c4),
                inc=("V", 1))
            sc.emit("vector", (lambda c: lambda e: e.scalar_tensor_tensor(
                st_v[:, c], st_q[:, c], 1.0 / 1024, st_v[:, c], ALU.mult,
                ALU.subtract))(c4), waits=[f"sq1_{t0 + 3}"], inc=("V", 1),
                ev=f"var1_{cc}")
            sc.emit("scalar", (lambda c: lambda e: e.activation(
                st_r1[:, c], st_v[:, c], AF.Ln, bias=cst[:, 0:1]))(c4),
                waits=[f"var1_{cc}"], inc=("A", 1))
            sc.emit("scalar", (lambda c: lambda e: e.activation(
                st_r1[:, c], st_r1[:, c], AF.Exp, scale=-0.5))(c4),
                inc=("A", 1), ev=f"rstd1_{cc}")
            sc.emit("vector", (lambda c: lambda e: e.tensor_tensor(
                st_nm2[:, c], st_r1[:, c], st_r1[:, c], ALU.mult))(c4),
                waits=[f"rstd1_{cc}"], inc=("V", 1))
            sc.emit("vector", (lambda c: lambda e: e.tensor_tensor(
                st_nm2[:, c], st_nm2[:, c], st_v[:, c], ALU.mult))(c4),
                inc=("V", 1), ev=f"var2_{cc}")
            sc.emit("scalar", (lambda c: lambda e: e.activation(
                st_r2[:, c], st_nm2[:, c], AF.Ln, bias=cst[:, 0:1]))(c4),
                waits=[f"var2_{cc}"], inc=("A", 1))
            sc.emit("scalar", (lambda c: lambda e: e.activation(
                st_r2[:, c], st_r2[:, c], AF.Exp, scale=-0.5))(c4),
                inc=("A", 1), ev=f"rstd2_{cc}")
            sc.emit("vector", (lambda c: lambda e: e.tensor_tensor(
                st_r2[:, c], st_r2[:, c], st_r1[:, c], ALU.mult))(c4),
                waits=[f"rstd2_{cc}"], inc=("V", 1))
            sc.emit("vector", (lambda c: lambda e: e.tensor_tensor(
                st_nm2[:, c], st_nm1[:, c], st_r2[:, c], ALU.mult))(c4),
                inc=("V", 1), ev=f"aab_{cc}")
            if is_qc:
                sc.emit("vector", (lambda c: lambda e: e.tensor_tensor(
                    st_v[:, c], st_nm1[:, c], st_r1[:, c], ALU.mult))(c4),
                    inc=("V", 1), ev=f"rb1_{cc}")

        def phD(cc, dpair):
            is_qc = cc < 4
            t0 = 4 * cc
            s, sw = take_turn(f"trE_{t0 + 2 * dpair + 1}")
            for j2 in range(2):
                ti = t0 + 2 * dpair + j2
                w = [f"aab_{cc}"] + rings.acquire(("xnh", ti % 2))
                sc.emit("scalar", (lambda t: lambda e: e.activation(
                    xnh_ring(t), x_ring(t), AF.Identity,
                    scale=st_r2[:, t:t + 1],
                    bias=st_nm2[:, t:t + 1]))(ti),
                    waits=w, inc=("A", 1), ev=f"xnh_{ti}")
                if is_qc:
                    w3 = [f"rb1_{cc}"] + rings.acquire(("xnb", ti % 4))
                    sc.emit("scalar", (lambda t: lambda e: e.activation(
                        xnb_ring(t), x_ring(t), AF.Identity,
                        scale=st_r1[:, t:t + 1],
                        bias=st_v[:, t:t + 1]))(ti), waits=w3,
                        inc=("A", 1), ev=f"xnb_{ti}")
                rings.release(("x", ti % 8),
                              f"xnb_{ti}" if is_qc else f"xnh_{ti}")
                rings.last[("x2", ti % 8)] = f"xsum_{ti}"
                bb = 2 * s + j2
                for kk in range(8):
                    wt = []
                    if kk == 0:
                        wt = [f"xnh_{ti}"] + (list(sw) if j2 == 0 else [])
                    fn = (lambda t, k2, b2: lambda e: e.transpose(
                        bank_bf(b2)[:, 128 * k2:128 * (k2 + 1)],
                        xnh_ring(t)[:, 128 * k2:128 * (k2 + 1)],
                        ident_bf[:, :]))(ti, kk, bb)
                    sc.emit("tensor", fn, waits=wt,
                            inc=("P", 1) if kk == 7 else None,
                            ev=f"trT_{ti}" if kk == 7 else None)
            for j2 in range(2):
                ti = t0 + 2 * dpair + j2
                j = 2 * dpair + j2
                bb = 2 * s + j2
                dv = xhT_buf(cc).rearrange("p (k t) -> p k t", t=512)[
                    :, :, 128 * j:128 * (j + 1)]
                fn = (lambda d, b2: lambda e: e.tensor_copy(
                    d, bank_bf(b2).rearrange("p (k t) -> p k t",
                                             t=128)))(dv, bb)
                w2 = [f"trT_{ti}"] + \
                    (rings.acquire(("xhT", cc % 2)) if j == 0 else [])
                sc.emit("vector", fn, waits=w2, inc=("V", 1),
                        ev=f"trE_{ti}")
                rings.release(("xnh", ti % 2), f"trE_{ti}")

        def phE(cc, epair, pname):
            t0 = 4 * cc
            wfn = wdqp if pname == "cq" else wdkvp
            s, sw = take_turn(f"{pname}E_{cc}_{2 * epair + 1}")
            for m2 in range(2):
                m = 2 * epair + m2
                for k in range(8):
                    wt = []
                    if k == 0:
                        wt = (list(sw) if m2 == 0 else []) + \
                            [f"trE_{t0 + 3}"]
                        if cc == 0 and m == 0:
                            wt = wt + (["wsc_q_7"] if pname == "cq"
                                       else ["wsc_kv_7"])
                    fn = (lambda mm, kk, wf, c, b2: lambda e: e.matmul(
                        bank(b2), wf(kk)[:, 128 * mm:128 * (mm + 1)],
                        xhT_buf(c)[:, 512 * kk:512 * (kk + 1)],
                        start=(kk == 0), stop=(kk == 7)))(
                            m, k, wfn, cc, 2 * s + m2)
                    ev = f"{pname}M_{cc}_{m}" if k == 7 else None
                    sc.emit("tensor", fn, waits=wt,
                            inc=("P", 1) if ev else None, ev=ev)
            for m2 in range(2):
                m = 2 * epair + m2
                if pname == "ckv":
                    w4 = [f"ckvM_{cc}_{m}"]
                    if m == 0:
                        w4 += rings.acquire(("ckvsb", cc % 2))
                        if cc >= 2:
                            w4.append(f"vprj_0_{cc - 2}_1")
                    fn = (lambda c, mm, b2: lambda e: e.tensor_scalar(
                        ckv_sb(c)[:, 512 * mm:512 * (mm + 1)], bank(b2),
                        vecs[:, 28 + mm:29 + mm], None, ALU.add))(
                            cc, m, 2 * s + m2)
                    sc.emit("vector", fn, waits=w4, inc=("V", 1),
                            ev=f"ckvE_{cc}_{m}")
                    sc.dma("gpsimd" if cc % 2 == 0 else "sync",
                           ckv_h[128 * m:128 * (m + 1),
                                 512 * cc:512 * (cc + 1)],
                           ckv_sb(cc)[:, 512 * m:512 * (m + 1)],
                           waits=[f"ckvE_{cc}_{m}"], ev=f"ckvD_{cc}_{m}",
                           sem=f"sb{cc % 2}")
                    if m == 3:
                        rings.release(("ckvsb", cc % 2), f"ckvD_{cc}_3")
                else:
                    fn = (lambda c3, mm, b2: lambda e: e.tensor_scalar(
                        cq_keep[:, 2048 * mm + 512 * c3:
                                2048 * mm + 512 * (c3 + 1)],
                        bank(b2), vecs[:, 24 + mm:25 + mm], None,
                        ALU.add))(cc, m, 2 * s + m2)
                    sc.emit("vector", fn, waits=[f"cqM_{cc}_{m}"],
                            inc=("V", 1), ev=f"cqE_{cc}_{m}")

        def phF(cc, fpair):
            t0 = 4 * cc
            s, sw = take_turn(f"xnTe_{t0 + 2 * fpair + 1}")
            for j2 in range(2):
                ti = t0 + 2 * fpair + j2
                bb = 2 * s + j2
                for kk in range(8):
                    wt = []
                    if kk == 0:
                        wt = [f"xnb_{ti}"] + \
                            (list(sw) if j2 == 0 else [])
                    fn = (lambda t, k2, b2: lambda e: e.transpose(
                        bank_bf(b2)[:, 128 * k2:128 * (k2 + 1)],
                        xnb_ring(t)[:, 128 * k2:128 * (k2 + 1)],
                        ident_bf[:, :]))(ti, kk, bb)
                    sc.emit("tensor", fn, waits=wt,
                            inc=("P", 1) if kk == 7 else None,
                            ev=f"xnT_{ti}" if kk == 7 else None)
            for j2 in range(2):
                ti = t0 + 2 * fpair + j2
                bb = 2 * s + j2
                w5 = [f"xnT_{ti}"] + rings.acquire(("xnts", ti % 2))
                fn = (lambda t, b2: lambda e: e.tensor_copy(
                    xnts(t), bank_bf(b2)))(ti, bb)
                sc.emit("vector", fn, waits=w5, inc=("V", 1),
                        ev=f"xnTe_{ti}")
                rings.release(("xnb", ti % 4), f"xnTe_{ti}")
                dstv = xnt_h.rearrange("(k p) t -> p k t", p=128)[
                    :, :, 128 * ti:128 * (ti + 1)]
                srcv = xnts(ti).rearrange("p (k t) -> p k t", t=128)
                sc.dma("gpsimd", dstv, srcv, waits=[f"xnTe_{ti}"],
                       ev=f"xnTd_{ti}", sem=f"xt{ti % 2}")
                rings.release(("xnts", ti % 2), f"xnTd_{ti}")

        NCHK = 16
        for i in range(NCHK + 3):
            cc = i - 3
            if 0 <= cc < NCHK:
                for ep in range(2):
                    phE(cc, ep, "ckv")
                    yield
                if cc < 4:
                    for ep in range(2):
                        phE(cc, ep, "cq")
                        yield
                rings.release(("xhT", cc % 2),
                              f"cqM_{cc}_3" if cc < 4 else f"ckvM_{cc}_3")
                yield ("E", cc)
                if cc < 4:
                    for fp in range(2):
                        phF(cc, fp)
                        yield
            if 2 <= i < NCHK + 2:
                for dp in range(2):
                    phD(i - 2, dp)
                    yield
            if 1 <= i < NCHK + 1:
                phB(i - 1)
            yield
            if i < NCHK:
                phA(i)
            yield
        sc.emit("vector", lambda e: e.memset(rec_sb[:, 0:1], 0.0),
                waits=["xnh_63"], inc=("V", 1), ev="xfree")

    def ffn_w_gen():
        for k in range(8):
            w = rings.acquire(("stage",)) + (["xfree"] if k == 0 else [])
            sc.dma("gpsimd", wstage_f[:, 0:1024],
                   din["base_w_t"][128 * k:128 * (k + 1), :], waits=w,
                   ev=f"fwl_{k}", sem="stg2")
            fn = (lambda kk: lambda e: e.tensor_copy(
                ffn_basew[:, 1024 * kk:1024 * (kk + 1)],
                wstage_f[:, 0:1024]))(k)
            sc.emit("vector", fn, waits=[f"fwl_{k}"], inc=("V", 1),
                    ev=f"fwc_{k}")
            rings.release(("stage",), f"fwc_{k}")
            if k % 2 == 1:
                yield
        srcA = din["lora_A"].rearrange("(k p) r -> p k r", p=128)
        sc.dma("gpsimd", wstage_f[:, 0:64].rearrange("p (k r) -> p k r", r=8),
               srcA, waits=rings.acquire(("stage",)), ev="flA", sem="stg2")
        sc.emit("vector", lambda e: e.tensor_copy(ffn_loraA[:, :],
                                                  wstage_f[:, 0:64]),
                waits=["flA"], inc=("V", 1), ev="fcA")
        rings.release(("stage",), "fcA")
        sc.dma("gpsimd", wstage_f[0:64, 0:1024], din["lora_Bp"],
               waits=rings.acquire(("stage",)), ev="flB", sem="stg2")
        sc.emit("vector", lambda e: e.tensor_copy(ffn_loraB[:, :],
                                                  wstage_f[0:64, 0:1024]),
                waits=["flB"], inc=("V", 1), ev="fcB")
        rings.release(("stage",), "fcB")
        srcR = din["rout_t"].rearrange("(k p) r -> p k r", p=128)
        sc.dma("gpsimd", wstage_f[:, 0:64].rearrange("p (k r) -> p k r", r=8),
               srcR, waits=rings.acquire(("stage",)), ev="flR", sem="stg2")
        sc.emit("vector", lambda e: e.tensor_copy(ffn_routw[:, :],
                                                  wstage_f[:, 0:64]),
                waits=["flR"], inc=("V", 1), ev="fcR")
        rings.release(("stage",), "fcR")
        yield
        for k in range(8):
            w = rings.acquire(("fxnt", k)) + ["xfree", "xnTd_14", "xnTd_15"]
            sc.dma("gpsimd", ffn_xnt(k),
                   xnt_h[128 * k:128 * (k + 1), 0:512],
                   waits=w, ev=f"fx_0_{k}", sem=f"fxd{k}")
            if k % 4 == 3:
                yield

    def ffn_gen():
        for n in range(4):
            if n > 0:
                for k in range(8):
                    w = rings.acquire(("fxnt", k))
                    sc.dma("gpsimd", ffn_xnt(k),
                           xnt_h[128 * k:128 * (k + 1), 512 * n:512 * (n + 1)],
                           waits=w, ev=f"fx_{n}_{k}",
                           sem=f"fxd{k}")
            yield
            for j in range(4):
                m = 4 * n + j
                s, sw = take_turn(f"logE_{m}")
                for k in range(8):
                    wt = []
                    if j == 0:
                        wt += [f"fx_{n}_{k}"]
                    if k == 0:
                        wt += list(sw)
                        if m == 0:
                            wt += ["fcR", "rbev"]
                    fn = (lambda jj, kk, ss: lambda e: e.matmul(
                        bank(2 * ss)[:, 0:8],
                        ffn_xnt(kk)[:, 128 * jj:128 * (jj + 1)],
                        ffn_routw[:, 8 * kk:8 * (kk + 1)],
                        start=(kk == 0), stop=(kk == 7)))(j, k, s)
                    sc.emit("tensor", fn, waits=wt,
                            inc=("P", 1) if k == 7 else None,
                            ev=f"logM_{m}" if k == 7 else None)
                fn = (lambda mm, ss: lambda e: e.tensor_tensor(
                    ffn_logits[:, 8 * mm:8 * (mm + 1)], bank(2 * ss)[:, 0:8],
                    vecs[:, 40:48], ALU.add))(m, s)
                sc.emit("vector", fn, waits=[f"logM_{m}"], inc=("V", 1),
                        ev=f"logE_{m}")
                fn = (lambda mm: lambda e: e.reduce_max(
                    st_v[:, mm:mm + 1], ffn_logits[:, 8 * mm:8 * (mm + 1)],
                    axis=AX.X))(m)
                sc.emit("vector", fn, inc=("V", 1))
                fn = (lambda mm: lambda e: e.tensor_scalar(
                    st_s[:, mm:mm + 1], st_v[:, mm:mm + 1], -1.0, None,
                    ALU.mult))(m)
                sc.emit("vector", fn, inc=("V", 1), ev=f"gmx_{m}")
                fn = (lambda mm: lambda e: e.activation(
                    stage2[:, 16:24], ffn_logits[:, 8 * mm:8 * (mm + 1)],
                    AF.Exp, bias=st_s[:, mm:mm + 1],
                    accum_out=st_q[:, mm:mm + 1]))(m)
                sc.emit("scalar", fn, waits=[f"gmx_{m}"], inc=("A", 1),
                        ev=f"gex_{m}")
                fn = (lambda mm: lambda e: e.reciprocal(
                    st_v[:, mm:mm + 1], st_q[:, mm:mm + 1]))(m)
                sc.emit("vector", fn, waits=[f"gex_{m}"], inc=("V", 1))
                fn = (lambda mm: lambda e: e.tensor_scalar(
                    ffn_gat[:, 8 * mm:8 * (mm + 1)], stage2[:, 16:24],
                    st_v[:, mm:mm + 1], None, ALU.mult))(m)
                sc.emit("vector", fn, inc=("V", 1), ev=f"gat_{m}")
                yield
            s, sw = take_turn(f"midE_{n}")
            for k in range(8):
                wt = (list(sw) + ["fcA"]) if k == 0 else []
                fn = (lambda kk, ss: lambda e: e.matmul(
                    bank(2 * ss)[0:8, :], ffn_loraA[:, 8 * kk:8 * (kk + 1)],
                    ffn_xnt(kk)[:, :], start=(kk == 0), stop=(kk == 7)))(k, s)
                sc.emit("tensor", fn, waits=wt,
                        inc=("P", 1) if k == 7 else None,
                        ev=f"midM_{n}" if k == 7 else None)
                if k == 3:
                    yield "cont"
            fn = (lambda nn, ss: lambda e: e.tensor_copy(
                ffn_midt[:, 512 * nn:512 * (nn + 1)],
                bank(2 * ss)[0:8, :]))(n, s)
            sc.emit("vector", fn, waits=[f"midM_{n}"], inc=("V", 1),
                    ev=f"midE_{n}")
            for k in range(8):
                rings.release(("fxnt", k), f"midE_{n}")
            yield
        s, sw = take_turn("gtE")
        for m in range(16):
            wt = []
            if m == 0:
                wt = list(sw) + [f"gat_{mm}" for mm in range(16)]
            b3 = 2 * s + (m // 8)
            fn = (lambda mm, bb: lambda e: e.transpose(
                bank_bf(bb)[0:8, 128 * (mm % 8):128 * ((mm % 8) + 1)],
                ffn_gat[:, 8 * mm:8 * (mm + 1)], ident_bf[:, :]))(m, b3)
            sc.emit("tensor", fn, waits=wt, inc=("P", 1) if m == 15 else None,
                    ev="gtT" if m == 15 else None)
        fn = (lambda ss: lambda e: e.tensor_copy(
            ffn_gt[:, 0:1024], bank_bf(2 * ss)[0:8, :]))(s)
        sc.emit("vector", fn, waits=["gtT"], inc=("V", 1))
        fn = (lambda ss: lambda e: e.tensor_copy(
            ffn_gt[:, 1024:2048], bank_bf(2 * ss + 1)[0:8, :]))(s)
        sc.emit("vector", fn, inc=("V", 1), ev="gtE")
        yield
        for n in range(4):
            s, sw = take_turn(f"gm_{n}")
            fn = (lambda nn, ss: lambda e: e.matmul(
                bank(2 * ss)[0:64, :], emat[:, :],
                ffn_gt[:, 512 * nn:512 * (nn + 1)], start=True,
                stop=True))(n, s)
            sc.emit("tensor", fn, waits=list(sw) + ["gtE"], inc=("P", 1),
                    ev=f"grepM_{n}")
            fn = (lambda nn, ss: lambda e: e.matmul(
                bank(2 * ss + 1)[0:64, :], mmat[:, :],
                ffn_midt[:, 512 * nn:512 * (nn + 1)], start=True,
                stop=True))(n, s)
            sc.emit("tensor", fn, waits=[f"midE_{n}"], inc=("P", 1),
                    ev=f"mrepM_{n}")
            fn = (lambda nn, ss: lambda e: e.tensor_copy(
                ffn_grep[:, :], bank(2 * ss)[0:64, :]))(n, s)
            sc.emit("vector", fn, waits=[f"grepM_{n}"], inc=("V", 1),
                    ev=f"grepE_{n}")
            fn = (lambda nn, ss: lambda e: e.tensor_tensor(
                ffn_gm[:, 512 * nn:512 * (nn + 1)], bank(2 * ss + 1)[0:64, :],
                ffn_grep[:, :], ALU.mult))(n, s)
            sc.emit("vector", fn, waits=[f"mrepM_{n}", f"grepE_{n}"],
                    inc=("V", 1), ev=f"gm_{n}")
            yield
        for n in range(4):
            for k in range(8):
                sc.dma("gpsimd", ffn_xnt(k),
                       xnt_h[128 * k:128 * (k + 1), 512 * n:512 * (n + 1)],
                       waits=rings.acquire(("fxnt", k)), ev=f"fx2_{n}_{k}",
                       sem=f"fxd{k}")
            yield
            for m in range(8):
                ii = n * 8 + m
                s, sw = take_turn(f"u_{n}_{m}")
                wt = list(sw) + [f"gm_{n}"]
                if n == 0 and m == 0:
                    wt += ["fcB"]
                fn = (lambda nn, mm, ss: lambda e: e.matmul(
                    bank(2 * ss + 1), ffn_loraB[:, 128 * mm:128 * (mm + 1)],
                    ffn_gm[:, 512 * nn:512 * (nn + 1)], start=True,
                    stop=True))(n, m, s)
                sc.emit("tensor", fn, waits=wt, inc=("P", 1),
                        ev=f"lraM_{n}_{m}")
                fn = (lambda ss: lambda e: e.tensor_copy(
                    ffn_lsb[:, :], bank(2 * ss + 1)))(s)
                sc.emit("vector", fn, waits=[f"lraM_{n}_{m}"], inc=("V", 1),
                        ev=f"lraE_{n}_{m}")
                for k in range(8):
                    wt = []
                    if m == 0:
                        wt += [f"fx2_{n}_{k}"]
                    if k == 0:
                        wt += ["fwc_7"]
                    fn = (lambda mm, kk, ss: lambda e: e.matmul(
                        bank(2 * ss), ffn_basew[:, 1024 * kk + 128 * mm:
                                                1024 * kk + 128 * (mm + 1)],
                        ffn_xnt(kk)[:, :], start=(kk == 0),
                        stop=(kk == 7)))(m, k, s)
                    sc.emit("tensor", fn, waits=wt,
                            inc=("P", 1) if k == 7 else None,
                            ev=f"basM_{n}_{m}" if k == 7 else None)
                    if k == 3:
                        yield "cont"
                fn = (lambda mm, ii2, ss: lambda e: e.scalar_tensor_tensor(
                    ffn_u(ii2), bank(2 * ss), vecs[:, 16 + mm:17 + mm],
                    ffn_lsb[:, :], ALU.add, ALU.add))(m, ii, s)
                w6 = [f"basM_{n}_{m}", f"lraE_{n}_{m}"] + \
                    rings.acquire(("fu", ii % 2))
                sc.emit("vector", fn, waits=w6, inc=("V", 1), ev=f"u_{n}_{m}")
                w7 = [f"u_{n}_{m}"]
                if ii >= 1:
                    w7.append(f"sg1_{(ii - 1) // 8}_{(ii - 1) % 8}")
                fn = (lambda ii2: lambda e: e.activation(
                    ffn_sig[:, :], ffn_u(ii2), AF.Exp, scale=-1.0))(ii)
                sc.emit("scalar", fn, waits=w7, inc=("A", 1), ev=f"es_{n}_{m}")
                fn = (lambda: lambda e: e.tensor_scalar(
                    stage2[:, 0:512], ffn_sig[:, :], 1.0, None, ALU.add))()
                sc.emit("vector", fn, waits=[f"es_{n}_{m}"], inc=("V", 1),
                        ev=f"sg1_{n}_{m}")
                fn = (lambda: lambda e: e.reciprocal(stage2[:, 512:1024],
                                                     stage2[:, 0:512]))()
                sc.emit("vector", fn, inc=("V", 1))
                fn = (lambda ii2: lambda e: e.tensor_tensor(
                    ffn_gout(ii2), ffn_u(ii2), stage2[:, 512:1024],
                    ALU.mult))(ii)
                w8 = rings.acquire(("gout", ii % 2))
                sc.emit("vector", fn, waits=w8, inc=("V", 1), ev=f"g_{n}_{m}")
                rings.release(("fu", ii % 2), f"g_{n}_{m}")
                sc.dma("gpsimd",
                       g_h[128 * m:128 * (m + 1), 512 * n:512 * (n + 1)],
                       ffn_gout(ii), waits=[f"g_{n}_{m}"], ev=f"gD_{n}_{m}",
                       sem=f"gd{ii % 2}")
                rings.release(("gout", ii % 2), f"gD_{n}_{m}")
                yield
            for k in range(8):
                rings.release(("fxnt", k), f"basM_{n}_{7}")

    def build_gen(g):
        par = g % 2
        if g >= 1:
            for _ in range(40):
                yield
        wuw = []
        if g >= 2:
            wuw = [f"qprj_{g - 2}_3"]
        for wi, wname in enumerate(("w_uk_t", "w_uv_t", "w_uq_t")):
            rearr = din[wname].rearrange("(k p) c -> p k c", p=128)
            srcw = rearr[:, :, 128 * g:128 * (g + 1)]
            w = rings.acquire(("stage",)) + (wuw if wi == 0 else [])
            dstw = stage[:, 0:512].rearrange("p (k c) -> p k c", c=128)
            sc.dma("gpsimd", dstw, srcw, waits=w, ev=f"wul_{g}_{wi}",
                   sem="stg2")
            fn = (lambda wi2: lambda e: e.tensor_copy(
                wu_d[:, 1536 * par + 512 * wi2:1536 * par + 512 * (wi2 + 1)],
                stage[:, 0:512]))(wi)
            sc.emit("vector", fn, waits=[f"wul_{g}_{wi}"], inc=("V", 1),
                    ev=f"wuc_{g}_{wi}")
            rings.release(("stage",), f"wuc_{g}_{wi}")
        yield

        def wu_k(kk):
            return wu_d[:, 1536 * par + 128 * kk:1536 * par + 128 * (kk + 1)]

        def wu_v(kk):
            o = 1536 * par + 512
            return wu_d[:, o + 128 * kk:o + 128 * (kk + 1)]

        def wu_q(kk):
            o = 1536 * par + 1024
            return wu_d[:, o + 128 * kk:o + 128 * (kk + 1)]

        def qbuild():
            for qc2 in range(4):
                tsl2 = qc2 % 4
                w = rings.acquire(("trig", tsl2))
                sc.dma("gpsimd", trig_r[:, 1024 * tsl2:1024 * tsl2 + 512],
                           din["trig_cq"][:, 512 * qc2:512 * (qc2 + 1)], waits=w,
                           ev=f"tqc_{g}_{qc2}", sem=f"trg{tsl2}")
                sc.dma("gpsimd", trig_r[:, 1024 * tsl2 + 512:1024 * (tsl2 + 1)],
                           din["trig_sq"][:, 512 * qc2:512 * (qc2 + 1)],
                           ev=f"tqs_{g}_{qc2}", sem=f"trg{tsl2}")
                s, sw = take_turn(f"qab_{g}_{qc2}")
                for k in range(4):
                        wt = []
                        if k == 0:
                            wt = list(sw)
                            if qc2 == 0:
                                    wt = wt + rings.acquire(("qtd", par)) + \
                                        [f"wuc_{g}_2"]
                            if qc2 == 0 and g <= 1:
                                    wt = wt + ["cqE_3_3"]
                        fn = (lambda c2, kk, ss: lambda e: e.matmul(
                            bank(2 * ss), wu_q(kk),
                            cq_keep[:, 2048 * kk + 512 * c2:2048 * kk + 512 * (c2 + 1)],
                            start=(kk == 0), stop=(kk == 3)))(qc2, k, s)
                        sc.emit("tensor", fn, waits=wt,
                                    inc=("P", 1) if k == 3 else None,
                                    ev=f"qprj_{g}_{qc2}" if k == 3 else None)
                asl = qc2 % 2
                wA = rings.acquire(("ab", asl)) + [f"qprj_{g}_{qc2}",
                                                                   f"tqc_{g}_{qc2}",
                                                                   f"tqs_{g}_{qc2}"]
                fn = (lambda a2, t3, ss: lambda e: e.tensor_tensor(
                        ab_buf[:, 1024 * a2:1024 * a2 + 512], bank(2 * ss),
                        trig_r[:, 1024 * t3:1024 * t3 + 512], ALU.mult))(asl, tsl2, s)
                sc.emit("vector", fn, waits=wA, inc=("V", 1))
                fn = (lambda a2, t3, ss: lambda e: e.tensor_tensor(
                        ab_buf[:, 1024 * a2 + 512:1024 * (a2 + 1)], bank(2 * ss),
                        trig_r[:, 1024 * t3 + 512:1024 * (t3 + 1)],
                        ALU.mult))(asl, tsl2, s)
                sc.emit("vector", fn, inc=("V", 1), ev=f"qab_{g}_{qc2}")
                rings.release(("trig", tsl2), f"qab_{g}_{qc2}")
                s2, sw2 = take_turn(f"qt_{g}_{qc2}")
                fn = (lambda a2, ss: lambda e: e.matmul(
                        bank(2 * ss), mrot[:, :],
                        ab_buf[:, 1024 * a2 + 512:1024 * (a2 + 1)],
                        start=True, stop=True))(asl, s2)
                sc.emit("tensor", fn, waits=list(sw2) + [f"qab_{g}_{qc2}"],
                            inc=("P", 1), ev=f"qrot_{g}_{qc2}")
                fn = (lambda c2, a2, ss: lambda e: e.tensor_tensor(
                        qt_d[:, NQ * par + 512 * c2:NQ * par + 512 * (c2 + 1)],
                        bank(2 * ss), ab_buf[:, 1024 * a2:1024 * a2 + 512],
                        ALU.add))(qc2, asl, s2)
                sc.emit("vector", fn, waits=[f"qrot_{g}_{qc2}"], inc=("V", 1),
                            ev=f"qt_{g}_{qc2}")
                rings.release(("ab", asl), f"qt_{g}_{qc2}")
                yield


        if g != 0:
            for _ in qbuild():
                yield

        war = []
        if g >= 2:
            war = [f"qk_{g - 2}_{NCH * NK - 1}", f"avs_{g - 2}_{NCH - 1}"]

        def chunk_dmas(ch):
            for k in (() if g == 0 else range(4)):
                sl = (4 * ch + k) % 12
                w = rings.acquire(("ckvin", sl))
                if g == 1:
                    w = w + [f"ckvD_{ch}_3"]
                sc.dma("gpsimd", ckv_in[:, 512 * sl:512 * (sl + 1)],
                       ckv_h[128 * k:128 * (k + 1), 512 * ch:512 * (ch + 1)],
                       waits=w, ev=f"kin_{g}_{ch}_{k}", sem=f"kin{sl}")
            tsl2 = ch % 4
            w = rings.acquire(("trig", tsl2))
            sc.dma("gpsimd", trig_r[:, 1024 * tsl2:1024 * tsl2 + 512],
                   din["trig_c"][:, 512 * ch:512 * (ch + 1)], waits=w,
                   ev=f"tcd_{g}_{ch}", sem=f"trg{tsl2}")
            sc.dma("gpsimd", trig_r[:, 1024 * tsl2 + 512:1024 * (tsl2 + 1)],
                   din["trig_s"][:, 512 * ch:512 * (ch + 1)],
                   ev=f"tsd_{g}_{ch}", sem=f"trg{tsl2}")

        chunk_dmas(0)
        yield
        chunk_dmas(1)
        yield
        for ch in range(16):
            tsl = ch % 4
            if ch + 2 < 16:
                chunk_dmas(ch + 2)
            yield
            s, sw = take_turn(f"kt_{g}_{ch}")

            def ckv_rhs(kk, lo, sz, c2=ch):
                if g == 0:
                    return ckv_sb(c2)[:, 512 * kk + lo:512 * kk + lo + sz]
                sl2 = (4 * c2 + kk) % 12
                return ckv_in[:, 512 * sl2 + lo:512 * sl2 + lo + sz]

            for k in range(4):
                wt = []
                if k == 0:
                    wt = list(sw)
                    wt += ([f"ckvE_{ch}_3"] if g == 0 else
                           [f"kin_{g}_{ch}_{k2}" for k2 in range(4)])
                    if ch == 0:
                        wt = wt + [f"wuc_{g}_2"]
                fn = (lambda kk, ss, rhs: lambda e: e.matmul(
                    bank(2 * ss), wu_k(kk), rhs,
                    start=(kk == 0), stop=(kk == 3)))(k, s, ckv_rhs(k, 0, 512))
                sc.emit("tensor", fn, waits=wt,
                        inc=("P", 1) if k == 3 else None,
                        ev=f"kprj_{g}_{ch}" if k == 3 else None)
                if k == 1:
                    yield "cont"
            asl = ch % 2
            wA = rings.acquire(("ab", asl)) + \
                [f"kprj_{g}_{ch}", f"tcd_{g}_{ch}", f"tsd_{g}_{ch}"]
            fn = (lambda a2, t3, ss: lambda e: e.tensor_tensor(
                ab_buf[:, 1024 * a2:1024 * a2 + 512], bank(2 * ss),
                trig_r[:, 1024 * t3:1024 * t3 + 512], ALU.mult))(asl, tsl, s)
            sc.emit("vector", fn, waits=wA, inc=("V", 1))
            fn = (lambda a2, t3, ss: lambda e: e.tensor_tensor(
                ab_buf[:, 1024 * a2 + 512:1024 * (a2 + 1)], bank(2 * ss),
                trig_r[:, 1024 * t3 + 512:1024 * (t3 + 1)],
                ALU.mult))(asl, tsl, s)
            sc.emit("vector", fn, inc=("V", 1), ev=f"kab_{g}_{ch}")
            rings.release(("trig", tsl), f"kab_{g}_{ch}")
            fn = (lambda a2, ss: lambda e: e.matmul(
                bank(2 * ss + 1), mrot[:, :],
                ab_buf[:, 1024 * a2 + 512:1024 * (a2 + 1)],
                start=True, stop=True))(asl, s)
            sc.emit("tensor", fn, waits=[f"kab_{g}_{ch}"], inc=("P", 1),
                    ev=f"krot_{g}_{ch}")
            ksl = ch % 2
            wkt = rings.acquire(("kts", ksl)) + [f"krot_{g}_{ch}"]
            fn = (lambda a2, ss, kl: lambda e: e.tensor_tensor(
                kts[:, 512 * kl:512 * (kl + 1)], bank(2 * ss + 1),
                ab_buf[:, 1024 * a2:1024 * a2 + 512], ALU.add))(asl, s, ksl)
            sc.emit("vector", fn, waits=wkt, inc=("V", 1), ev=f"kt_{g}_{ch}")
            rings.release(("ab", asl), f"kt_{g}_{ch}")
            w = [f"kt_{g}_{ch}"] + (war if ch == 0 else [])
            sc.dma("gpsimd",
                   kt_h[128 * par:128 * (par + 1), 512 * ch:512 * (ch + 1)],
                   kts[:, 512 * ksl:512 * (ksl + 1)], waits=w,
                   ev=f"kto_{g}_{ch}", sem=f"kto{ksl}")
            rings.release(("kts", ksl), f"kto_{g}_{ch}")
            yield
            vsl = ch % 2
            wv0 = rings.acquire(("vss", vsl))
            for half in range(2):
                s, sw = take_turn(f"vse_{g}_{ch}_{half}")
                for mj2 in range(2):
                    mj = 2 * half + mj2
                    kt2 = 4 * ch + mj
                    for k in range(4):
                        wt = []
                        if k == 0:
                            wt = list(sw) if mj2 == 0 else []
                            if half == 0 and mj2 == 0:
                                wt = wt + wv0
                        fn = (lambda kk, bb, rhs: lambda e: e.matmul(
                            ps[:, 512 * bb:512 * bb + 128], rhs,
                            wu_v(kk), start=(kk == 0), stop=(kk == 3)))(
                                k, 2 * s + mj2, ckv_rhs(k, 128 * mj, 128))
                        sc.emit("tensor", fn, waits=wt,
                                inc=("P", 1) if (k == 3 and mj2 == 1) else None,
                                ev=f"vprj_{g}_{ch}_{half}"
                                if (k == 3 and mj2 == 1) else None)
                for mj2 in range(2):
                    mj = 2 * half + mj2
                    fn = (lambda mj3, bb, vl: lambda e: e.tensor_copy(
                        vss[:, 512 * vl + 128 * mj3:512 * vl + 128 * (mj3 + 1)],
                        ps[:, 512 * bb:512 * bb + 128]))(mj, 2 * s + mj2, vsl)
                    sc.emit("vector", fn,
                            waits=[f"vprj_{g}_{ch}_{half}"] if mj2 == 0 else [],
                            inc=("V", 1),
                            ev=f"vse_{g}_{ch}_{half}" if mj2 == 1 else None)
                yield
            VH = NK * 128
            srcA = bass.AP(sbt["vss"], 512 * vsl, [[1024, 128], [128, 4], [1, 64]])
            dstA = bass.AP(vs_h.tensor, (128 * par) * VH + 128 * (4 * ch),
                           [[VH, 128], [128, 4], [1, 64]])
            sc.dma("gpsimd", dstA, srcA, waits=[f"vse_{g}_{ch}_1"],
                   ev=f"vso_{g}_{ch}_A", sem=f"vso{vsl}")
            srcB = bass.AP(sbt["vss"], 512 * vsl + 64, [[1024, 128], [128, 4], [1, 64]])
            dstB = bass.AP(vs_h.tensor,
                           (128 * par) * VH + 128 * (4 * ch) + 64,
                           [[VH, 128], [128, 4], [1, 64]])
            sc.dma("gpsimd", dstB, srcB, waits=[f"vse_{g}_{ch}_1"],
                   ev=f"vso_{g}_{ch}_B", sem=f"vso{vsl}")
            rings.release(("vss", vsl), f"vso_{g}_{ch}_B")
        if g == 0:
            for _ in qbuild():
                yield
    PF = 12

    def emit_stream(g, idx, first_waits):
        par = g % 2
        kk = idx % NK
        sl = idx % 16
        VH = NK * 128
        w = rings.acquire(("vs", sl)) + list(first_waits) + \
            [f"vso_{g}_{kk // 4}_B"]
        if g == 0 and idx < 16:
            w.append("vsr_init")
        dstv = bass.AP(sbt["vs_r"], VST * sl, [[16 * VST, 128], [129, 2], [1, 64]])
        srcv = bass.AP(vs_h.tensor, (128 * par) * VH + 128 * kk,
                       [[VH, 128], [64, 2], [1, 64]])
        sc.dma("sync", dstv, srcv, waits=w, ev=f"vsd_{g}_{idx}",
               sem=f"vs{sl}")

    def emit_ktload(g, kk, first_waits=()):
        par = g % 2
        w = list(first_waits) + [f"kto_{g}_{kk // 4}"]
        if g >= 1:
            w.append(f"qk_{g - 1}_{3 * NK + kk}")
        if kk >= 16:
            w.append(f"qk_{g}_{kk - 16}")
        sc.dma("sync", kt_res[:, 128 * kk:128 * (kk + 1)],
               kt_h[128 * par:128 * (par + 1), 128 * kk:128 * (kk + 1)],
               waits=w, ev=f"ktr_{g}_{kk}", sem=f"ks{kk % 16}")

    def tail_pre_gen():
        for k in range(8):
            w = rings.acquire(("stage",)) + \
                (["gD_3_6", "gD_3_7"] if k == 0 else [])
            sc.dma("gpsimd", tl_wstage[:, 0:1024],
                   din["w_out_t"][128 * k:128 * (k + 1), :], waits=w,
                   ev=f"tlWl_{k}", sem="stg2")
            fn = (lambda kk: lambda e: e.tensor_copy(
                tl_wout[:, 1024 * kk:1024 * (kk + 1)],
                tl_wstage[:, 0:1024]))(k)
            sc.emit("vector", fn, waits=[f"tlWl_{k}"], inc=("V", 1),
                    ev=f"tlW_{k}")
            rings.release(("stage",), f"tlW_{k}")
            yield
        for n in range(2):
            for gg in range(NG - 1):
                w = [f"aod_{gg}_{n}"]
                if gg == 0:
                    w = w + rings.acquire(("tla", n % 2))
                    if n == 0:
                        w = w + ["gD_3_6", "gD_3_7"]
                sc.dma("gpsimd",
                       tl_attn_r(n)[:, 512 * gg:512 * (gg + 1)],
                       attn_h[128 * gg:128 * (gg + 1),
                              512 * n:512 * (n + 1)],
                       waits=w, ev=f"tlA_{n}_{gg}", sem=f"tla{n % 2}")
                if gg % 4 == 3:
                    yield
            for k in range(8):
                w = rings.acquire(("tlg", n % 2))
                if k == 0:
                    w = w + ["gD_3_6", "gD_3_7"]
                sc.dma("gpsimd", tl_gr(n)[:, 512 * k:512 * (k + 1)],
                       g_h[128 * k:128 * (k + 1), 512 * n:512 * (n + 1)],
                       waits=w, ev=f"tlG_{n}_{k}", sem=f"tlg{n % 2}")
                if k % 4 == 3:
                    yield
        for tt in range(2):
            w = rings.acquire(("tlx", tt % 2))
            if tt == 1:
                w = w + ["kprj_7_15", "vprj_7_15_1"]
            sc.dma("gpsimd", tlx_buf(tt),
                   din["xq"][128 * tt:128 * (tt + 1), :], waits=w,
                   ev=f"tlx_{tt}", sem=f"xlq{tt % 2}")
            yield

    ph1g = ph1_gen()
    b0 = build_gen(0)

    def drive_b0(n=1):
        for _ in range(n):
            try:
                next(b0)
            except StopIteration:
                return False
        return True

    def ph1_until(target):
        for tok in ph1g:
            if tok == ("E", target):
                return

    drive_b0(1)
    drive_b0(2)
    for ch in range(16):
        ph1_until(ch)
        drive_b0(5)
    drive_b0(4)
    for _ in ph1g:
        pass
    assert not drive_b0()

    pulls = []
    pull_i = [0]
    cont = [None]
    defq = []

    def pull():
        gen = cont[0]
        cont[0] = None
        if gen is None and defq:
            gen = defq[0]
        if gen is None and pulls:
            gen = pulls[pull_i[0] % len(pulls)]
        if gen is not None:
            try:
                if next(gen) == "cont":
                    cont[0] = gen
            except StopIteration:
                if gen in pulls:
                    pulls.remove(gen)
                if defq and gen is defq[0]:
                    defq.pop(0)
        pull_i[0] += 1

    for g in range(NG):
        TT = NCH * NK
        if g == 0:
            pulls.append(ffn_w_gen())
        if g == 1:
            pulls.append(ffn_gen())
        if g + 1 < NG:
            pulls.append(build_gen(g + 1))
        if g == NG - 1:
            pulls.append(tail_pre_gen())

        bld = [f"qt_{g}_3"]
        qk_meta = []

        def emit_qk(idx, g=g, qk_meta=qk_meta, bld=bld):
            c, k = divmod(idx, NK)
            s, sw = take_turn(f"ex_{g}_{idx}")
            qk_meta.append((s, idx))
            wt = list(sw)
            if idx < NK:
                wt.append(f"ktr_{g}_{k}")
            if idx == 0:
                wt = wt + bld
            fn = (lambda cc2, ss, kk2, par2: lambda e: e.matmul(
                bank(2 * ss), kt_res[0:64, 128 * kk2:128 * (kk2 + 1)],
                qt_d[0:64, NQ * par2 + 512 * cc2:NQ * par2 + 512 * (cc2 + 1)],
                start=True, stop=True))(c, s, k, g % 2)
            sc.emit("tensor", fn, waits=wt)
            fn = (lambda cc2, ss, kk2, par2: lambda e: e.matmul(
                bank(2 * ss + 1), kt_res[64:128, 128 * kk2:128 * (kk2 + 1)],
                qt_d[64:128, NQ * par2 + 512 * cc2:NQ * par2 + 512 * (cc2 + 1)],
                start=True, stop=True))(c, s, k, g % 2)
            sc.emit("tensor", fn, inc=("QK", 1), ev=f"qk_{g}_{idx}")

        def emit_exp(idx, g=g, qk_meta=qk_meta):
            s = qk_meta[idx][0]
            p3 = idx % 3
            wt = [f"qk_{g}_{idx}"]
            fn = (lambda ss, pp: lambda e: e.activation(
                probs[:, 1024 * pp:1024 * pp + 512],
                ps[:, 1024 * ss:1024 * ss + 512], AF.Exp, scale=0.125))(s, p3)
            sc.emit("scalar", fn, waits=wt, inc=("EX", 1), ev=f"exA_{g}_{idx}")
            fn = (lambda ss, pp: lambda e: e.activation(
                probs[:, 1024 * pp + 512:1024 * (pp + 1)],
                ps[:, 1024 * ss + 512:1024 * (ss + 1)], AF.Exp,
                scale=0.125))(s, p3)
            sc.emit("scalar", fn, inc=("EX", 1), ev=f"ex_{g}_{idx}")
            last_ex_ev[0] = f"ex_{g}_{idx}"

        def emit_av(idx, g=g):
            c, k = divmod(idx, NK)
            p3 = idx % 3
            sl = idx % 16
            wt = [f"exA_{g}_{idx}", f"vsd_{g}_{idx}"]
            if k == 0:
                wt = wt + banks.acquire(6) + banks.acquire(7)
            fn = (lambda sll, pp, kk2: lambda e: e.matmul(
                bank(6)[0:65, :], statA_ap(sll),
                probs[:, 1024 * pp:1024 * pp + 512],
                start=(kk2 == 0), stop=(kk2 == NK - 1)))(sl, p3, k)
            sc.emit("tensor", fn, waits=wt)
            fn = (lambda sll, pp, kk2: lambda e: e.matmul(
                bank(7), statB_ap(sll),
                probs[:, 1024 * pp + 512:1024 * (pp + 1)],
                start=(kk2 == 0), stop=(kk2 == NK - 1)))(sl, p3, k)
            sc.emit("tensor", fn, waits=[f"ex_{g}_{idx}"], inc=("AVT", 1))
            ev = f"avs_{g}_{c}" if k == NK - 1 else None
            if ev:
                sc.events[ev] = ("AVT", sc.cnt["AVT"])
            rings.release(("vs", sl), ("#AVT", sc.cnt["AVT"]))
            if k == NK - 1:
                ec = f"avs_{g}_{c}"
                sc.emit("vector", (lambda: lambda e: e.tensor_copy(
                    av_sb[0:65, 0:512], bank(6)[0:65, :]))(), waits=[ec],
                    inc=("V", 1))
                sc.emit("vector", (lambda: lambda e: e.tensor_copy(
                    av_sb[:, 512:1024], bank(7)))(), inc=("V", 1),
                    ev=f"avc_{g}_{c}")
                banks.release(6, f"avc_{g}_{c}")
                banks.release(7, f"avc_{g}_{c}")
                defq.append(epi_gen(c, g))

        def epi_gen(c, g=g):
            sc.emit("scalar", (lambda: lambda e: e.activation(
                rec_sb[64:65, 0:512], av_sb[64:65, 0:512], AF.Ln))(),
                waits=[f"avc_{g}_{c}"], inc=("A", 1))
            sc.emit("scalar", (lambda: lambda e: e.activation(
                rec_sb[64:65, 512:1024], rec_sb[64:65, 0:512], AF.Exp,
                scale=-1.0))(), inc=("A", 1), ev=f"rcpA_{g}_{c}")
            yield
            sc.emit("scalar", (lambda: lambda e: e.activation(
                rec_sb[32:33, 0:512], av_sb[32:33, 512:1024], AF.Ln))(),
                inc=("A", 1))
            sc.emit("scalar", (lambda: lambda e: e.activation(
                rec_sb[32:33, 512:1024], rec_sb[32:33, 0:512], AF.Exp,
                scale=-1.0))(), inc=("A", 1), ev=f"rcp_{g}_{c}")
            yield
            s, sw = take_turn(f"bce_{g}_{c}")
            sc.emit("tensor", (lambda ss: lambda e: e.matmul(
                bank(2 * ss), bmask[64:65, 0:128], rec_sb[64:65, 512:1024],
                start=True, stop=True))(s), waits=list(sw) + [f"rcpA_{g}_{c}"])
            sc.emit("tensor", (lambda ss: lambda e: e.matmul(
                bank(2 * ss + 1), bmask[32:33, 0:128],
                rec_sb[32:33, 512:1024], start=True, stop=True))(s),
                waits=[f"rcp_{g}_{c}"], inc=("P", 1), ev=f"bcm_{g}_{c}")
            sc.emit("vector", (lambda ss: lambda e: e.tensor_copy(
                bc_sb[0:64, :], bank(2 * ss)[0:64, :]))(s),
                waits=[f"bcm_{g}_{c}"], inc=("V", 1))
            sc.emit("vector", (lambda ss: lambda e: e.tensor_copy(
                bc_sb[64:128, :], bank(2 * ss + 1)[64:128, :]))(s),
                inc=("V", 1), ev=f"bce_{g}_{c}")
            yield
            aosl = c % 2
            w7 = rings.acquire(("ao", aosl))
            sc.emit("vector", (lambda a2: lambda e: e.tensor_tensor(
                ao_sb[0:64, 512 * a2:512 * (a2 + 1)], av_sb[0:64, 0:512],
                bc_sb[0:64, :], ALU.mult))(aosl),
                waits=[f"bce_{g}_{c}"] + w7, inc=("V", 1))
            sc.emit("vector", (lambda a2: lambda e: e.tensor_tensor(
                ao_sb[64:128, 512 * a2:512 * (a2 + 1)],
                av_sb[64:128, 512:1024],
                bc_sb[64:128, :], ALU.mult))(aosl),
                inc=("V", 1), ev=f"ep_{g}_{c}")
            sc.dma("gpsimd",
                   attn_h[128 * g:128 * (g + 1), 512 * c:512 * (c + 1)],
                   ao_sb[:, 512 * aosl:512 * (aosl + 1)],
                   waits=[f"ep_{g}_{c}"], ev=f"aod_{g}_{c}", sem=f"ao{aosl}")
            rings.release(("ao", aosl), f"aod_{g}_{c}")

        for kk in range(16):
            emit_ktload(g, kk)
        for pidx in range(PF):
            emit_stream(g, pidx, [])
        emit_qk(0)
        emit_qk(1)
        for idx in range(TT):
            emit_exp(idx)
            pull()
            if idx < NK - 16:
                emit_ktload(g, idx + 16)
            if idx + PF < TT:
                emit_stream(g, idx + PF, [])
            if idx + 2 < TT:
                emit_qk(idx + 2)
            emit_av(idx)
        rings.release(("qtd", g % 2), f"qk_{g}_{TT - 1}")

    while pulls or defq or cont[0] is not None:
        pull()
    for s in range(3):
        if s in slot_free:
            banks.release(2 * s, slot_free[s])
            banks.release(2 * s + 1, slot_free[s])

    for n in range(NCH):
        ggs = [NG - 1] if n < 2 else list(range(NG))
        for gg in ggs:
            w = [f"aod_{gg}_{n}"]
            if n >= 2 and gg == 0:
                w = w + rings.acquire(("tla", n % 2))
            sc.dma("sync", tl_attn_r(n)[:, 512 * gg:512 * (gg + 1)],
                   attn_h[128 * gg:128 * (gg + 1), 512 * n:512 * (n + 1)],
                   waits=w, ev=f"tlA_{n}_{gg}", sem=f"tla{n % 2}")
        if n >= 2:
            for k in range(8):
                w = rings.acquire(("tlg", n % 2))
                sc.dma("gpsimd",
                       tl_gr(n)[:, 512 * k:512 * (k + 1)],
                       g_h[128 * k:128 * (k + 1), 512 * n:512 * (n + 1)],
                       waits=w, ev=f"tlG_{n}_{k}", sem=f"tlg{n % 2}")
        for m in range(8):
            for k in range(8):
                wt = []
                if k == 0:
                    wt = banks.acquire(m % 2)
                    if m <= 1:
                        wt = wt + ["tlW_7"] + \
                            [f"tlA_{n}_{gg}" for gg in range(NG)]
                fn = (lambda nn, mm, kk: lambda e: e.matmul(
                    bank(mm % 2), tl_wout[:, 1024 * kk + 128 * mm:
                                          1024 * kk + 128 * (mm + 1)],
                    tl_attn_r(nn)[:, 512 * kk:512 * (kk + 1)],
                    start=(kk == 0), stop=(kk == 7)))(n, m, k)
                sc.emit("tensor", fn, waits=wt,
                        inc=("P", 1) if k == 7 else None,
                        ev=f"wo_{n}_{m}" if k == 7 else None)
            w8 = [f"wo_{n}_{m}"]
            if m == 0:
                w8 = w8 + rings.acquire(("sT", n % 2))
            fn = (lambda nn, mm: lambda e: e.tensor_copy(
                tl_sT(nn)[:, 512 * mm:512 * (mm + 1)], bank(mm % 2)))(n, m)
            sc.emit("vector", fn, waits=w8, inc=("V", 1), ev=f"woE_{n}_{m}")
            banks.release(m % 2, f"woE_{n}_{m}")
        for j in range(4):
            tt = 4 * n + j
            for kk in range(8):
                wt = []
                if kk == 0:
                    wt = banks.acquire(2) + banks.acquire(3) + \
                        [f"woE_{n}_{mm}" for mm in range(8)]
                fn = (lambda nn, jj, k2: lambda e: e.transpose(
                    ps[:, 1024 + 128 * k2:1024 + 128 * (k2 + 1)],
                    tl_sT(nn)[:, 512 * k2 + 128 * jj:
                              512 * k2 + 128 * (jj + 1)],
                    ident_f[:, :]))(n, j, kk)
                sc.emit("tensor", fn, waits=wt,
                        inc=("P", 1) if kk == 7 else None,
                        ev=f"wT_{tt}" if kk == 7 else None)
            gb = (4, 5) if tt % 2 == 0 else (6, 7)
            gbase = 2048 if tt % 2 == 0 else 3072
            for kk in range(8):
                wt = []
                if kk == 0:
                    wt = banks.acquire(gb[0]) + banks.acquire(gb[1]) + \
                        [f"tlG_{n}_{k2}" for k2 in range(8)]
                fn = (lambda nn, jj, k2, gp: lambda e: e.transpose(
                    ps[:, gp + 128 * k2:gp + 128 * (k2 + 1)],
                    tl_gr(nn)[:, 512 * k2 + 128 * jj:
                              512 * k2 + 128 * (jj + 1)],
                    ident_f[:, :]))(n, j, kk, gbase)
                sc.emit("tensor", fn, waits=wt,
                        inc=("P", 1) if kk == 7 else None,
                        ev=f"gT_{tt}" if kk == 7 else None)
            if tt >= 2:
                w9 = rings.acquire(("tlx", tt % 2))
                sc.dma("gpsimd", tlx_buf(tt),
                       din["xq"][128 * tt:128 * (tt + 1), :], waits=w9,
                       ev=f"tlx_{tt}", sem=f"xlq{tt % 2}")
            fn = (lambda t2, xb: lambda e: e.tensor_scalar(
                tl_s1[:, :], xb, st_nm1[:, t2:t2 + 1],
                st_r1[:, t2:t2 + 1], ALU.add, ALU.mult))(tt, tlx_buf(tt))
            sc.emit("vector", fn, waits=[f"tlx_{tt}"], inc=("V", 1))
            wsa = [f"wT_{tt}"] + rings.acquire(("tls2", tt % 2))
            if tt == 1:
                wsa = wsa + ["kprj_7_15", "vprj_7_15_1"]
            fn = (lambda sb: lambda e: e.tensor_tensor(
                sb, ps[:, 1024:2048], tl_s1[:, :], ALU.add))(tls2_buf(tt))
            sc.emit("vector", fn, waits=wsa, inc=("V", 1),
                    ev=f"sA_{tt}")
            banks.release(2, f"sA_{tt}")
            banks.release(3, f"sA_{tt}")
            fn = (lambda sb, gp: lambda e: e.tensor_tensor(
                tl_s1[:, :], gp, sb, ALU.mult))(tls2_buf(tt), gT_ps(tt))
            sc.emit("vector", fn, waits=[f"gT_{tt}"], inc=("V", 1),
                    ev=f"sB_{tt}")
            banks.release(gb[0], f"sB_{tt}")
            banks.release(gb[1], f"sB_{tt}")
            fn = (lambda sb, xb: lambda e: e.tensor_tensor(
                sb, tl_s1[:, :], xb, ALU.add))(tls2_buf(tt), tlx_buf(tt))
            sc.emit("vector", fn, inc=("V", 1), ev=f"cmb_{tt}")
            rings.release(("tlx", tt % 2), f"cmb_{tt}")
            sc.dma("gpsimd" if tt % 2 == 0 else "sync",
                   out_d[128 * tt:128 * (tt + 1), :], tls2_buf(tt),
                   waits=[f"cmb_{tt}"], ev=f"out_{tt}",
                   sem="outd" if tt % 2 == 0 else "xlq2")
            rings.release(("tls2", tt % 2), f"out_{tt}")
        rings.release(("sT", n % 2), f"wT_{4 * n + 3}")
        rings.release(("tlg", n % 2), f"gT_{4 * n + 3}")
        rings.release(("tla", n % 2), f"wo_{n}_7")

    sc.emit("gpsimd", lambda e: None, waits=["out_14", "out_15"])

    with nc.Block() as block:
        @block.sync
        def _(eng):
            sc.replay("sync", eng, semobjs)

        @block.gpsimd
        def _(eng):
            sc.replay("gpsimd", eng, semobjs)

        @block.vector
        def _(eng):
            sc.replay("vector", eng, semobjs)

        @block.scalar
        def _(eng):
            sc.replay("scalar", eng, semobjs)

        @block.tensor
        def _(eng):
            sc.replay("tensor", eng, semobjs)

    ctx.close()
    return nc


_NC_CACHE = []


def kernel(**inputs):
    in_maps = _host_prepare(inputs)
    if not _NC_CACHE:
        _NC_CACHE.append(build_nc())
    nc = _NC_CACHE[0]
    res = run_bass_kernel_spmd(nc, in_maps, core_ids=list(range(NCORE)))
    out = np.empty((B, S, D), dtype=np.float32)
    for c in range(NCORE):
        bb = c // 4
        q0 = (c % 4) * NQ
        out[bb, q0:q0 + NQ] = res.results[c]["out"]
    return out


# version 15
# speedup vs baseline: 1.0400x; 1.0273x over previous
import numpy as np
import ml_dtypes
import concourse.bass as bass
import concourse.mybir as mybir
from concourse.bass_utils import run_bass_kernel_spmd

F32 = mybir.dt.float32
BF16 = mybir.dt.bfloat16
AF = mybir.ActivationFunctionType
ALU = mybir.AluOpType
AX = mybir.AxisListType

B, S, D = 2, 8192, 1024
H, DH = 16, 64
DKV = 512
NQ = 2048
NCORE = 8
NG = 8
NCH = 4
NK = 64
QC = 512
EPS = 1e-5
DEBUG_TAPS = False
VST = 193
VPITCH = NK * VST


def _rope_tables():
    ROPE_BASE = 10000.0
    inv_freq = 1.0 / ROPE_BASE ** (np.arange(0, DH, 2, dtype=np.float32) / DH)
    t = np.arange(S, dtype=np.float32)
    freqs = np.outer(t, inv_freq)
    cos = np.concatenate([np.cos(freqs), np.cos(freqs)], -1)[:, ::2]
    sin = np.concatenate([np.sin(freqs), np.sin(freqs)], -1)[:, ::2]
    return cos, sin


def _host_prepare(inp):
    bf = ml_dtypes.bfloat16
    perm = np.zeros(D, dtype=np.int64)
    for h0 in range(H):
        perm[64 * h0:64 * h0 + 32] = np.arange(0, 64, 2) + 64 * h0
        perm[64 * h0 + 32:64 * h0 + 64] = np.arange(1, 64, 2) + 64 * h0

    x = np.ascontiguousarray(np.asarray(inp["L0_attn_x"], dtype=np.float32))

    def fm(v):
        return np.ascontiguousarray(np.asarray(v, np.float32).reshape(8, 128).T)

    cos, sin = _rope_tables()
    trig_c = np.ascontiguousarray(np.tile(cos.T, (4, 1)).astype(bf))
    trig_s = np.ascontiguousarray(np.tile(sin.T, (4, 1)).astype(bf))

    ident = np.eye(128, dtype=np.float32)
    bmask = np.zeros((128, 128), dtype=np.float32)
    bmask[0, :] = 1.0
    bmask[64, 0:64] = 1.0
    bmask[32, 64:128] = 1.0
    mrot = np.zeros((128, 128), dtype=np.float32)
    for blk in range(2):
        o = 64 * blk
        for i in range(32):
            mrot[o + 32 + i, o + i] = -1.0
            mrot[o + i, o + 32 + i] = 1.0
    emat = np.zeros((8, 64), dtype=np.float32)
    mmat = np.zeros((8, 64), dtype=np.float32)
    for e in range(8):
        for r in range(8):
            emat[e, 8 * e + r] = 2.0
            mmat[r, 8 * e + r] = 1.0

    shared = {
        "w_dq_t": np.ascontiguousarray(np.asarray(inp["L0_attn_w_down_q"]).T),
        "w_dkv_t": np.ascontiguousarray(np.asarray(inp["L0_attn_w_down_kv"]).T),
        "w_uq_t": np.ascontiguousarray(np.asarray(inp["L0_attn_w_up_q"])[perm].T),
        "w_uk_t": np.ascontiguousarray(np.asarray(inp["L0_attn_w_up_k"])[perm].T),
        "w_uv_t": np.ascontiguousarray(np.asarray(inp["L0_attn_w_up_v"]).T),
        "w_out_t": np.ascontiguousarray(np.asarray(inp["L0_attn_w_out"]).T),
        "base_w_t": np.ascontiguousarray(np.asarray(inp["L0_ffn_base_w"]).T),
        "lora_A": np.ascontiguousarray(np.asarray(inp["L0_ffn_lora_A"])),
        "lora_Bp": np.ascontiguousarray(
            np.asarray(inp["L0_ffn_lora_B"]).reshape(64, D)),
        "rout_t": np.ascontiguousarray(np.asarray(inp["L0_ffn_router_w"]).T),
        "ln_g_fm": fm(inp["L0_attn_ln_scale"]),
        "ln_b_fm": fm(inp["L0_attn_ln_bias"]),
        "base_b_fm": fm(inp["L0_ffn_base_b"]),
        "rout_b": np.ascontiguousarray(
            np.asarray(inp["L0_ffn_router_b"], np.float32).reshape(1, 8)),
        "trig_c": trig_c, "trig_s": trig_s,
        "ident_f": ident, "ident_bf": ident.astype(bf),
        "bmask": bmask, "mrot": mrot.astype(bf),
        "emat": emat.astype(bf), "mmat": mmat.astype(bf),
    }
    in_maps = []
    for c in range(NCORE):
        bb = c // 4
        q0 = (c % 4) * NQ
        m = dict(shared)
        m["xb"] = np.ascontiguousarray(np.roll(x[bb], -q0, axis=0))
        m["xq"] = np.ascontiguousarray(x[bb, q0:q0 + NQ])
        m["trig_c"] = np.ascontiguousarray(np.roll(trig_c, -q0, axis=1))
        m["trig_s"] = np.ascontiguousarray(np.roll(trig_s, -q0, axis=1))
        m["trig_cq"] = np.ascontiguousarray(trig_c[:, q0:q0 + NQ])
        m["trig_sq"] = np.ascontiguousarray(trig_s[:, q0:q0 + NQ])
        in_maps.append(m)
    return in_maps


class Sched:

    ENGINES = ("sync", "gpsimd", "vector", "scalar", "tensor")

    def __init__(self):
        self.ops = {e: [] for e in self.ENGINES}
        self.cnt = {}
        self.events = {}

    ENGSEM = {"vector": "V", "scalar": "A"}

    def emit(self, engine, fn, waits=(), inc=None, ev=None):
        rw = list(waits)
        es = self.ENGSEM.get(engine)
        if es is not None:
            if inc is None:
                inc = (es, 1)
            if inc[0] == es:
                rw.append((es, self.cnt.get(es, 0)))
        if inc is not None:
            sem, k = inc
            self.cnt[sem] = self.cnt.get(sem, 0) + k
            if ev is not None:
                self.events[ev] = (sem, self.cnt[sem])
        self.ops[engine].append((fn, tuple(rw), inc))

    def dma(self, queue, out_ap, in_ap, waits=(), ev=None, sem=None):
        assert sem is not None
        self.emit(queue, lambda e, o=out_ap, i=in_ap: e.dma_start(out=o, in_=i),
                  waits=waits, inc=(sem, 16), ev=ev)

    def replay(self, engine_name, eng, semobjs):
        for fn, waits, inc in self.ops[engine_name]:
            best = {}
            for w in waits:
                if isinstance(w, tuple):
                    sem, val = w[0][1:] if w[0].startswith("#") else w[0], w[1]
                else:
                    sem, val = self.events[w]
                if val > best.get(sem, -1):
                    best[sem] = val
            for sem, val in best.items():
                eng.wait_ge(semobjs[sem], val)
            instr = fn(eng)
            if inc is not None:
                sem, k = inc
                instr.then_inc(semobjs[sem], k)


class Res:
    def __init__(self):
        self.last = {}

    def acquire(self, key):
        ev = self.last.get(key)
        return [ev] if ev is not None else []

    def release(self, key, ev):
        self.last[key] = ev


def build_nc():
    nc = bass.Bass("TRN2", target_bir_lowering=False)
    dt = nc.dram_tensor

    din = {}
    for name, shape, dty in [
            ("xb", [S, D], F32), ("xq", [NQ, D], F32),
            ("w_dq_t", [D, DKV], F32), ("w_dkv_t", [D, DKV], F32),
            ("w_uq_t", [DKV, D], F32), ("w_uk_t", [DKV, D], F32),
            ("w_uv_t", [DKV, D], F32), ("w_out_t", [D, D], F32),
            ("base_w_t", [D, D], F32), ("lora_A", [D, 8], F32),
            ("lora_Bp", [64, D], F32), ("rout_t", [D, 8], F32),
            ("ln_g_fm", [128, 8], F32), ("ln_b_fm", [128, 8], F32),
            ("base_b_fm", [128, 8], F32), ("rout_b", [1, 8], F32),
            ("trig_c", [128, S], BF16), ("trig_s", [128, S], BF16),
            ("trig_cq", [128, NQ], BF16), ("trig_sq", [128, NQ], BF16),
            ("ident_f", [128, 128], F32), ("ident_bf", [128, 128], BF16),
            ("bmask", [128, 128], F32), ("mrot", [128, 128], BF16),
            ("emat", [8, 64], BF16), ("mmat", [8, 64], BF16)]:
        din[name] = dt(name, shape, dty, kind="ExternalInput").ap()

    out_d = dt("out", [NQ, D], F32, kind="ExternalOutput").ap()
    _dbg = "ExternalOutput" if DEBUG_TAPS else "Internal"
    ckv_h = dt("ckv_h", [DKV, S], BF16, kind=_dbg).ap()
    xnt_h = dt("xnt_h", [D, NQ], BF16, kind=_dbg).ap()
    attn_h = dt("attn_h", [D, NQ], BF16, kind=_dbg).ap()
    g_h = dt("g_h", [D, NQ], F32, kind=_dbg).ap()
    kt_h = dt("kt_h", [2 * 128, S], BF16).ap()
    vs_h = dt("vs_h", [2 * 128, NK * 128], BF16).ap()
    if DEBUG_TAPS:
        dbg_kt = dt("dbg_kt", [128, S], BF16, kind="ExternalOutput").ap()
        dbg_qt = dt("dbg_qt", [128, NQ], BF16, kind="ExternalOutput").ap()
        dbg_vs = dt("dbg_vs", [128, VPITCH], BF16, kind="ExternalOutput").ap()
        dbg_cq = dt("dbg_cq", [128, 8192], BF16, kind="ExternalOutput").ap()

    from contextlib import ExitStack
    ctx = ExitStack()
    ent = ctx.enter_context

    sbt = {}

    def T(name, shape, dty):
        sbt[name] = ent(nc.sbuf_tensor(name, shape, dty))
        return sbt[name]

    st_nm1 = T("st_nm1", [128, 80], F32)
    st_r1 = T("st_r1", [128, 80], F32)
    st_s = T("st_s", [128, 80], F32)
    st_q = T("st_q", [128, 80], F32)
    st_nm2 = T("st_nm2", [128, 80], F32)
    st_r2 = T("st_r2", [128, 80], F32)
    st_v = T("st_v", [128, 80], F32)
    ident_f = T("identf", [128, 128], F32)
    ident_bf = T("identb", [128, 128], BF16)
    bmask = T("bmaskt", [128, 128], F32)
    mrot = T("mrott", [128, 128], BF16)
    emat = T("ematt", [8, 64], BF16)
    mmat = T("mmatt", [8, 64], BF16)
    vecs = T("vecs", [128, 64], F32)
    probs = T("probs", [128, 3072], BF16)
    rec_sb = T("rec_sb", [128, 1024], F32)
    bc_sb = T("bc_sb", [128, 512], F32)
    ao_sb = T("ao_sb", [128, 1024], BF16)
    av_sb = T("av_sb", [128, 1024], F32)
    ckv_in = T("ckv_in", [128, 6144], BF16)
    ab_buf = T("ab_buf", [128, 2048], BF16)
    trig_r = T("trig_r", [128, 4096], BF16)
    wu_d = T("wu_d", [128, 2 * 1536], BF16)
    kt_res = T("kt_res", [128, NK * 128], BF16)
    vs_r = T("vs_r", [128, 16 * VST], BF16)
    kts = T("kts", [128, 2 * 512], BF16)
    vss = T("vss", [128, 2 * 512], BF16)
    xnts_t = T("xnts_t", [128, 2048], BF16)
    qt_d = T("qt_d", [128, 2 * NQ], BF16)
    sqd = T("sqd", [128, 1024], BF16)
    stage = T("stage", [128, 1024], F32)
    stage2 = T("stage2", [128, 1024], F32)
    cst = T("cst", [128, 4], F32)
    beta_bf = T("beta_bf", [128, 8], BF16)

    AR = 26624
    arena = T("arena", [128, AR], F32)
    ar = arena[:, :]
    arb = ar.bitcast(BF16)
    wbeta = arb[:, 26624:34816]

    def x_ring(i):
        return ar[:, 1024 * (i % 8):1024 * (i % 8) + 1024]

    wstage = stage[:, :]

    def wdqp(k):
        return arb[:, 16384 + 512 * k:16384 + 512 * (k + 1)]

    def wdkvp(k):
        return arb[:, 20480 + 512 * k:20480 + 512 * (k + 1)]

    def xnh_ring(i):
        return arb[:, 24576 + 1024 * (i % 2):24576 + 1024 * (i % 2) + 1024]

    def xnb_ring(i):
        return arb[:, 47104 + 1024 * (i % 4):47104 + 1024 * (i % 4) + 1024]

    def xhT_buf(cc):
        o = 26624 + 4096 * (cc % 2)
        return arb[:, o:o + 4096]

    def ckv_sb(cc):
        o = 34816 + 2048 * (cc % 2)
        return arb[:, o:o + 2048]

    cq_keep = arb[:, 38912:47104]

    def xnts(i):
        return xnts_t[:, 1024 * (i % 2):1024 * (i % 2) + 1024]

    ffn_basew = arb[:, 0:8192]

    def ffn_xnt(k):
        return arb[:, 8192 + 512 * k:8192 + 512 * (k + 1)]

    ffn_loraA = arb[:, 12288:12352]
    ffn_loraB = arb[0:64, 12352:13376]
    ffn_routw = arb[:, 13376:13440]
    ffn_logits = ar[:, 6720:6848]
    ffn_gat = arb[:, 13696:13824]
    ffn_gt = arb[0:8, 13824:15872]
    ffn_midt = arb[0:8, 15872:17920]
    ffn_grep = arb[0:64, 17920:18432]
    ffn_gm = arb[0:64, 18432:20480]
    ffn_lsb = ar[:, 10240:10752]

    def ffn_u(i):
        return ar[:, 10752 + 512 * (i % 2):10752 + 512 * (i % 2) + 512]

    ffn_sig = ar[:, 11776:12288]

    def ffn_gout(i):
        return ar[:, 12288 + 512 * (i % 2):12288 + 512 * (i % 2) + 512]

    wstage_f = stage[:, :]

    def tl_attn_r(i):
        return arb[:, 4096 * (i % 2):4096 * (i % 2) + 4096]

    tl_wout = arb[:, 8192:16384]
    def tl_gr(i):
        return ar[:, 8192 + 4096 * (i % 2):8192 + 4096 * (i % 2) + 4096]
    tl_wstage = stage[:, :]

    def tl_sT(i):
        return ar[:, 16384 + 4096 * (i % 2):16384 + 4096 * (i % 2) + 4096]

    tl_x = ar[:, 24576:25600]
    tl_s1 = ar[:, 25600:26624]
    tl_s2 = stage2[:, :]
    ckv_f = ckv_in[:, :].bitcast(F32)

    def tlx_buf(tt):
        return tl_x if tt % 2 == 0 else ckv_f[:, 1024:2048]

    def tls2_buf(tt):
        return tl_s2 if tt % 2 == 0 else ckv_f[:, 2048:3072]

    def gT_ps(tt):
        return ps[:, 2048:3072] if tt % 2 == 0 else ps[:, 3072:4096]

    ps_t = ent(nc.psum_tensor("ps", [128, 4096], F32))
    ps = ps_t[:, :]
    psb = ps.bitcast(BF16)

    def bank(b3):
        return ps[:, 512 * b3:512 * (b3 + 1)]

    def bank_bf(b3):
        return psb[:, 1024 * b3:1024 * (b3 + 1)]

    semnames = (["V", "A", "P", "QK", "EX", "AVT", "ph0", "stg", "tlx",
                 "outd", "dbg"] +
                [f"xld{i}" for i in range(8)] +
                [f"trg{i}" for i in range(4)] +
                [f"kin{i}" for i in range(12)] +
                [f"fxd{i}" for i in range(8)] +
                [f"tla{i}" for i in range(2)] +
                [f"tlg{i}" for i in range(2)] +
                [f"sb{i}" for i in range(2)] +
                [f"ks{i}" for i in range(16)] +
                [f"vs{i}" for i in range(16)] +
                ["stg2", "kto0", "kto1", "vso0", "vso1", "xt0", "xt1"] +
                [f"xlq{i}" for i in range(8)] +
                [f"gd{i}" for i in range(2)] +
                [f"ao{i}" for i in range(2)])
    semobjs = {n: ent(nc.semaphore(n)) for n in semnames}

    sc = Sched()
    banks = Res()
    rings = Res()
    sc.emit("vector", lambda e: e.memset(cst[:, 0:1], EPS), inc=("V", 1),
            ev="cst_ready")

    def statA_ap(sl):
        return vs_r[:, VST * sl:VST * sl + 65]

    def statB_ap(sl):
        return vs_r[:, VST * sl + 65:VST * (sl + 1)]

    for dst, src in [(ident_f[:, :], din["ident_f"]),
                     (ident_bf[:, :], din["ident_bf"]),
                     (bmask[:, :], din["bmask"]), (mrot[:, :], din["mrot"]),
                     (emat[:, :], din["emat"]), (mmat[:, :], din["mmat"]),
                     (vecs[:, 0:8], din["ln_g_fm"]),
                     (vecs[:, 8:16], din["ln_b_fm"]),
                     (vecs[:, 16:24], din["base_b_fm"]),
                     (vecs[0:1, 32:40], din["rout_b"])]:
        sc.dma("sync", dst, src, ev="ph0last", sem="ph0")

    for ti in range(8):
        qd = "gpsimd" if ti % 2 == 0 else "sync"
        sc.dma(qd, x_ring(ti), din["xb"][128 * ti:128 * (ti + 1), :],
               ev=f"xld_{ti}", sem=f"xld{ti % 8}")

    conv_i = [0]

    def conv(name, dram, nkt, cols, dst_fn):
        for k in range(nkt):
            par = conv_i[0] % 2
            conv_i[0] += 1
            buf = wstage if par == 0 else stage2[:, :]
            key = ("stage",) if par == 0 else ("stg2c",)
            w = rings.acquire(key) + (["ph0last"] if conv_i[0] <= 2 else [])
            sc.dma("sync", buf[:, 0:cols], dram[128 * k:128 * (k + 1), :],
                   waits=w, ev=f"wl_{name}_{k}", sem="stg" if par == 0 else "xlq3")
            fn = (lambda dk, c, b: lambda e: e.tensor_copy(dk, b[:, 0:c]))(
                dst_fn(k), cols, buf)
            sc.emit("vector", fn, waits=[f"wl_{name}_{k}"], inc=("V", 1),
                    ev=f"wc_{name}_{k}")
            rings.release(key, f"wc_{name}_{k}")

    conv("dq", din["w_dq_t"], 8, 512, lambda k: wbeta[:, 512 * k:512 * (k + 1)])
    conv("dkv", din["w_dkv_t"], 8, 512,
         lambda k: wbeta[:, 4096 + 512 * k:4096 + 512 * (k + 1)])
    sc.emit("vector", lambda e: e.tensor_copy(beta_bf[:, :], vecs[:, 8:16]),
            waits=["ph0last"], inc=("V", 1), ev="beta_bf")

    for which, base, vcol in (("q", 0, 24), ("kv", 4096, 28)):
        for m in range(4):
            for k in range(8):
                wt = []
                if k == 0:
                    wt = banks.acquire(4)
                    if m == 0:
                        wt = wt + [f"wc_d{which}_7", "beta_bf"]
                fn = (lambda ba, mm, kk: lambda e: e.matmul(
                    bank(4)[:, mm:mm + 1],
                    wbeta[:, ba + 512 * kk + 128 * mm:
                          ba + 512 * kk + 128 * (mm + 1)],
                    beta_bf[:, kk:kk + 1], start=(kk == 0),
                    stop=(kk == 7)))(base, m, k)
                ev = f"biasmm_{which}" if (m == 3 and k == 7) else None
                sc.emit("tensor", fn, waits=wt, inc=("P", 1) if ev else None,
                        ev=ev)
        fn = (lambda vc: lambda e: e.tensor_copy(
            vecs[:, vc:vc + 4], bank(4)[:, 0:4]))(vcol)
        sc.emit("vector", fn, waits=[f"biasmm_{which}"], inc=("V", 1),
                ev=f"biasev_{which}")
        banks.release(4, f"biasev_{which}")

    for which, base, dstf in (("q", 0, wdqp), ("kv", 4096, wdkvp)):
        for k in range(8):
            fn = (lambda ba, kk, df: lambda e: e.tensor_scalar(
                df(kk), wbeta[:, ba + 512 * kk:ba + 512 * (kk + 1)],
                vecs[:, kk:kk + 1], None, ALU.mult))(base, k, dstf)
            sc.emit("vector", fn,
                    waits=[f"biasev_{which}"] if k == 0 else (),
                    inc=("V", 1), ev=f"wsc_{which}_{k}")

    sc.emit("tensor", lambda e: e.matmul(
        bank(4)[:, 0:8], bmask[0:1, 0:128], vecs[0:1, 32:40],
        start=True, stop=True),
        waits=["ph0last"] + banks.acquire(4), inc=("P", 1), ev="rbmm")
    sc.emit("vector", lambda e: e.tensor_copy(vecs[:, 40:48], bank(4)[:, 0:8]),
            waits=["rbmm"], inc=("V", 1), ev="rbev")
    banks.release(4, "rbev")

    sc.emit("vector", lambda e: e.memset(vs_r[:, :], 0.0), inc=("V", 1))
    on64 = bass.AP(sbt["vs_r"], 64, [[16 * VST, 128], [VST, 16]])
    on97 = bass.AP(sbt["vs_r"], 97, [[16 * VST, 128], [VST, 16]])
    sc.emit("vector", (lambda a: lambda e: e.memset(a, 1.0))(on64),
            inc=("V", 1))
    sc.emit("vector", (lambda a: lambda e: e.memset(a, 1.0))(on97),
            inc=("V", 1), ev="vsr_init")

    turn = [0]
    slot_free = {}

    def take_turn(free_ev):
        s = turn[0] % 3
        turn[0] += 1
        w = [slot_free[s]] if s in slot_free else []
        slot_free[s] = free_ev
        return s, w

    slot_free[2] = "rbev"

    last_ex_ev = [None]

    def ph1_gen():
        def phA(cc):
            if cc < 2:
                return
            t0 = 4 * cc
            for j in range(4):
                ti = t0 + j
                qd = "gpsimd" if ti % 2 == 0 else "sync"
                sc.dma(qd, x_ring(ti), din["xb"][128 * ti:128 * (ti + 1), :],
                       waits=rings.acquire(("x", ti % 8)) +
                       rings.acquire(("x2", ti % 8)), ev=f"xld_{ti}",
                       sem=f"xld{ti % 8}")

        def phB(cc):
            is_qc = cc < 4
            t0 = 4 * cc
            for j in range(4):
                ti = t0 + j
                sc.emit("vector", (lambda t: lambda e: e.reduce_sum(
                    st_s[:, t:t + 1], x_ring(t), axis=AX.X))(ti),
                    waits=[f"xld_{ti}"], inc=("V", 1), ev=f"xsum_{ti}")
                sc.emit("scalar", (lambda t: lambda e: e.activation(
                    sqd[:, 0:1024], x_ring(t), AF.Square,
                    accum_out=st_q[:, t:t + 1]))(ti),
                    waits=[f"xld_{ti}"], inc=("A", 1), ev=f"sq1_{ti}")
            c4 = slice(4 * cc, 4 * cc + 4)
            sc.emit("vector", (lambda c: lambda e: e.tensor_scalar(
                st_nm1[:, c], st_s[:, c], -1.0 / 1024, None,
                ALU.mult))(c4), inc=("V", 1))
            sc.emit("vector", (lambda c: lambda e: e.tensor_tensor(
                st_v[:, c], st_nm1[:, c], st_nm1[:, c], ALU.mult))(c4),
                inc=("V", 1))
            sc.emit("vector", (lambda c: lambda e: e.scalar_tensor_tensor(
                st_v[:, c], st_q[:, c], 1.0 / 1024, st_v[:, c], ALU.mult,
                ALU.subtract))(c4), waits=[f"sq1_{t0 + 3}"], inc=("V", 1),
                ev=f"var1_{cc}")
            sc.emit("scalar", (lambda c: lambda e: e.activation(
                st_r1[:, c], st_v[:, c], AF.Ln, bias=cst[:, 0:1]))(c4),
                waits=[f"var1_{cc}"], inc=("A", 1))
            sc.emit("scalar", (lambda c: lambda e: e.activation(
                st_r1[:, c], st_r1[:, c], AF.Exp, scale=-0.5))(c4),
                inc=("A", 1), ev=f"rstd1_{cc}")
            sc.emit("vector", (lambda c: lambda e: e.tensor_tensor(
                st_nm2[:, c], st_r1[:, c], st_r1[:, c], ALU.mult))(c4),
                waits=[f"rstd1_{cc}"], inc=("V", 1))
            sc.emit("vector", (lambda c: lambda e: e.tensor_tensor(
                st_nm2[:, c], st_nm2[:, c], st_v[:, c], ALU.mult))(c4),
                inc=("V", 1), ev=f"var2_{cc}")
            sc.emit("scalar", (lambda c: lambda e: e.activation(
                st_r2[:, c], st_nm2[:, c], AF.Ln, bias=cst[:, 0:1]))(c4),
                waits=[f"var2_{cc}"], inc=("A", 1))
            sc.emit("scalar", (lambda c: lambda e: e.activation(
                st_r2[:, c], st_r2[:, c], AF.Exp, scale=-0.5))(c4),
                inc=("A", 1), ev=f"rstd2_{cc}")
            sc.emit("vector", (lambda c: lambda e: e.tensor_tensor(
                st_r2[:, c], st_r2[:, c], st_r1[:, c], ALU.mult))(c4),
                waits=[f"rstd2_{cc}"], inc=("V", 1))
            sc.emit("vector", (lambda c: lambda e: e.tensor_tensor(
                st_nm2[:, c], st_nm1[:, c], st_r2[:, c], ALU.mult))(c4),
                inc=("V", 1), ev=f"aab_{cc}")
            if is_qc:
                sc.emit("vector", (lambda c: lambda e: e.tensor_tensor(
                    st_v[:, c], st_nm1[:, c], st_r1[:, c], ALU.mult))(c4),
                    inc=("V", 1), ev=f"rb1_{cc}")

        def phD(cc, dpair):
            is_qc = cc < 4
            t0 = 4 * cc
            s, sw = take_turn(f"trE_{t0 + 2 * dpair + 1}")
            for j2 in range(2):
                ti = t0 + 2 * dpair + j2
                w = [f"aab_{cc}"] + rings.acquire(("xnh", ti % 2))
                sc.emit("scalar", (lambda t: lambda e: e.activation(
                    xnh_ring(t), x_ring(t), AF.Identity,
                    scale=st_r2[:, t:t + 1],
                    bias=st_nm2[:, t:t + 1]))(ti),
                    waits=w, inc=("A", 1), ev=f"xnh_{ti}")
                if is_qc:
                    w3 = [f"rb1_{cc}"] + rings.acquire(("xnb", ti % 4))
                    sc.emit("scalar", (lambda t: lambda e: e.activation(
                        xnb_ring(t), x_ring(t), AF.Identity,
                        scale=st_r1[:, t:t + 1],
                        bias=st_v[:, t:t + 1]))(ti), waits=w3,
                        inc=("A", 1), ev=f"xnb_{ti}")
                rings.release(("x", ti % 8),
                              f"xnb_{ti}" if is_qc else f"xnh_{ti}")
                rings.last[("x2", ti % 8)] = f"xsum_{ti}"
                bb = 2 * s + j2
                for kk in range(8):
                    wt = []
                    if kk == 0:
                        wt = [f"xnh_{ti}"] + (list(sw) if j2 == 0 else [])
                    fn = (lambda t, k2, b2: lambda e: e.transpose(
                        bank_bf(b2)[:, 128 * k2:128 * (k2 + 1)],
                        xnh_ring(t)[:, 128 * k2:128 * (k2 + 1)],
                        ident_bf[:, :]))(ti, kk, bb)
                    sc.emit("tensor", fn, waits=wt,
                            inc=("P", 1) if kk == 7 else None,
                            ev=f"trT_{ti}" if kk == 7 else None)
            for j2 in range(2):
                ti = t0 + 2 * dpair + j2
                j = 2 * dpair + j2
                bb = 2 * s + j2
                dv = xhT_buf(cc).rearrange("p (k t) -> p k t", t=512)[
                    :, :, 128 * j:128 * (j + 1)]
                fn = (lambda d, b2: lambda e: e.tensor_copy(
                    d, bank_bf(b2).rearrange("p (k t) -> p k t",
                                             t=128)))(dv, bb)
                w2 = [f"trT_{ti}"] + \
                    (rings.acquire(("xhT", cc % 2)) if j == 0 else [])
                sc.emit("vector", fn, waits=w2, inc=("V", 1),
                        ev=f"trE_{ti}")
                rings.release(("xnh", ti % 2), f"trE_{ti}")

        def phE(cc, epair, pname):
            t0 = 4 * cc
            wfn = wdqp if pname == "cq" else wdkvp
            s, sw = take_turn(f"{pname}E_{cc}_{2 * epair + 1}")
            for m2 in range(2):
                m = 2 * epair + m2
                for k in range(8):
                    wt = []
                    if k == 0:
                        wt = (list(sw) if m2 == 0 else []) + \
                            [f"trE_{t0 + 3}"]
                        if cc == 0 and m == 0:
                            wt = wt + (["wsc_q_7"] if pname == "cq"
                                       else ["wsc_kv_7"])
                    fn = (lambda mm, kk, wf, c, b2: lambda e: e.matmul(
                        bank(b2), wf(kk)[:, 128 * mm:128 * (mm + 1)],
                        xhT_buf(c)[:, 512 * kk:512 * (kk + 1)],
                        start=(kk == 0), stop=(kk == 7)))(
                            m, k, wfn, cc, 2 * s + m2)
                    ev = f"{pname}M_{cc}_{m}" if k == 7 else None
                    sc.emit("tensor", fn, waits=wt,
                            inc=("P", 1) if ev else None, ev=ev)
            for m2 in range(2):
                m = 2 * epair + m2
                if pname == "ckv":
                    w4 = [f"ckvM_{cc}_{m}"]
                    if m == 0:
                        w4 += rings.acquire(("ckvsb", cc % 2))
                        if cc >= 2:
                            w4.append(f"vprj_0_{cc - 2}_1")
                    fn = (lambda c, mm, b2: lambda e: e.tensor_scalar(
                        ckv_sb(c)[:, 512 * mm:512 * (mm + 1)], bank(b2),
                        vecs[:, 28 + mm:29 + mm], None, ALU.add))(
                            cc, m, 2 * s + m2)
                    sc.emit("vector", fn, waits=w4, inc=("V", 1),
                            ev=f"ckvE_{cc}_{m}")
                    sc.dma("gpsimd" if cc % 2 == 0 else "sync",
                           ckv_h[128 * m:128 * (m + 1),
                                 512 * cc:512 * (cc + 1)],
                           ckv_sb(cc)[:, 512 * m:512 * (m + 1)],
                           waits=[f"ckvE_{cc}_{m}"], ev=f"ckvD_{cc}_{m}",
                           sem=f"sb{cc % 2}")
                    if m == 3:
                        rings.release(("ckvsb", cc % 2), f"ckvD_{cc}_3")
                else:
                    fn = (lambda c3, mm, b2: lambda e: e.tensor_scalar(
                        cq_keep[:, 2048 * mm + 512 * c3:
                                2048 * mm + 512 * (c3 + 1)],
                        bank(b2), vecs[:, 24 + mm:25 + mm], None,
                        ALU.add))(cc, m, 2 * s + m2)
                    sc.emit("vector", fn, waits=[f"cqM_{cc}_{m}"],
                            inc=("V", 1), ev=f"cqE_{cc}_{m}")

        def phF(cc, fpair):
            t0 = 4 * cc
            s, sw = take_turn(f"xnTe_{t0 + 2 * fpair + 1}")
            for j2 in range(2):
                ti = t0 + 2 * fpair + j2
                bb = 2 * s + j2
                for kk in range(8):
                    wt = []
                    if kk == 0:
                        wt = [f"xnb_{ti}"] + \
                            (list(sw) if j2 == 0 else [])
                    fn = (lambda t, k2, b2: lambda e: e.transpose(
                        bank_bf(b2)[:, 128 * k2:128 * (k2 + 1)],
                        xnb_ring(t)[:, 128 * k2:128 * (k2 + 1)],
                        ident_bf[:, :]))(ti, kk, bb)
                    sc.emit("tensor", fn, waits=wt,
                            inc=("P", 1) if kk == 7 else None,
                            ev=f"xnT_{ti}" if kk == 7 else None)
            for j2 in range(2):
                ti = t0 + 2 * fpair + j2
                bb = 2 * s + j2
                w5 = [f"xnT_{ti}"] + rings.acquire(("xnts", ti % 2))
                fn = (lambda t, b2: lambda e: e.tensor_copy(
                    xnts(t), bank_bf(b2)))(ti, bb)
                sc.emit("vector", fn, waits=w5, inc=("V", 1),
                        ev=f"xnTe_{ti}")
                rings.release(("xnb", ti % 4), f"xnTe_{ti}")
                dstv = xnt_h.rearrange("(k p) t -> p k t", p=128)[
                    :, :, 128 * ti:128 * (ti + 1)]
                srcv = xnts(ti).rearrange("p (k t) -> p k t", t=128)
                sc.dma("gpsimd", dstv, srcv, waits=[f"xnTe_{ti}"],
                       ev=f"xnTd_{ti}", sem=f"xt{ti % 2}")
                rings.release(("xnts", ti % 2), f"xnTd_{ti}")

        NCHK = 16
        for i in range(NCHK + 3):
            cc = i - 3
            if 0 <= cc < NCHK:
                for ep in range(2):
                    phE(cc, ep, "ckv")
                    yield
                if cc < 4:
                    for ep in range(2):
                        phE(cc, ep, "cq")
                        yield
                rings.release(("xhT", cc % 2),
                              f"cqM_{cc}_3" if cc < 4 else f"ckvM_{cc}_3")
                yield ("E", cc)
                if cc < 4:
                    for fp in range(2):
                        phF(cc, fp)
                        yield
            if 2 <= i < NCHK + 2:
                for dp in range(2):
                    phD(i - 2, dp)
                    yield
            if 1 <= i < NCHK + 1:
                phB(i - 1)
            yield
            if i < NCHK:
                phA(i)
            yield
        sc.emit("vector", lambda e: e.memset(rec_sb[:, 0:1], 0.0),
                waits=["xnh_63"], inc=("V", 1), ev="xfree")

    def ffn_w_gen():
        for k in range(8):
            w = rings.acquire(("stage",)) + (["xfree"] if k == 0 else [])
            sc.dma("gpsimd", wstage_f[:, 0:1024],
                   din["base_w_t"][128 * k:128 * (k + 1), :], waits=w,
                   ev=f"fwl_{k}", sem="stg2")
            fn = (lambda kk: lambda e: e.tensor_copy(
                ffn_basew[:, 1024 * kk:1024 * (kk + 1)],
                wstage_f[:, 0:1024]))(k)
            sc.emit("vector", fn, waits=[f"fwl_{k}"], inc=("V", 1),
                    ev=f"fwc_{k}")
            rings.release(("stage",), f"fwc_{k}")
            if k % 2 == 1:
                yield
        srcA = din["lora_A"].rearrange("(k p) r -> p k r", p=128)
        sc.dma("gpsimd", wstage_f[:, 0:64].rearrange("p (k r) -> p k r", r=8),
               srcA, waits=rings.acquire(("stage",)), ev="flA", sem="stg2")
        sc.emit("vector", lambda e: e.tensor_copy(ffn_loraA[:, :],
                                                  wstage_f[:, 0:64]),
                waits=["flA"], inc=("V", 1), ev="fcA")
        rings.release(("stage",), "fcA")
        sc.dma("gpsimd", wstage_f[0:64, 0:1024], din["lora_Bp"],
               waits=rings.acquire(("stage",)), ev="flB", sem="stg2")
        sc.emit("vector", lambda e: e.tensor_copy(ffn_loraB[:, :],
                                                  wstage_f[0:64, 0:1024]),
                waits=["flB"], inc=("V", 1), ev="fcB")
        rings.release(("stage",), "fcB")
        srcR = din["rout_t"].rearrange("(k p) r -> p k r", p=128)
        sc.dma("gpsimd", wstage_f[:, 0:64].rearrange("p (k r) -> p k r", r=8),
               srcR, waits=rings.acquire(("stage",)), ev="flR", sem="stg2")
        sc.emit("vector", lambda e: e.tensor_copy(ffn_routw[:, :],
                                                  wstage_f[:, 0:64]),
                waits=["flR"], inc=("V", 1), ev="fcR")
        rings.release(("stage",), "fcR")
        yield
        for k in range(8):
            w = rings.acquire(("fxnt", k)) + ["xfree", "xnTd_14", "xnTd_15"]
            sc.dma("gpsimd", ffn_xnt(k),
                   xnt_h[128 * k:128 * (k + 1), 0:512],
                   waits=w, ev=f"fx_0_{k}", sem=f"fxd{k}")
            if k % 4 == 3:
                yield

    def ffn_gen():
        for n in range(4):
            if n > 0:
                for k in range(8):
                    w = rings.acquire(("fxnt", k))
                    sc.dma("gpsimd", ffn_xnt(k),
                           xnt_h[128 * k:128 * (k + 1), 512 * n:512 * (n + 1)],
                           waits=w, ev=f"fx_{n}_{k}",
                           sem=f"fxd{k}")
            yield
            for j in range(4):
                m = 4 * n + j
                s, sw = take_turn(f"logE_{m}")
                for k in range(8):
                    wt = []
                    if j == 0:
                        wt += [f"fx_{n}_{k}"]
                    if k == 0:
                        wt += list(sw)
                        if m == 0:
                            wt += ["fcR", "rbev"]
                    fn = (lambda jj, kk, ss: lambda e: e.matmul(
                        bank(2 * ss)[:, 0:8],
                        ffn_xnt(kk)[:, 128 * jj:128 * (jj + 1)],
                        ffn_routw[:, 8 * kk:8 * (kk + 1)],
                        start=(kk == 0), stop=(kk == 7)))(j, k, s)
                    sc.emit("tensor", fn, waits=wt,
                            inc=("P", 1) if k == 7 else None,
                            ev=f"logM_{m}" if k == 7 else None)
                fn = (lambda mm, ss: lambda e: e.tensor_tensor(
                    ffn_logits[:, 8 * mm:8 * (mm + 1)], bank(2 * ss)[:, 0:8],
                    vecs[:, 40:48], ALU.add))(m, s)
                sc.emit("vector", fn, waits=[f"logM_{m}"], inc=("V", 1),
                        ev=f"logE_{m}")
                fn = (lambda mm: lambda e: e.reduce_max(
                    st_v[:, mm:mm + 1], ffn_logits[:, 8 * mm:8 * (mm + 1)],
                    axis=AX.X))(m)
                sc.emit("vector", fn, inc=("V", 1))
                fn = (lambda mm: lambda e: e.tensor_scalar(
                    st_s[:, mm:mm + 1], st_v[:, mm:mm + 1], -1.0, None,
                    ALU.mult))(m)
                sc.emit("vector", fn, inc=("V", 1), ev=f"gmx_{m}")
                fn = (lambda mm: lambda e: e.activation(
                    stage2[:, 16:24], ffn_logits[:, 8 * mm:8 * (mm + 1)],
                    AF.Exp, bias=st_s[:, mm:mm + 1],
                    accum_out=st_q[:, mm:mm + 1]))(m)
                sc.emit("scalar", fn, waits=[f"gmx_{m}"], inc=("A", 1),
                        ev=f"gex_{m}")
                fn = (lambda mm: lambda e: e.reciprocal(
                    st_v[:, mm:mm + 1], st_q[:, mm:mm + 1]))(m)
                sc.emit("vector", fn, waits=[f"gex_{m}"], inc=("V", 1))
                fn = (lambda mm: lambda e: e.tensor_scalar(
                    ffn_gat[:, 8 * mm:8 * (mm + 1)], stage2[:, 16:24],
                    st_v[:, mm:mm + 1], None, ALU.mult))(m)
                sc.emit("vector", fn, inc=("V", 1), ev=f"gat_{m}")
                yield
            s, sw = take_turn(f"midE_{n}")
            for k in range(8):
                wt = (list(sw) + ["fcA"]) if k == 0 else []
                fn = (lambda kk, ss: lambda e: e.matmul(
                    bank(2 * ss)[0:8, :], ffn_loraA[:, 8 * kk:8 * (kk + 1)],
                    ffn_xnt(kk)[:, :], start=(kk == 0), stop=(kk == 7)))(k, s)
                sc.emit("tensor", fn, waits=wt,
                        inc=("P", 1) if k == 7 else None,
                        ev=f"midM_{n}" if k == 7 else None)
                if k == 3:
                    yield "cont"
            fn = (lambda nn, ss: lambda e: e.tensor_copy(
                ffn_midt[:, 512 * nn:512 * (nn + 1)],
                bank(2 * ss)[0:8, :]))(n, s)
            sc.emit("vector", fn, waits=[f"midM_{n}"], inc=("V", 1),
                    ev=f"midE_{n}")
            for k in range(8):
                rings.release(("fxnt", k), f"midE_{n}")
            yield
        s, sw = take_turn("gtE")
        for m in range(16):
            wt = []
            if m == 0:
                wt = list(sw) + [f"gat_{mm}" for mm in range(16)]
            b3 = 2 * s + (m // 8)
            fn = (lambda mm, bb: lambda e: e.transpose(
                bank_bf(bb)[0:8, 128 * (mm % 8):128 * ((mm % 8) + 1)],
                ffn_gat[:, 8 * mm:8 * (mm + 1)], ident_bf[:, :]))(m, b3)
            sc.emit("tensor", fn, waits=wt, inc=("P", 1) if m == 15 else None,
                    ev="gtT" if m == 15 else None)
        fn = (lambda ss: lambda e: e.tensor_copy(
            ffn_gt[:, 0:1024], bank_bf(2 * ss)[0:8, :]))(s)
        sc.emit("vector", fn, waits=["gtT"], inc=("V", 1))
        fn = (lambda ss: lambda e: e.tensor_copy(
            ffn_gt[:, 1024:2048], bank_bf(2 * ss + 1)[0:8, :]))(s)
        sc.emit("vector", fn, inc=("V", 1), ev="gtE")
        yield
        for n in range(4):
            s, sw = take_turn(f"gm_{n}")
            fn = (lambda nn, ss: lambda e: e.matmul(
                bank(2 * ss)[0:64, :], emat[:, :],
                ffn_gt[:, 512 * nn:512 * (nn + 1)], start=True,
                stop=True))(n, s)
            sc.emit("tensor", fn, waits=list(sw) + ["gtE"], inc=("P", 1),
                    ev=f"grepM_{n}")
            fn = (lambda nn, ss: lambda e: e.matmul(
                bank(2 * ss + 1)[0:64, :], mmat[:, :],
                ffn_midt[:, 512 * nn:512 * (nn + 1)], start=True,
                stop=True))(n, s)
            sc.emit("tensor", fn, waits=[f"midE_{n}"], inc=("P", 1),
                    ev=f"mrepM_{n}")
            fn = (lambda nn, ss: lambda e: e.tensor_copy(
                ffn_grep[:, :], bank(2 * ss)[0:64, :]))(n, s)
            sc.emit("vector", fn, waits=[f"grepM_{n}"], inc=("V", 1),
                    ev=f"grepE_{n}")
            fn = (lambda nn, ss: lambda e: e.tensor_tensor(
                ffn_gm[:, 512 * nn:512 * (nn + 1)], bank(2 * ss + 1)[0:64, :],
                ffn_grep[:, :], ALU.mult))(n, s)
            sc.emit("vector", fn, waits=[f"mrepM_{n}", f"grepE_{n}"],
                    inc=("V", 1), ev=f"gm_{n}")
            yield
        for n in range(4):
            for k in range(8):
                sc.dma("gpsimd", ffn_xnt(k),
                       xnt_h[128 * k:128 * (k + 1), 512 * n:512 * (n + 1)],
                       waits=rings.acquire(("fxnt", k)), ev=f"fx2_{n}_{k}",
                       sem=f"fxd{k}")
            yield
            for m in range(8):
                ii = n * 8 + m
                s, sw = take_turn(f"u_{n}_{m}")
                wt = list(sw) + [f"gm_{n}"]
                if n == 0 and m == 0:
                    wt += ["fcB"]
                fn = (lambda nn, mm, ss: lambda e: e.matmul(
                    bank(2 * ss + 1), ffn_loraB[:, 128 * mm:128 * (mm + 1)],
                    ffn_gm[:, 512 * nn:512 * (nn + 1)], start=True,
                    stop=True))(n, m, s)
                sc.emit("tensor", fn, waits=wt, inc=("P", 1),
                        ev=f"lraM_{n}_{m}")
                fn = (lambda ss: lambda e: e.tensor_copy(
                    ffn_lsb[:, :], bank(2 * ss + 1)))(s)
                sc.emit("vector", fn, waits=[f"lraM_{n}_{m}"], inc=("V", 1),
                        ev=f"lraE_{n}_{m}")
                for k in range(8):
                    wt = []
                    if m == 0:
                        wt += [f"fx2_{n}_{k}"]
                    if k == 0:
                        wt += ["fwc_7"]
                    fn = (lambda mm, kk, ss: lambda e: e.matmul(
                        bank(2 * ss), ffn_basew[:, 1024 * kk + 128 * mm:
                                                1024 * kk + 128 * (mm + 1)],
                        ffn_xnt(kk)[:, :], start=(kk == 0),
                        stop=(kk == 7)))(m, k, s)
                    sc.emit("tensor", fn, waits=wt,
                            inc=("P", 1) if k == 7 else None,
                            ev=f"basM_{n}_{m}" if k == 7 else None)
                    if k == 3:
                        yield "cont"
                fn = (lambda mm, ii2, ss: lambda e: e.scalar_tensor_tensor(
                    ffn_u(ii2), bank(2 * ss), vecs[:, 16 + mm:17 + mm],
                    ffn_lsb[:, :], ALU.add, ALU.add))(m, ii, s)
                w6 = [f"basM_{n}_{m}", f"lraE_{n}_{m}"] + \
                    rings.acquire(("fu", ii % 2))
                sc.emit("vector", fn, waits=w6, inc=("V", 1), ev=f"u_{n}_{m}")
                w7 = [f"u_{n}_{m}"]
                if ii >= 1:
                    w7.append(f"sg1_{(ii - 1) // 8}_{(ii - 1) % 8}")
                fn = (lambda ii2: lambda e: e.activation(
                    ffn_sig[:, :], ffn_u(ii2), AF.Exp, scale=-1.0))(ii)
                sc.emit("scalar", fn, waits=w7, inc=("A", 1), ev=f"es_{n}_{m}")
                fn = (lambda: lambda e: e.tensor_scalar(
                    stage2[:, 0:512], ffn_sig[:, :], 1.0, None, ALU.add))()
                sc.emit("vector", fn, waits=[f"es_{n}_{m}"], inc=("V", 1),
                        ev=f"sg1_{n}_{m}")
                fn = (lambda: lambda e: e.reciprocal(stage2[:, 512:1024],
                                                     stage2[:, 0:512]))()
                sc.emit("vector", fn, inc=("V", 1))
                fn = (lambda ii2: lambda e: e.tensor_tensor(
                    ffn_gout(ii2), ffn_u(ii2), stage2[:, 512:1024],
                    ALU.mult))(ii)
                w8 = rings.acquire(("gout", ii % 2))
                sc.emit("vector", fn, waits=w8, inc=("V", 1), ev=f"g_{n}_{m}")
                rings.release(("fu", ii % 2), f"g_{n}_{m}")
                sc.dma("gpsimd",
                       g_h[128 * m:128 * (m + 1), 512 * n:512 * (n + 1)],
                       ffn_gout(ii), waits=[f"g_{n}_{m}"], ev=f"gD_{n}_{m}",
                       sem=f"gd{ii % 2}")
                rings.release(("gout", ii % 2), f"gD_{n}_{m}")
                yield
            for k in range(8):
                rings.release(("fxnt", k), f"basM_{n}_{7}")

    def build_gen(g):
        par = g % 2
        if g >= 1:
            for _ in range(40):
                yield
        wuw = []
        if g >= 2:
            wuw = [f"qprj_{g - 2}_3"]
        for wi, wname in enumerate(("w_uk_t", "w_uv_t", "w_uq_t")):
            rearr = din[wname].rearrange("(k p) c -> p k c", p=128)
            srcw = rearr[:, :, 128 * g:128 * (g + 1)]
            w = rings.acquire(("stage",)) + (wuw if wi == 0 else [])
            dstw = stage[:, 0:512].rearrange("p (k c) -> p k c", c=128)
            sc.dma("gpsimd", dstw, srcw, waits=w, ev=f"wul_{g}_{wi}",
                   sem="stg2")
            fn = (lambda wi2: lambda e: e.tensor_copy(
                wu_d[:, 1536 * par + 512 * wi2:1536 * par + 512 * (wi2 + 1)],
                stage[:, 0:512]))(wi)
            sc.emit("vector", fn, waits=[f"wul_{g}_{wi}"], inc=("V", 1),
                    ev=f"wuc_{g}_{wi}")
            rings.release(("stage",), f"wuc_{g}_{wi}")
        yield

        def wu_k(kk):
            return wu_d[:, 1536 * par + 128 * kk:1536 * par + 128 * (kk + 1)]

        def wu_v(kk):
            o = 1536 * par + 512
            return wu_d[:, o + 128 * kk:o + 128 * (kk + 1)]

        def wu_q(kk):
            o = 1536 * par + 1024
            return wu_d[:, o + 128 * kk:o + 128 * (kk + 1)]

        def qbuild():
            for qc2 in range(4):
                tsl2 = qc2 % 4
                w = rings.acquire(("trig", tsl2))
                sc.dma("gpsimd", trig_r[:, 1024 * tsl2:1024 * tsl2 + 512],
                           din["trig_cq"][:, 512 * qc2:512 * (qc2 + 1)], waits=w,
                           ev=f"tqc_{g}_{qc2}", sem=f"trg{tsl2}")
                sc.dma("gpsimd", trig_r[:, 1024 * tsl2 + 512:1024 * (tsl2 + 1)],
                           din["trig_sq"][:, 512 * qc2:512 * (qc2 + 1)],
                           ev=f"tqs_{g}_{qc2}", sem=f"trg{tsl2}")
                s, sw = take_turn(f"qab_{g}_{qc2}")
                for k in range(4):
                        wt = []
                        if k == 0:
                            wt = list(sw)
                            if qc2 == 0:
                                    wt = wt + rings.acquire(("qtd", par)) + \
                                        [f"wuc_{g}_2"]
                            if qc2 == 0 and g <= 1:
                                    wt = wt + ["cqE_3_3"]
                        fn = (lambda c2, kk, ss: lambda e: e.matmul(
                            bank(2 * ss), wu_q(kk),
                            cq_keep[:, 2048 * kk + 512 * c2:2048 * kk + 512 * (c2 + 1)],
                            start=(kk == 0), stop=(kk == 3)))(qc2, k, s)
                        sc.emit("tensor", fn, waits=wt,
                                    inc=("P", 1) if k == 3 else None,
                                    ev=f"qprj_{g}_{qc2}" if k == 3 else None)
                asl = qc2 % 2
                wA = rings.acquire(("ab", asl)) + [f"qprj_{g}_{qc2}",
                                                                   f"tqc_{g}_{qc2}",
                                                                   f"tqs_{g}_{qc2}"]
                fn = (lambda a2, t3, ss: lambda e: e.tensor_tensor(
                        ab_buf[:, 1024 * a2:1024 * a2 + 512], bank(2 * ss),
                        trig_r[:, 1024 * t3:1024 * t3 + 512], ALU.mult))(asl, tsl2, s)
                sc.emit("vector", fn, waits=wA, inc=("V", 1))
                fn = (lambda a2, t3, ss: lambda e: e.tensor_tensor(
                        ab_buf[:, 1024 * a2 + 512:1024 * (a2 + 1)], bank(2 * ss),
                        trig_r[:, 1024 * t3 + 512:1024 * (t3 + 1)],
                        ALU.mult))(asl, tsl2, s)
                sc.emit("vector", fn, inc=("V", 1), ev=f"qab_{g}_{qc2}")
                rings.release(("trig", tsl2), f"qab_{g}_{qc2}")
                s2, sw2 = take_turn(f"qt_{g}_{qc2}")
                fn = (lambda a2, ss: lambda e: e.matmul(
                        bank(2 * ss), mrot[:, :],
                        ab_buf[:, 1024 * a2 + 512:1024 * (a2 + 1)],
                        start=True, stop=True))(asl, s2)
                sc.emit("tensor", fn, waits=list(sw2) + [f"qab_{g}_{qc2}"],
                            inc=("P", 1), ev=f"qrot_{g}_{qc2}")
                fn = (lambda c2, a2, ss: lambda e: e.tensor_tensor(
                        qt_d[:, NQ * par + 512 * c2:NQ * par + 512 * (c2 + 1)],
                        bank(2 * ss), ab_buf[:, 1024 * a2:1024 * a2 + 512],
                        ALU.add))(qc2, asl, s2)
                sc.emit("vector", fn, waits=[f"qrot_{g}_{qc2}"], inc=("V", 1),
                            ev=f"qt_{g}_{qc2}")
                rings.release(("ab", asl), f"qt_{g}_{qc2}")
                yield


        if g != 0:
            for _ in qbuild():
                yield

        war = []
        if g >= 2:
            war = [f"qk_{g - 2}_{NCH * NK - 1}", f"avs_{g - 2}_{NCH - 1}"]

        def chunk_dmas(ch):
            for k in (() if g == 0 else range(4)):
                sl = (4 * ch + k) % 12
                w = rings.acquire(("ckvin", sl))
                if g == 1:
                    w = w + [f"ckvD_{ch}_3"]
                sc.dma("gpsimd", ckv_in[:, 512 * sl:512 * (sl + 1)],
                       ckv_h[128 * k:128 * (k + 1), 512 * ch:512 * (ch + 1)],
                       waits=w, ev=f"kin_{g}_{ch}_{k}", sem=f"kin{sl}")
            tsl2 = ch % 4
            w = rings.acquire(("trig", tsl2))
            sc.dma("gpsimd", trig_r[:, 1024 * tsl2:1024 * tsl2 + 512],
                   din["trig_c"][:, 512 * ch:512 * (ch + 1)], waits=w,
                   ev=f"tcd_{g}_{ch}", sem=f"trg{tsl2}")
            sc.dma("gpsimd", trig_r[:, 1024 * tsl2 + 512:1024 * (tsl2 + 1)],
                   din["trig_s"][:, 512 * ch:512 * (ch + 1)],
                   ev=f"tsd_{g}_{ch}", sem=f"trg{tsl2}")

        chunk_dmas(0)
        yield
        chunk_dmas(1)
        yield
        for ch in range(16):
            tsl = ch % 4
            if ch + 2 < 16:
                chunk_dmas(ch + 2)
            yield
            s, sw = take_turn(f"kt_{g}_{ch}")

            def ckv_rhs(kk, lo, sz, c2=ch):
                if g == 0:
                    return ckv_sb(c2)[:, 512 * kk + lo:512 * kk + lo + sz]
                sl2 = (4 * c2 + kk) % 12
                return ckv_in[:, 512 * sl2 + lo:512 * sl2 + lo + sz]

            for k in range(4):
                wt = []
                if k == 0:
                    wt = list(sw)
                    wt += ([f"ckvE_{ch}_3"] if g == 0 else
                           [f"kin_{g}_{ch}_{k2}" for k2 in range(4)])
                    if ch == 0:
                        wt = wt + [f"wuc_{g}_2"]
                fn = (lambda kk, ss, rhs: lambda e: e.matmul(
                    bank(2 * ss), wu_k(kk), rhs,
                    start=(kk == 0), stop=(kk == 3)))(k, s, ckv_rhs(k, 0, 512))
                sc.emit("tensor", fn, waits=wt,
                        inc=("P", 1) if k == 3 else None,
                        ev=f"kprj_{g}_{ch}" if k == 3 else None)
                if k == 1:
                    yield "cont"
            asl = ch % 2
            wA = rings.acquire(("ab", asl)) + \
                [f"kprj_{g}_{ch}", f"tcd_{g}_{ch}", f"tsd_{g}_{ch}"]
            fn = (lambda a2, t3, ss: lambda e: e.tensor_tensor(
                ab_buf[:, 1024 * a2:1024 * a2 + 512], bank(2 * ss),
                trig_r[:, 1024 * t3:1024 * t3 + 512], ALU.mult))(asl, tsl, s)
            sc.emit("vector", fn, waits=wA, inc=("V", 1))
            fn = (lambda a2, t3, ss: lambda e: e.tensor_tensor(
                ab_buf[:, 1024 * a2 + 512:1024 * (a2 + 1)], bank(2 * ss),
                trig_r[:, 1024 * t3 + 512:1024 * (t3 + 1)],
                ALU.mult))(asl, tsl, s)
            sc.emit("vector", fn, inc=("V", 1), ev=f"kab_{g}_{ch}")
            rings.release(("trig", tsl), f"kab_{g}_{ch}")
            fn = (lambda a2, ss: lambda e: e.matmul(
                bank(2 * ss + 1), mrot[:, :],
                ab_buf[:, 1024 * a2 + 512:1024 * (a2 + 1)],
                start=True, stop=True))(asl, s)
            sc.emit("tensor", fn, waits=[f"kab_{g}_{ch}"], inc=("P", 1),
                    ev=f"krot_{g}_{ch}")
            ksl = ch % 2
            wkt = rings.acquire(("kts", ksl)) + [f"krot_{g}_{ch}"]
            fn = (lambda a2, ss, kl: lambda e: e.tensor_tensor(
                kts[:, 512 * kl:512 * (kl + 1)], bank(2 * ss + 1),
                ab_buf[:, 1024 * a2:1024 * a2 + 512], ALU.add))(asl, s, ksl)
            sc.emit("vector", fn, waits=wkt, inc=("V", 1), ev=f"kt_{g}_{ch}")
            rings.release(("ab", asl), f"kt_{g}_{ch}")
            w = [f"kt_{g}_{ch}"] + (war if ch == 0 else [])
            sc.dma("gpsimd",
                   kt_h[128 * par:128 * (par + 1), 512 * ch:512 * (ch + 1)],
                   kts[:, 512 * ksl:512 * (ksl + 1)], waits=w,
                   ev=f"kto_{g}_{ch}", sem=f"kto{ksl}")
            rings.release(("kts", ksl), f"kto_{g}_{ch}")
            yield
            vsl = ch % 2
            wv0 = rings.acquire(("vss", vsl))
            for half in range(2):
                s, sw = take_turn(f"vse_{g}_{ch}_{half}")
                for mj2 in range(2):
                    mj = 2 * half + mj2
                    kt2 = 4 * ch + mj
                    for k in range(4):
                        wt = []
                        if k == 0:
                            wt = list(sw) if mj2 == 0 else []
                            if half == 0 and mj2 == 0:
                                wt = wt + wv0
                        fn = (lambda kk, bb, rhs: lambda e: e.matmul(
                            ps[:, 512 * bb:512 * bb + 128], rhs,
                            wu_v(kk), start=(kk == 0), stop=(kk == 3)))(
                                k, 2 * s + mj2, ckv_rhs(k, 128 * mj, 128))
                        sc.emit("tensor", fn, waits=wt,
                                inc=("P", 1) if (k == 3 and mj2 == 1) else None,
                                ev=f"vprj_{g}_{ch}_{half}"
                                if (k == 3 and mj2 == 1) else None)
                for mj2 in range(2):
                    mj = 2 * half + mj2
                    fn = (lambda mj3, bb, vl: lambda e: e.tensor_copy(
                        vss[:, 512 * vl + 128 * mj3:512 * vl + 128 * (mj3 + 1)],
                        ps[:, 512 * bb:512 * bb + 128]))(mj, 2 * s + mj2, vsl)
                    sc.emit("vector", fn,
                            waits=[f"vprj_{g}_{ch}_{half}"] if mj2 == 0 else [],
                            inc=("V", 1),
                            ev=f"vse_{g}_{ch}_{half}" if mj2 == 1 else None)
                yield
            VH = NK * 128
            srcA = bass.AP(sbt["vss"], 512 * vsl, [[1024, 128], [128, 4], [1, 64]])
            dstA = bass.AP(vs_h.tensor, (128 * par) * VH + 128 * (4 * ch),
                           [[VH, 128], [128, 4], [1, 64]])
            sc.dma("gpsimd", dstA, srcA, waits=[f"vse_{g}_{ch}_1"],
                   ev=f"vso_{g}_{ch}_A", sem=f"vso{vsl}")
            srcB = bass.AP(sbt["vss"], 512 * vsl + 64, [[1024, 128], [128, 4], [1, 64]])
            dstB = bass.AP(vs_h.tensor,
                           (128 * par) * VH + 128 * (4 * ch) + 64,
                           [[VH, 128], [128, 4], [1, 64]])
            sc.dma("gpsimd", dstB, srcB, waits=[f"vse_{g}_{ch}_1"],
                   ev=f"vso_{g}_{ch}_B", sem=f"vso{vsl}")
            rings.release(("vss", vsl), f"vso_{g}_{ch}_B")
        if g == 0:
            for _ in qbuild():
                yield
    PF = 12

    def emit_stream(g, idx, first_waits):
        par = g % 2
        kk = idx % NK
        sl = idx % 16
        VH = NK * 128
        w = rings.acquire(("vs", sl)) + list(first_waits) + \
            [f"vso_{g}_{kk // 4}_B"]
        if g == 0 and idx < 16:
            w.append("vsr_init")
        dstv = bass.AP(sbt["vs_r"], VST * sl, [[16 * VST, 128], [129, 2], [1, 64]])
        srcv = bass.AP(vs_h.tensor, (128 * par) * VH + 128 * kk,
                       [[VH, 128], [64, 2], [1, 64]])
        sc.dma("sync", dstv, srcv, waits=w, ev=f"vsd_{g}_{idx}",
               sem=f"vs{sl}")

    def emit_ktload(g, kk, first_waits=()):
        par = g % 2
        w = list(first_waits) + [f"kto_{g}_{kk // 4}"]
        if g >= 1:
            w.append(f"qk_{g - 1}_{3 * NK + kk}")
        if kk >= 16:
            w.append(f"qk_{g}_{kk - 16}")
        sc.dma("sync", kt_res[:, 128 * kk:128 * (kk + 1)],
               kt_h[128 * par:128 * (par + 1), 128 * kk:128 * (kk + 1)],
               waits=w, ev=f"ktr_{g}_{kk}", sem=f"ks{kk % 16}")

    def tail_pre_gen():
        for k in range(8):
            w = rings.acquire(("stage",)) + \
                (["gD_3_6", "gD_3_7"] if k == 0 else [])
            sc.dma("gpsimd", tl_wstage[:, 0:1024],
                   din["w_out_t"][128 * k:128 * (k + 1), :], waits=w,
                   ev=f"tlWl_{k}", sem="stg2")
            fn = (lambda kk: lambda e: e.tensor_copy(
                tl_wout[:, 1024 * kk:1024 * (kk + 1)],
                tl_wstage[:, 0:1024]))(k)
            sc.emit("vector", fn, waits=[f"tlWl_{k}"], inc=("V", 1),
                    ev=f"tlW_{k}")
            rings.release(("stage",), f"tlW_{k}")
            yield
        for n in range(2):
            for gg in range(NG - 1):
                w = [f"aod_{gg}_{n}"]
                if gg == 0:
                    w = w + rings.acquire(("tla", n % 2))
                    if n == 0:
                        w = w + ["gD_3_6", "gD_3_7"]
                sc.dma("gpsimd",
                       tl_attn_r(n)[:, 512 * gg:512 * (gg + 1)],
                       attn_h[128 * gg:128 * (gg + 1),
                              512 * n:512 * (n + 1)],
                       waits=w, ev=f"tlA_{n}_{gg}", sem=f"tla{n % 2}")
                if gg % 4 == 3:
                    yield
            for k in range(8):
                w = rings.acquire(("tlg", n % 2))
                if k == 0:
                    w = w + ["gD_3_6", "gD_3_7"]
                sc.dma("gpsimd", tl_gr(n)[:, 512 * k:512 * (k + 1)],
                       g_h[128 * k:128 * (k + 1), 512 * n:512 * (n + 1)],
                       waits=w, ev=f"tlG_{n}_{k}", sem=f"tlg{n % 2}")
                if k % 4 == 3:
                    yield
        for tt in range(2):
            w = rings.acquire(("tlx", tt % 2))
            if tt == 1:
                w = w + ["kprj_7_15", "vprj_7_15_1"]
            sc.dma("gpsimd", tlx_buf(tt),
                   din["xq"][128 * tt:128 * (tt + 1), :], waits=w,
                   ev=f"tlx_{tt}", sem=f"xlq{tt % 2}")
            yield

    ph1g = ph1_gen()
    b0 = build_gen(0)

    def drive_b0(n=1):
        for _ in range(n):
            try:
                next(b0)
            except StopIteration:
                return False
        return True

    def ph1_until(target):
        for tok in ph1g:
            if tok == ("E", target):
                return

    drive_b0(1)
    drive_b0(2)
    for ch in range(16):
        ph1_until(ch)
        drive_b0(5)
    drive_b0(4)
    for _ in ph1g:
        pass
    assert not drive_b0()

    pulls = []
    pull_i = [0]
    cont = [None]
    defq = []

    def pull():
        gen = cont[0]
        cont[0] = None
        if gen is None and defq:
            gen = defq[0]
        if gen is None and pulls:
            gen = pulls[pull_i[0] % len(pulls)]
        if gen is not None:
            try:
                if next(gen) == "cont":
                    cont[0] = gen
            except StopIteration:
                if gen in pulls:
                    pulls.remove(gen)
                if defq and gen is defq[0]:
                    defq.pop(0)
        pull_i[0] += 1

    for g in range(NG):
        TT = NCH * NK
        if g == 0:
            pulls.append(ffn_w_gen())
        if g == 1:
            pulls.append(ffn_gen())
        if g + 1 < NG:
            pulls.append(build_gen(g + 1))
        if g == NG - 1:
            pulls.append(tail_pre_gen())

        bld = [f"qt_{g}_3"]
        qk_meta = []

        def emit_qk(idx, g=g, qk_meta=qk_meta, bld=bld):
            c, k = divmod(idx, NK)
            s, sw = take_turn(f"ex_{g}_{idx}")
            qk_meta.append((s, idx))
            wt = list(sw)
            if idx < NK:
                wt.append(f"ktr_{g}_{k}")
            if idx == 0:
                wt = wt + bld
            fn = (lambda cc2, ss, kk2, par2: lambda e: e.matmul(
                bank(2 * ss), kt_res[0:64, 128 * kk2:128 * (kk2 + 1)],
                qt_d[0:64, NQ * par2 + 512 * cc2:NQ * par2 + 512 * (cc2 + 1)],
                start=True, stop=True))(c, s, k, g % 2)
            sc.emit("tensor", fn, waits=wt)
            fn = (lambda cc2, ss, kk2, par2: lambda e: e.matmul(
                bank(2 * ss + 1), kt_res[64:128, 128 * kk2:128 * (kk2 + 1)],
                qt_d[64:128, NQ * par2 + 512 * cc2:NQ * par2 + 512 * (cc2 + 1)],
                start=True, stop=True))(c, s, k, g % 2)
            sc.emit("tensor", fn, inc=("QK", 1), ev=f"qk_{g}_{idx}")

        def emit_exp(idx, g=g, qk_meta=qk_meta):
            s = qk_meta[idx][0]
            p3 = idx % 3
            wt = [f"qk_{g}_{idx}"]
            fn = (lambda ss, pp: lambda e: e.activation(
                probs[:, 1024 * pp:1024 * (pp + 1)],
                ps[:, 1024 * ss:1024 * (ss + 1)], AF.Exp, scale=0.125))(s, p3)
            sc.emit("scalar", fn, waits=wt, inc=("EX", 1), ev=f"ex_{g}_{idx}")
            last_ex_ev[0] = f"ex_{g}_{idx}"

        def emit_av(idx, g=g):
            c, k = divmod(idx, NK)
            p3 = idx % 3
            sl = idx % 16
            wt = [f"ex_{g}_{idx}", f"vsd_{g}_{idx}"]
            if k == 0:
                wt = wt + banks.acquire(6) + banks.acquire(7)
            fn = (lambda sll, pp, kk2: lambda e: e.matmul(
                bank(6)[0:65, :], statA_ap(sll),
                probs[:, 1024 * pp:1024 * pp + 512],
                start=(kk2 == 0), stop=(kk2 == NK - 1)))(sl, p3, k)
            sc.emit("tensor", fn, waits=wt)
            fn = (lambda sll, pp, kk2: lambda e: e.matmul(
                bank(7), statB_ap(sll),
                probs[:, 1024 * pp + 512:1024 * (pp + 1)],
                start=(kk2 == 0), stop=(kk2 == NK - 1)))(sl, p3, k)
            sc.emit("tensor", fn, inc=("AVT", 1))
            ev = f"avs_{g}_{c}" if k == NK - 1 else None
            if ev:
                sc.events[ev] = ("AVT", sc.cnt["AVT"])
            rings.release(("vs", sl), ("#AVT", sc.cnt["AVT"]))
            if k == NK - 1:
                ec = f"avs_{g}_{c}"
                sc.emit("vector", (lambda: lambda e: e.tensor_copy(
                    av_sb[0:65, 0:512], bank(6)[0:65, :]))(), waits=[ec],
                    inc=("V", 1))
                sc.emit("vector", (lambda: lambda e: e.tensor_copy(
                    av_sb[:, 512:1024], bank(7)))(), inc=("V", 1),
                    ev=f"avc_{g}_{c}")
                banks.release(6, f"avc_{g}_{c}")
                banks.release(7, f"avc_{g}_{c}")
                defq.append(epi_gen(c, g))

        def epi_gen(c, g=g):
            sc.emit("scalar", (lambda: lambda e: e.activation(
                rec_sb[64:65, 0:512], av_sb[64:65, 0:512], AF.Ln))(),
                waits=[f"avc_{g}_{c}"], inc=("A", 1))
            sc.emit("scalar", (lambda: lambda e: e.activation(
                rec_sb[64:65, 512:1024], rec_sb[64:65, 0:512], AF.Exp,
                scale=-1.0))(), inc=("A", 1), ev=f"rcpA_{g}_{c}")
            yield
            sc.emit("scalar", (lambda: lambda e: e.activation(
                rec_sb[32:33, 0:512], av_sb[32:33, 512:1024], AF.Ln))(),
                inc=("A", 1))
            sc.emit("scalar", (lambda: lambda e: e.activation(
                rec_sb[32:33, 512:1024], rec_sb[32:33, 0:512], AF.Exp,
                scale=-1.0))(), inc=("A", 1), ev=f"rcp_{g}_{c}")
            yield
            s, sw = take_turn(f"bce_{g}_{c}")
            sc.emit("tensor", (lambda ss: lambda e: e.matmul(
                bank(2 * ss), bmask[64:65, 0:128], rec_sb[64:65, 512:1024],
                start=True, stop=True))(s), waits=list(sw) + [f"rcpA_{g}_{c}"])
            sc.emit("tensor", (lambda ss: lambda e: e.matmul(
                bank(2 * ss + 1), bmask[32:33, 0:128],
                rec_sb[32:33, 512:1024], start=True, stop=True))(s),
                waits=[f"rcp_{g}_{c}"], inc=("P", 1), ev=f"bcm_{g}_{c}")
            sc.emit("vector", (lambda ss: lambda e: e.tensor_copy(
                bc_sb[0:64, :], bank(2 * ss)[0:64, :]))(s),
                waits=[f"bcm_{g}_{c}"], inc=("V", 1))
            sc.emit("vector", (lambda ss: lambda e: e.tensor_copy(
                bc_sb[64:128, :], bank(2 * ss + 1)[64:128, :]))(s),
                inc=("V", 1), ev=f"bce_{g}_{c}")
            yield
            aosl = c % 2
            w7 = rings.acquire(("ao", aosl))
            sc.emit("vector", (lambda a2: lambda e: e.tensor_tensor(
                ao_sb[0:64, 512 * a2:512 * (a2 + 1)], av_sb[0:64, 0:512],
                bc_sb[0:64, :], ALU.mult))(aosl),
                waits=[f"bce_{g}_{c}"] + w7, inc=("V", 1))
            sc.emit("vector", (lambda a2: lambda e: e.tensor_tensor(
                ao_sb[64:128, 512 * a2:512 * (a2 + 1)],
                av_sb[64:128, 512:1024],
                bc_sb[64:128, :], ALU.mult))(aosl),
                inc=("V", 1), ev=f"ep_{g}_{c}")
            sc.dma("gpsimd",
                   attn_h[128 * g:128 * (g + 1), 512 * c:512 * (c + 1)],
                   ao_sb[:, 512 * aosl:512 * (aosl + 1)],
                   waits=[f"ep_{g}_{c}"], ev=f"aod_{g}_{c}", sem=f"ao{aosl}")
            rings.release(("ao", aosl), f"aod_{g}_{c}")

        for kk in range(16):
            emit_ktload(g, kk)
        for pidx in range(PF):
            emit_stream(g, pidx, [])
        emit_qk(0)
        emit_qk(1)
        for idx in range(TT):
            emit_exp(idx)
            pull()
            if idx < NK - 16:
                emit_ktload(g, idx + 16)
            if idx + PF < TT:
                emit_stream(g, idx + PF, [])
            if idx + 2 < TT:
                emit_qk(idx + 2)
            emit_av(idx)
        rings.release(("qtd", g % 2), f"qk_{g}_{TT - 1}")

    while pulls or defq or cont[0] is not None:
        pull()
    for s in range(3):
        if s in slot_free:
            banks.release(2 * s, slot_free[s])
            banks.release(2 * s + 1, slot_free[s])

    for n in range(NCH):
        ggs = [NG - 1] if n < 2 else list(range(NG))
        for gg in ggs:
            w = [f"aod_{gg}_{n}"]
            if n >= 2 and gg == 0:
                w = w + rings.acquire(("tla", n % 2))
            sc.dma("sync", tl_attn_r(n)[:, 512 * gg:512 * (gg + 1)],
                   attn_h[128 * gg:128 * (gg + 1), 512 * n:512 * (n + 1)],
                   waits=w, ev=f"tlA_{n}_{gg}", sem=f"tla{n % 2}")
        if n >= 2:
            for k in range(8):
                w = rings.acquire(("tlg", n % 2))
                sc.dma("gpsimd",
                       tl_gr(n)[:, 512 * k:512 * (k + 1)],
                       g_h[128 * k:128 * (k + 1), 512 * n:512 * (n + 1)],
                       waits=w, ev=f"tlG_{n}_{k}", sem=f"tlg{n % 2}")
        for m in range(8):
            for k in range(8):
                wt = []
                if k == 0:
                    wt = banks.acquire(m % 2)
                    if m <= 1:
                        wt = wt + ["tlW_7"] + \
                            [f"tlA_{n}_{gg}" for gg in range(NG)]
                fn = (lambda nn, mm, kk: lambda e: e.matmul(
                    bank(mm % 2), tl_wout[:, 1024 * kk + 128 * mm:
                                          1024 * kk + 128 * (mm + 1)],
                    tl_attn_r(nn)[:, 512 * kk:512 * (kk + 1)],
                    start=(kk == 0), stop=(kk == 7)))(n, m, k)
                sc.emit("tensor", fn, waits=wt,
                        inc=("P", 1) if k == 7 else None,
                        ev=f"wo_{n}_{m}" if k == 7 else None)
            w8 = [f"wo_{n}_{m}"]
            if m == 0:
                w8 = w8 + rings.acquire(("sT", n % 2))
            fn = (lambda nn, mm: lambda e: e.tensor_copy(
                tl_sT(nn)[:, 512 * mm:512 * (mm + 1)], bank(mm % 2)))(n, m)
            sc.emit("vector", fn, waits=w8, inc=("V", 1), ev=f"woE_{n}_{m}")
            banks.release(m % 2, f"woE_{n}_{m}")
        for j in range(4):
            tt = 4 * n + j
            for kk in range(8):
                wt = []
                if kk == 0:
                    wt = banks.acquire(2) + banks.acquire(3) + \
                        [f"woE_{n}_{mm}" for mm in range(8)]
                fn = (lambda nn, jj, k2: lambda e: e.transpose(
                    ps[:, 1024 + 128 * k2:1024 + 128 * (k2 + 1)],
                    tl_sT(nn)[:, 512 * k2 + 128 * jj:
                              512 * k2 + 128 * (jj + 1)],
                    ident_f[:, :]))(n, j, kk)
                sc.emit("tensor", fn, waits=wt,
                        inc=("P", 1) if kk == 7 else None,
                        ev=f"wT_{tt}" if kk == 7 else None)
            gb = (4, 5) if tt % 2 == 0 else (6, 7)
            gbase = 2048 if tt % 2 == 0 else 3072
            for kk in range(8):
                wt = []
                if kk == 0:
                    wt = banks.acquire(gb[0]) + banks.acquire(gb[1]) + \
                        [f"tlG_{n}_{k2}" for k2 in range(8)]
                fn = (lambda nn, jj, k2, gp: lambda e: e.transpose(
                    ps[:, gp + 128 * k2:gp + 128 * (k2 + 1)],
                    tl_gr(nn)[:, 512 * k2 + 128 * jj:
                              512 * k2 + 128 * (jj + 1)],
                    ident_f[:, :]))(n, j, kk, gbase)
                sc.emit("tensor", fn, waits=wt,
                        inc=("P", 1) if kk == 7 else None,
                        ev=f"gT_{tt}" if kk == 7 else None)
            if tt >= 2:
                w9 = rings.acquire(("tlx", tt % 2))
                sc.dma("gpsimd", tlx_buf(tt),
                       din["xq"][128 * tt:128 * (tt + 1), :], waits=w9,
                       ev=f"tlx_{tt}", sem=f"xlq{tt % 2}")
            fn = (lambda t2, xb: lambda e: e.tensor_scalar(
                tl_s1[:, :], xb, st_nm1[:, t2:t2 + 1],
                st_r1[:, t2:t2 + 1], ALU.add, ALU.mult))(tt, tlx_buf(tt))
            sc.emit("vector", fn, waits=[f"tlx_{tt}"], inc=("V", 1))
            wsa = [f"wT_{tt}"] + rings.acquire(("tls2", tt % 2))
            if tt == 1:
                wsa = wsa + ["kprj_7_15", "vprj_7_15_1"]
            fn = (lambda sb: lambda e: e.tensor_tensor(
                sb, ps[:, 1024:2048], tl_s1[:, :], ALU.add))(tls2_buf(tt))
            sc.emit("vector", fn, waits=wsa, inc=("V", 1),
                    ev=f"sA_{tt}")
            banks.release(2, f"sA_{tt}")
            banks.release(3, f"sA_{tt}")
            fn = (lambda sb, gp: lambda e: e.tensor_tensor(
                tl_s1[:, :], gp, sb, ALU.mult))(tls2_buf(tt), gT_ps(tt))
            sc.emit("vector", fn, waits=[f"gT_{tt}"], inc=("V", 1),
                    ev=f"sB_{tt}")
            banks.release(gb[0], f"sB_{tt}")
            banks.release(gb[1], f"sB_{tt}")
            fn = (lambda sb, xb: lambda e: e.tensor_tensor(
                sb, tl_s1[:, :], xb, ALU.add))(tls2_buf(tt), tlx_buf(tt))
            sc.emit("vector", fn, inc=("V", 1), ev=f"cmb_{tt}")
            rings.release(("tlx", tt % 2), f"cmb_{tt}")
            sc.dma("gpsimd" if tt % 2 == 0 else "sync",
                   out_d[128 * tt:128 * (tt + 1), :], tls2_buf(tt),
                   waits=[f"cmb_{tt}"], ev=f"out_{tt}",
                   sem="outd" if tt % 2 == 0 else "xlq2")
            rings.release(("tls2", tt % 2), f"out_{tt}")
        rings.release(("sT", n % 2), f"wT_{4 * n + 3}")
        rings.release(("tlg", n % 2), f"gT_{4 * n + 3}")
        rings.release(("tla", n % 2), f"wo_{n}_7")

    sc.emit("gpsimd", lambda e: None, waits=["out_14", "out_15"])

    with nc.Block() as block:
        @block.sync
        def _(eng):
            sc.replay("sync", eng, semobjs)

        @block.gpsimd
        def _(eng):
            sc.replay("gpsimd", eng, semobjs)

        @block.vector
        def _(eng):
            sc.replay("vector", eng, semobjs)

        @block.scalar
        def _(eng):
            sc.replay("scalar", eng, semobjs)

        @block.tensor
        def _(eng):
            sc.replay("tensor", eng, semobjs)

    ctx.close()
    return nc


_NC_CACHE = []


def kernel(**inputs):
    in_maps = _host_prepare(inputs)
    if not _NC_CACHE:
        _NC_CACHE.append(build_nc())
    nc = _NC_CACHE[0]
    res = run_bass_kernel_spmd(nc, in_maps, core_ids=list(range(NCORE)))
    out = np.empty((B, S, D), dtype=np.float32)
    for c in range(NCORE):
        bb = c // 4
        q0 = (c % 4) * NQ
        out[bb, q0:q0 + NQ] = res.results[c]["out"]
    return out


# version 16
# speedup vs baseline: 1.0577x; 1.0171x over previous
import numpy as np
import ml_dtypes
import concourse.bass as bass
import concourse.mybir as mybir
from concourse.bass_utils import run_bass_kernel_spmd

F32 = mybir.dt.float32
BF16 = mybir.dt.bfloat16
AF = mybir.ActivationFunctionType
ALU = mybir.AluOpType
AX = mybir.AxisListType

B, S, D = 2, 8192, 1024
H, DH = 16, 64
DKV = 512
NQ = 2048
NCORE = 8
NG = 8
NCH = 4
NK = 64
QC = 512
EPS = 1e-5
DEBUG_TAPS = False
VST = 193
VPITCH = NK * VST


def _rope_tables():
    ROPE_BASE = 10000.0
    inv_freq = 1.0 / ROPE_BASE ** (np.arange(0, DH, 2, dtype=np.float32) / DH)
    t = np.arange(S, dtype=np.float32)
    freqs = np.outer(t, inv_freq)
    cos = np.concatenate([np.cos(freqs), np.cos(freqs)], -1)[:, ::2]
    sin = np.concatenate([np.sin(freqs), np.sin(freqs)], -1)[:, ::2]
    return cos, sin


def _host_prepare(inp):
    bf = ml_dtypes.bfloat16
    perm = np.zeros(D, dtype=np.int64)
    for h0 in range(H):
        perm[64 * h0:64 * h0 + 32] = np.arange(0, 64, 2) + 64 * h0
        perm[64 * h0 + 32:64 * h0 + 64] = np.arange(1, 64, 2) + 64 * h0

    x = np.ascontiguousarray(np.asarray(inp["L0_attn_x"], dtype=np.float32))

    def fm(v):
        return np.ascontiguousarray(np.asarray(v, np.float32).reshape(8, 128).T)

    cos, sin = _rope_tables()
    trig_c = np.ascontiguousarray(np.tile(cos.T, (4, 1)).astype(bf))
    trig_s = np.ascontiguousarray(np.tile(sin.T, (4, 1)).astype(bf))

    ident = np.eye(128, dtype=np.float32)
    bmask = np.zeros((128, 128), dtype=np.float32)
    bmask[0, :] = 1.0
    bmask[64, 0:64] = 1.0
    bmask[32, 64:128] = 1.0
    mrot = np.zeros((128, 128), dtype=np.float32)
    for blk in range(2):
        o = 64 * blk
        for i in range(32):
            mrot[o + 32 + i, o + i] = -1.0
            mrot[o + i, o + 32 + i] = 1.0
    emat = np.zeros((8, 64), dtype=np.float32)
    mmat = np.zeros((8, 64), dtype=np.float32)
    for e in range(8):
        for r in range(8):
            emat[e, 8 * e + r] = 2.0
            mmat[r, 8 * e + r] = 1.0

    shared = {
        "w_dq_t": np.ascontiguousarray(np.asarray(inp["L0_attn_w_down_q"]).T),
        "w_dkv_t": np.ascontiguousarray(np.asarray(inp["L0_attn_w_down_kv"]).T),
        "w_uq_t": np.ascontiguousarray(np.asarray(inp["L0_attn_w_up_q"])[perm].T),
        "w_uk_t": np.ascontiguousarray(np.asarray(inp["L0_attn_w_up_k"])[perm].T),
        "w_uv_t": np.ascontiguousarray(np.asarray(inp["L0_attn_w_up_v"]).T),
        "w_out_t": np.ascontiguousarray(np.asarray(inp["L0_attn_w_out"]).T),
        "base_w_t": np.ascontiguousarray(np.asarray(inp["L0_ffn_base_w"]).T),
        "lora_A": np.ascontiguousarray(np.asarray(inp["L0_ffn_lora_A"])),
        "lora_Bp": np.ascontiguousarray(
            np.asarray(inp["L0_ffn_lora_B"]).reshape(64, D)),
        "rout_t": np.ascontiguousarray(np.asarray(inp["L0_ffn_router_w"]).T),
        "ln_g_fm": fm(inp["L0_attn_ln_scale"]),
        "ln_b_fm": fm(inp["L0_attn_ln_bias"]),
        "base_b_fm": fm(inp["L0_ffn_base_b"]),
        "rout_b": np.ascontiguousarray(
            np.asarray(inp["L0_ffn_router_b"], np.float32).reshape(1, 8)),
        "trig_c": trig_c, "trig_s": trig_s,
        "ident_f": ident, "ident_bf": ident.astype(bf),
        "bmask": bmask, "mrot": mrot.astype(bf),
        "emat": emat.astype(bf), "mmat": mmat.astype(bf),
    }
    in_maps = []
    for c in range(NCORE):
        bb = c // 4
        q0 = (c % 4) * NQ
        m = dict(shared)
        m["xb"] = np.ascontiguousarray(np.roll(x[bb], -q0, axis=0))
        m["xq"] = np.ascontiguousarray(x[bb, q0:q0 + NQ])
        m["trig_c"] = np.ascontiguousarray(np.roll(trig_c, -q0, axis=1))
        m["trig_s"] = np.ascontiguousarray(np.roll(trig_s, -q0, axis=1))
        m["trig_cq"] = np.ascontiguousarray(trig_c[:, q0:q0 + NQ])
        m["trig_sq"] = np.ascontiguousarray(trig_s[:, q0:q0 + NQ])
        in_maps.append(m)
    return in_maps


class Sched:

    ENGINES = ("sync", "gpsimd", "vector", "scalar", "tensor")

    def __init__(self):
        self.ops = {e: [] for e in self.ENGINES}
        self.cnt = {}
        self.events = {}

    ENGSEM = {"vector": "V", "scalar": "A"}

    def emit(self, engine, fn, waits=(), inc=None, ev=None):
        rw = list(waits)
        es = self.ENGSEM.get(engine)
        if es is not None:
            if inc is None:
                inc = (es, 1)
            if inc[0] == es:
                rw.append((es, self.cnt.get(es, 0)))
        if inc is not None:
            sem, k = inc
            self.cnt[sem] = self.cnt.get(sem, 0) + k
            if ev is not None:
                self.events[ev] = (sem, self.cnt[sem])
        self.ops[engine].append((fn, tuple(rw), inc))

    def dma(self, queue, out_ap, in_ap, waits=(), ev=None, sem=None):
        assert sem is not None
        self.emit(queue, lambda e, o=out_ap, i=in_ap: e.dma_start(out=o, in_=i),
                  waits=waits, inc=(sem, 16), ev=ev)

    def replay(self, engine_name, eng, semobjs):
        for fn, waits, inc in self.ops[engine_name]:
            best = {}
            for w in waits:
                if isinstance(w, tuple):
                    sem, val = w[0][1:] if w[0].startswith("#") else w[0], w[1]
                else:
                    sem, val = self.events[w]
                if val > best.get(sem, -1):
                    best[sem] = val
            for sem, val in best.items():
                eng.wait_ge(semobjs[sem], val)
            instr = fn(eng)
            if inc is not None:
                sem, k = inc
                instr.then_inc(semobjs[sem], k)


class Res:
    def __init__(self):
        self.last = {}

    def acquire(self, key):
        ev = self.last.get(key)
        return [ev] if ev is not None else []

    def release(self, key, ev):
        self.last[key] = ev


def build_nc():
    nc = bass.Bass("TRN2", target_bir_lowering=False)
    dt = nc.dram_tensor

    din = {}
    for name, shape, dty in [
            ("xb", [S, D], F32), ("xq", [NQ, D], F32),
            ("w_dq_t", [D, DKV], F32), ("w_dkv_t", [D, DKV], F32),
            ("w_uq_t", [DKV, D], F32), ("w_uk_t", [DKV, D], F32),
            ("w_uv_t", [DKV, D], F32), ("w_out_t", [D, D], F32),
            ("base_w_t", [D, D], F32), ("lora_A", [D, 8], F32),
            ("lora_Bp", [64, D], F32), ("rout_t", [D, 8], F32),
            ("ln_g_fm", [128, 8], F32), ("ln_b_fm", [128, 8], F32),
            ("base_b_fm", [128, 8], F32), ("rout_b", [1, 8], F32),
            ("trig_c", [128, S], BF16), ("trig_s", [128, S], BF16),
            ("trig_cq", [128, NQ], BF16), ("trig_sq", [128, NQ], BF16),
            ("ident_f", [128, 128], F32), ("ident_bf", [128, 128], BF16),
            ("bmask", [128, 128], F32), ("mrot", [128, 128], BF16),
            ("emat", [8, 64], BF16), ("mmat", [8, 64], BF16)]:
        din[name] = dt(name, shape, dty, kind="ExternalInput").ap()

    out_d = dt("out", [NQ, D], F32, kind="ExternalOutput").ap()
    _dbg = "ExternalOutput" if DEBUG_TAPS else "Internal"
    ckv_h = dt("ckv_h", [DKV, S], BF16, kind=_dbg).ap()
    xnt_h = dt("xnt_h", [D, NQ], BF16, kind=_dbg).ap()
    attn_h = dt("attn_h", [D, NQ], BF16, kind=_dbg).ap()
    g_h = dt("g_h", [D, NQ], F32, kind=_dbg).ap()
    kt_h = dt("kt_h", [2 * 128, S], BF16).ap()
    vs_h = dt("vs_h", [2 * 128, NK * 128], BF16).ap()
    if DEBUG_TAPS:
        dbg_kt = dt("dbg_kt", [128, S], BF16, kind="ExternalOutput").ap()
        dbg_qt = dt("dbg_qt", [128, NQ], BF16, kind="ExternalOutput").ap()
        dbg_vs = dt("dbg_vs", [128, VPITCH], BF16, kind="ExternalOutput").ap()
        dbg_cq = dt("dbg_cq", [128, 8192], BF16, kind="ExternalOutput").ap()

    from contextlib import ExitStack
    ctx = ExitStack()
    ent = ctx.enter_context

    sbt = {}

    def T(name, shape, dty):
        sbt[name] = ent(nc.sbuf_tensor(name, shape, dty))
        return sbt[name]

    st_nm1 = T("st_nm1", [128, 80], F32)
    st_r1 = T("st_r1", [128, 80], F32)
    st_s = T("st_s", [128, 80], F32)
    st_q = T("st_q", [128, 80], F32)
    st_nm2 = T("st_nm2", [128, 80], F32)
    st_r2 = T("st_r2", [128, 80], F32)
    st_v = T("st_v", [128, 80], F32)
    ident_f = T("identf", [128, 128], F32)
    ident_bf = T("identb", [128, 128], BF16)
    bmask = T("bmaskt", [128, 128], F32)
    mrot = T("mrott", [128, 128], BF16)
    emat = T("ematt", [8, 64], BF16)
    mmat = T("mmatt", [8, 64], BF16)
    vecs = T("vecs", [128, 64], F32)
    probs = T("probs", [128, 3072], BF16)
    rec_sb = T("rec_sb", [128, 1024], F32)
    bc_sb = T("bc_sb", [128, 512], F32)
    ao_sb = T("ao_sb", [128, 1024], BF16)
    av_sb = T("av_sb", [128, 1024], F32)
    ckv_in = T("ckv_in", [128, 6144], BF16)
    ab_buf = T("ab_buf", [128, 2048], BF16)
    trig_r = T("trig_r", [128, 4096], BF16)
    wu_d = T("wu_d", [128, 2 * 1536], BF16)
    kt_res = T("kt_res", [128, NK * 128], BF16)
    vs_r = T("vs_r", [128, 16 * VST], BF16)
    kts = T("kts", [128, 2 * 512], BF16)
    vss = T("vss", [128, 2 * 512], BF16)
    xnts_t = T("xnts_t", [128, 2048], BF16)
    qt_d = T("qt_d", [128, 2 * NQ], BF16)
    sqd = T("sqd", [128, 1024], BF16)
    stage = T("stage", [128, 1024], F32)
    stage2 = T("stage2", [128, 1024], F32)
    cst = T("cst", [128, 4], F32)
    beta_bf = T("beta_bf", [128, 8], BF16)

    AR = 26624
    arena = T("arena", [128, AR], F32)
    ar = arena[:, :]
    arb = ar.bitcast(BF16)
    wbeta = arb[:, 26624:34816]

    def x_ring(i):
        return ar[:, 1024 * (i % 8):1024 * (i % 8) + 1024]

    wstage = stage[:, :]

    def wdqp(k):
        return arb[:, 16384 + 512 * k:16384 + 512 * (k + 1)]

    def wdkvp(k):
        return arb[:, 20480 + 512 * k:20480 + 512 * (k + 1)]

    def xnh_ring(i):
        return arb[:, 24576 + 1024 * (i % 2):24576 + 1024 * (i % 2) + 1024]

    def xnb_ring(i):
        return arb[:, 47104 + 1024 * (i % 4):47104 + 1024 * (i % 4) + 1024]

    def xhT_buf(cc):
        o = 26624 + 4096 * (cc % 2)
        return arb[:, o:o + 4096]

    def ckv_sb(cc):
        o = 34816 + 2048 * (cc % 2)
        return arb[:, o:o + 2048]

    cq_keep = arb[:, 38912:47104]

    def xnts(i):
        return xnts_t[:, 1024 * (i % 2):1024 * (i % 2) + 1024]

    ffn_basew = arb[:, 0:8192]

    def ffn_xnt(k):
        return arb[:, 8192 + 512 * k:8192 + 512 * (k + 1)]

    ffn_loraA = arb[:, 12288:12352]
    ffn_loraB = arb[0:64, 12352:13376]
    ffn_routw = arb[:, 13376:13440]
    ffn_logits = ar[:, 6720:6848]
    ffn_gat = arb[:, 13696:13824]
    ffn_gt = arb[0:8, 13824:15872]
    ffn_midt = arb[0:8, 15872:17920]
    ffn_grep = arb[0:64, 17920:18432]
    ffn_gm = arb[0:64, 18432:20480]
    ffn_lsb = ar[:, 10240:10752]

    def ffn_u(i):
        return ar[:, 10752 + 512 * (i % 2):10752 + 512 * (i % 2) + 512]

    ffn_sig = ar[:, 11776:12288]

    def ffn_gout(i):
        return ar[:, 12288 + 512 * (i % 2):12288 + 512 * (i % 2) + 512]

    wstage_f = stage[:, :]

    def tl_attn_r(i):
        return arb[:, 4096 * (i % 2):4096 * (i % 2) + 4096]

    tl_wout = arb[:, 8192:16384]
    def tl_gr(i):
        return ar[:, 8192 + 4096 * (i % 2):8192 + 4096 * (i % 2) + 4096]
    tl_wstage = stage[:, :]

    def tl_sT(i):
        return ar[:, 16384 + 4096 * (i % 2):16384 + 4096 * (i % 2) + 4096]

    tl_x = ar[:, 24576:25600]
    tl_s1 = ar[:, 25600:26624]
    tl_s2 = stage2[:, :]
    ckv_f = ckv_in[:, :].bitcast(F32)

    def tlx_buf(tt):
        return tl_x if tt % 2 == 0 else ckv_f[:, 1024:2048]

    def tls2_buf(tt):
        return tl_s2 if tt % 2 == 0 else ckv_f[:, 2048:3072]

    def gT_ps(tt):
        return ps[:, 2048:3072] if tt % 2 == 0 else ps[:, 3072:4096]

    ps_t = ent(nc.psum_tensor("ps", [128, 4096], F32))
    ps = ps_t[:, :]
    psb = ps.bitcast(BF16)

    def bank(b3):
        return ps[:, 512 * b3:512 * (b3 + 1)]

    def bank_bf(b3):
        return psb[:, 1024 * b3:1024 * (b3 + 1)]

    semnames = (["V", "A", "P", "QK", "EX", "AVT", "ph0", "stg", "tlx",
                 "outd", "dbg"] +
                [f"xld{i}" for i in range(8)] +
                [f"trg{i}" for i in range(4)] +
                [f"kin{i}" for i in range(12)] +
                [f"fxd{i}" for i in range(8)] +
                [f"tla{i}" for i in range(2)] +
                [f"tlg{i}" for i in range(2)] +
                [f"sb{i}" for i in range(2)] +
                [f"ks{i}" for i in range(16)] +
                [f"vs{i}" for i in range(16)] +
                ["stg2", "kto0", "kto1", "vso0", "vso1", "xt0", "xt1"] +
                [f"xlq{i}" for i in range(8)] +
                [f"gd{i}" for i in range(2)] +
                [f"ao{i}" for i in range(2)])
    semobjs = {n: ent(nc.semaphore(n)) for n in semnames}

    sc = Sched()
    banks = Res()
    rings = Res()
    sc.emit("vector", lambda e: e.memset(cst[:, 0:1], EPS), inc=("V", 1),
            ev="cst_ready")

    def statA_ap(sl):
        return vs_r[:, VST * sl:VST * sl + 65]

    def statB_ap(sl):
        return vs_r[:, VST * sl + 65:VST * (sl + 1)]

    for dst, src in [(ident_f[:, :], din["ident_f"]),
                     (ident_bf[:, :], din["ident_bf"]),
                     (bmask[:, :], din["bmask"]), (mrot[:, :], din["mrot"]),
                     (emat[:, :], din["emat"]), (mmat[:, :], din["mmat"]),
                     (vecs[:, 0:8], din["ln_g_fm"]),
                     (vecs[:, 8:16], din["ln_b_fm"]),
                     (vecs[:, 16:24], din["base_b_fm"]),
                     (vecs[0:1, 32:40], din["rout_b"])]:
        sc.dma("sync", dst, src, ev="ph0last", sem="ph0")

    for ti in range(8):
        qd = "gpsimd" if ti % 2 == 0 else "sync"
        sc.dma(qd, x_ring(ti), din["xb"][128 * ti:128 * (ti + 1), :],
               ev=f"xld_{ti}", sem=f"xld{ti % 8}")

    conv_i = [0]

    def conv(name, dram, nkt, cols, dst_fn):
        for k in range(nkt):
            par = conv_i[0] % 2
            conv_i[0] += 1
            buf = wstage if par == 0 else stage2[:, :]
            key = ("stage",) if par == 0 else ("stg2c",)
            w = rings.acquire(key) + (["ph0last"] if conv_i[0] <= 2 else [])
            sc.dma("sync", buf[:, 0:cols], dram[128 * k:128 * (k + 1), :],
                   waits=w, ev=f"wl_{name}_{k}", sem="stg" if par == 0 else "xlq3")
            fn = (lambda dk, c, b: lambda e: e.tensor_copy(dk, b[:, 0:c]))(
                dst_fn(k), cols, buf)
            sc.emit("vector", fn, waits=[f"wl_{name}_{k}"], inc=("V", 1),
                    ev=f"wc_{name}_{k}")
            rings.release(key, f"wc_{name}_{k}")

    conv("dq", din["w_dq_t"], 8, 512, lambda k: wbeta[:, 512 * k:512 * (k + 1)])
    conv("dkv", din["w_dkv_t"], 8, 512,
         lambda k: wbeta[:, 4096 + 512 * k:4096 + 512 * (k + 1)])
    sc.emit("vector", lambda e: e.tensor_copy(beta_bf[:, :], vecs[:, 8:16]),
            waits=["ph0last"], inc=("V", 1), ev="beta_bf")

    for which, base, vcol in (("q", 0, 24), ("kv", 4096, 28)):
        for m in range(4):
            for k in range(8):
                wt = []
                if k == 0:
                    wt = banks.acquire(4)
                    if m == 0:
                        wt = wt + [f"wc_d{which}_7", "beta_bf"]
                fn = (lambda ba, mm, kk: lambda e: e.matmul(
                    bank(4)[:, mm:mm + 1],
                    wbeta[:, ba + 512 * kk + 128 * mm:
                          ba + 512 * kk + 128 * (mm + 1)],
                    beta_bf[:, kk:kk + 1], start=(kk == 0),
                    stop=(kk == 7)))(base, m, k)
                ev = f"biasmm_{which}" if (m == 3 and k == 7) else None
                sc.emit("tensor", fn, waits=wt, inc=("P", 1) if ev else None,
                        ev=ev)
        fn = (lambda vc: lambda e: e.tensor_copy(
            vecs[:, vc:vc + 4], bank(4)[:, 0:4]))(vcol)
        sc.emit("vector", fn, waits=[f"biasmm_{which}"], inc=("V", 1),
                ev=f"biasev_{which}")
        banks.release(4, f"biasev_{which}")

    for which, base, dstf in (("q", 0, wdqp), ("kv", 4096, wdkvp)):
        for k in range(8):
            fn = (lambda ba, kk, df: lambda e: e.tensor_scalar(
                df(kk), wbeta[:, ba + 512 * kk:ba + 512 * (kk + 1)],
                vecs[:, kk:kk + 1], None, ALU.mult))(base, k, dstf)
            sc.emit("vector", fn,
                    waits=[f"biasev_{which}"] if k == 0 else (),
                    inc=("V", 1), ev=f"wsc_{which}_{k}")

    sc.emit("tensor", lambda e: e.matmul(
        bank(4)[:, 0:8], bmask[0:1, 0:128], vecs[0:1, 32:40],
        start=True, stop=True),
        waits=["ph0last"] + banks.acquire(4), inc=("P", 1), ev="rbmm")
    sc.emit("vector", lambda e: e.tensor_copy(vecs[:, 40:48], bank(4)[:, 0:8]),
            waits=["rbmm"], inc=("V", 1), ev="rbev")
    banks.release(4, "rbev")

    sc.emit("vector", lambda e: e.memset(vs_r[:, :], 0.0), inc=("V", 1))
    on64 = bass.AP(sbt["vs_r"], 64, [[16 * VST, 128], [VST, 16]])
    on97 = bass.AP(sbt["vs_r"], 97, [[16 * VST, 128], [VST, 16]])
    sc.emit("vector", (lambda a: lambda e: e.memset(a, 1.0))(on64),
            inc=("V", 1))
    sc.emit("vector", (lambda a: lambda e: e.memset(a, 1.0))(on97),
            inc=("V", 1), ev="vsr_init")

    turn = [0]
    slot_free = {}

    def take_turn(free_ev):
        s = turn[0] % 3
        turn[0] += 1
        w = [slot_free[s]] if s in slot_free else []
        slot_free[s] = free_ev
        return s, w

    slot_free[2] = "rbev"

    last_ex_ev = [None]

    def ph1_gen():
        def phA(cc):
            if cc < 2:
                return
            t0 = 4 * cc
            for j in range(4):
                ti = t0 + j
                qd = "gpsimd" if ti % 2 == 0 else "sync"
                sc.dma(qd, x_ring(ti), din["xb"][128 * ti:128 * (ti + 1), :],
                       waits=rings.acquire(("x", ti % 8)) +
                       rings.acquire(("x2", ti % 8)), ev=f"xld_{ti}",
                       sem=f"xld{ti % 8}")

        def phB(cc):
            is_qc = cc < 4
            t0 = 4 * cc
            for j in range(4):
                ti = t0 + j
                sc.emit("vector", (lambda t: lambda e: e.reduce_sum(
                    st_s[:, t:t + 1], x_ring(t), axis=AX.X))(ti),
                    waits=[f"xld_{ti}"], inc=("V", 1), ev=f"xsum_{ti}")
                sc.emit("scalar", (lambda t: lambda e: e.activation(
                    sqd[:, 0:1024], x_ring(t), AF.Square,
                    accum_out=st_q[:, t:t + 1]))(ti),
                    waits=[f"xld_{ti}"], inc=("A", 1), ev=f"sq1_{ti}")
            c4 = slice(4 * cc, 4 * cc + 4)
            sc.emit("vector", (lambda c: lambda e: e.tensor_scalar(
                st_nm1[:, c], st_s[:, c], -1.0 / 1024, None,
                ALU.mult))(c4), inc=("V", 1))
            sc.emit("vector", (lambda c: lambda e: e.tensor_tensor(
                st_v[:, c], st_nm1[:, c], st_nm1[:, c], ALU.mult))(c4),
                inc=("V", 1))
            sc.emit("vector", (lambda c: lambda e: e.scalar_tensor_tensor(
                st_v[:, c], st_q[:, c], 1.0 / 1024, st_v[:, c], ALU.mult,
                ALU.subtract))(c4), waits=[f"sq1_{t0 + 3}"], inc=("V", 1),
                ev=f"var1_{cc}")
            sc.emit("scalar", (lambda c: lambda e: e.activation(
                st_r1[:, c], st_v[:, c], AF.Ln, bias=cst[:, 0:1]))(c4),
                waits=[f"var1_{cc}"], inc=("A", 1))
            sc.emit("scalar", (lambda c: lambda e: e.activation(
                st_r1[:, c], st_r1[:, c], AF.Exp, scale=-0.5))(c4),
                inc=("A", 1), ev=f"rstd1_{cc}")
            sc.emit("vector", (lambda c: lambda e: e.tensor_tensor(
                st_nm2[:, c], st_r1[:, c], st_r1[:, c], ALU.mult))(c4),
                waits=[f"rstd1_{cc}"], inc=("V", 1))
            sc.emit("vector", (lambda c: lambda e: e.tensor_tensor(
                st_nm2[:, c], st_nm2[:, c], st_v[:, c], ALU.mult))(c4),
                inc=("V", 1), ev=f"var2_{cc}")
            sc.emit("scalar", (lambda c: lambda e: e.activation(
                st_r2[:, c], st_nm2[:, c], AF.Ln, bias=cst[:, 0:1]))(c4),
                waits=[f"var2_{cc}"], inc=("A", 1))
            sc.emit("scalar", (lambda c: lambda e: e.activation(
                st_r2[:, c], st_r2[:, c], AF.Exp, scale=-0.5))(c4),
                inc=("A", 1), ev=f"rstd2_{cc}")
            sc.emit("vector", (lambda c: lambda e: e.tensor_tensor(
                st_r2[:, c], st_r2[:, c], st_r1[:, c], ALU.mult))(c4),
                waits=[f"rstd2_{cc}"], inc=("V", 1))
            sc.emit("vector", (lambda c: lambda e: e.tensor_tensor(
                st_nm2[:, c], st_nm1[:, c], st_r2[:, c], ALU.mult))(c4),
                inc=("V", 1), ev=f"aab_{cc}")
            if is_qc:
                sc.emit("vector", (lambda c: lambda e: e.tensor_tensor(
                    st_v[:, c], st_nm1[:, c], st_r1[:, c], ALU.mult))(c4),
                    inc=("V", 1), ev=f"rb1_{cc}")

        def phD(cc, dpair):
            is_qc = cc < 4
            t0 = 4 * cc
            s, sw = take_turn(f"trE_{t0 + 2 * dpair + 1}")
            for j2 in range(2):
                ti = t0 + 2 * dpair + j2
                w = [f"aab_{cc}"] + rings.acquire(("xnh", ti % 2))
                sc.emit("scalar", (lambda t: lambda e: e.activation(
                    xnh_ring(t), x_ring(t), AF.Identity,
                    scale=st_r2[:, t:t + 1],
                    bias=st_nm2[:, t:t + 1]))(ti),
                    waits=w, inc=("A", 1), ev=f"xnh_{ti}")
                if is_qc:
                    w3 = [f"rb1_{cc}"] + rings.acquire(("xnb", ti % 4))
                    sc.emit("scalar", (lambda t: lambda e: e.activation(
                        xnb_ring(t), x_ring(t), AF.Identity,
                        scale=st_r1[:, t:t + 1],
                        bias=st_v[:, t:t + 1]))(ti), waits=w3,
                        inc=("A", 1), ev=f"xnb_{ti}")
                rings.release(("x", ti % 8),
                              f"xnb_{ti}" if is_qc else f"xnh_{ti}")
                rings.last[("x2", ti % 8)] = f"xsum_{ti}"
                bb = 2 * s + j2
                for kk in range(8):
                    wt = []
                    if kk == 0:
                        wt = [f"xnh_{ti}"] + (list(sw) if j2 == 0 else [])
                    fn = (lambda t, k2, b2: lambda e: e.transpose(
                        bank_bf(b2)[:, 128 * k2:128 * (k2 + 1)],
                        xnh_ring(t)[:, 128 * k2:128 * (k2 + 1)],
                        ident_bf[:, :]))(ti, kk, bb)
                    sc.emit("tensor", fn, waits=wt,
                            inc=("P", 1) if kk == 7 else None,
                            ev=f"trT_{ti}" if kk == 7 else None)
            for j2 in range(2):
                ti = t0 + 2 * dpair + j2
                j = 2 * dpair + j2
                bb = 2 * s + j2
                dv = xhT_buf(cc).rearrange("p (k t) -> p k t", t=512)[
                    :, :, 128 * j:128 * (j + 1)]
                fn = (lambda d, b2: lambda e: e.tensor_copy(
                    d, bank_bf(b2).rearrange("p (k t) -> p k t",
                                             t=128)))(dv, bb)
                w2 = [f"trT_{ti}"] + \
                    (rings.acquire(("xhT", cc % 2)) if j == 0 else [])
                sc.emit("vector", fn, waits=w2, inc=("V", 1),
                        ev=f"trE_{ti}")
                rings.release(("xnh", ti % 2), f"trE_{ti}")

        def phE(cc, epair, pname):
            t0 = 4 * cc
            wfn = wdqp if pname == "cq" else wdkvp
            s, sw = take_turn(f"{pname}E_{cc}_{2 * epair + 1}")
            for m2 in range(2):
                m = 2 * epair + m2
                for k in range(8):
                    wt = []
                    if k == 0:
                        wt = (list(sw) if m2 == 0 else []) + \
                            [f"trE_{t0 + 3}"]
                        if cc == 0 and m == 0:
                            wt = wt + (["wsc_q_7"] if pname == "cq"
                                       else ["wsc_kv_7"])
                    fn = (lambda mm, kk, wf, c, b2: lambda e: e.matmul(
                        bank(b2), wf(kk)[:, 128 * mm:128 * (mm + 1)],
                        xhT_buf(c)[:, 512 * kk:512 * (kk + 1)],
                        start=(kk == 0), stop=(kk == 7)))(
                            m, k, wfn, cc, 2 * s + m2)
                    ev = f"{pname}M_{cc}_{m}" if k == 7 else None
                    sc.emit("tensor", fn, waits=wt,
                            inc=("P", 1) if ev else None, ev=ev)
            for m2 in range(2):
                m = 2 * epair + m2
                if pname == "ckv":
                    w4 = [f"ckvM_{cc}_{m}"]
                    if m == 0:
                        w4 += rings.acquire(("ckvsb", cc % 2))
                        if cc >= 2:
                            w4.append(f"vprj_0_{cc - 2}_1")
                    fn = (lambda c, mm, b2: lambda e: e.tensor_scalar(
                        ckv_sb(c)[:, 512 * mm:512 * (mm + 1)], bank(b2),
                        vecs[:, 28 + mm:29 + mm], None, ALU.add))(
                            cc, m, 2 * s + m2)
                    sc.emit("vector", fn, waits=w4, inc=("V", 1),
                            ev=f"ckvE_{cc}_{m}")
                    sc.dma("gpsimd" if cc % 2 == 0 else "sync",
                           ckv_h[128 * m:128 * (m + 1),
                                 512 * cc:512 * (cc + 1)],
                           ckv_sb(cc)[:, 512 * m:512 * (m + 1)],
                           waits=[f"ckvE_{cc}_{m}"], ev=f"ckvD_{cc}_{m}",
                           sem=f"sb{cc % 2}")
                    if m == 3:
                        rings.release(("ckvsb", cc % 2), f"ckvD_{cc}_3")
                else:
                    fn = (lambda c3, mm, b2: lambda e: e.tensor_scalar(
                        cq_keep[:, 2048 * mm + 512 * c3:
                                2048 * mm + 512 * (c3 + 1)],
                        bank(b2), vecs[:, 24 + mm:25 + mm], None,
                        ALU.add))(cc, m, 2 * s + m2)
                    sc.emit("vector", fn, waits=[f"cqM_{cc}_{m}"],
                            inc=("V", 1), ev=f"cqE_{cc}_{m}")

        def phF(cc, fpair):
            t0 = 4 * cc
            s, sw = take_turn(f"xnTe_{t0 + 2 * fpair + 1}")
            for j2 in range(2):
                ti = t0 + 2 * fpair + j2
                bb = 2 * s + j2
                for kk in range(8):
                    wt = []
                    if kk == 0:
                        wt = [f"xnb_{ti}"] + \
                            (list(sw) if j2 == 0 else [])
                    fn = (lambda t, k2, b2: lambda e: e.transpose(
                        bank_bf(b2)[:, 128 * k2:128 * (k2 + 1)],
                        xnb_ring(t)[:, 128 * k2:128 * (k2 + 1)],
                        ident_bf[:, :]))(ti, kk, bb)
                    sc.emit("tensor", fn, waits=wt,
                            inc=("P", 1) if kk == 7 else None,
                            ev=f"xnT_{ti}" if kk == 7 else None)
            for j2 in range(2):
                ti = t0 + 2 * fpair + j2
                bb = 2 * s + j2
                w5 = [f"xnT_{ti}"] + rings.acquire(("xnts", ti % 2))
                fn = (lambda t, b2: lambda e: e.tensor_copy(
                    xnts(t), bank_bf(b2)))(ti, bb)
                sc.emit("vector", fn, waits=w5, inc=("V", 1),
                        ev=f"xnTe_{ti}")
                rings.release(("xnb", ti % 4), f"xnTe_{ti}")
                dstv = xnt_h.rearrange("(k p) t -> p k t", p=128)[
                    :, :, 128 * ti:128 * (ti + 1)]
                srcv = xnts(ti).rearrange("p (k t) -> p k t", t=128)
                sc.dma("gpsimd", dstv, srcv, waits=[f"xnTe_{ti}"],
                       ev=f"xnTd_{ti}", sem=f"xt{ti % 2}")
                rings.release(("xnts", ti % 2), f"xnTd_{ti}")

        NCHK = 16
        for i in range(NCHK + 3):
            cc = i - 3
            if 0 <= cc < NCHK:
                for ep in range(2):
                    phE(cc, ep, "ckv")
                    yield
                if cc < 4:
                    for ep in range(2):
                        phE(cc, ep, "cq")
                        yield
                rings.release(("xhT", cc % 2),
                              f"cqM_{cc}_3" if cc < 4 else f"ckvM_{cc}_3")
                yield ("E", cc)
                if cc < 4:
                    for fp in range(2):
                        phF(cc, fp)
                        yield
            if 2 <= i < NCHK + 2:
                for dp in range(2):
                    phD(i - 2, dp)
                    yield
            if 1 <= i < NCHK + 1:
                phB(i - 1)
            yield
            if i < NCHK:
                phA(i)
            yield
        sc.emit("vector", lambda e: e.memset(rec_sb[:, 0:1], 0.0),
                waits=["xnh_63"], inc=("V", 1), ev="xfree")

    def ffn_w_gen():
        for k in range(8):
            w = rings.acquire(("stage",)) + (["xfree"] if k == 0 else [])
            sc.dma("gpsimd", wstage_f[:, 0:1024],
                   din["base_w_t"][128 * k:128 * (k + 1), :], waits=w,
                   ev=f"fwl_{k}", sem="stg2")
            fn = (lambda kk: lambda e: e.tensor_copy(
                ffn_basew[:, 1024 * kk:1024 * (kk + 1)],
                wstage_f[:, 0:1024]))(k)
            sc.emit("vector", fn, waits=[f"fwl_{k}"], inc=("V", 1),
                    ev=f"fwc_{k}")
            rings.release(("stage",), f"fwc_{k}")
            if k % 2 == 1:
                yield
        srcA = din["lora_A"].rearrange("(k p) r -> p k r", p=128)
        sc.dma("gpsimd", wstage_f[:, 0:64].rearrange("p (k r) -> p k r", r=8),
               srcA, waits=rings.acquire(("stage",)), ev="flA", sem="stg2")
        sc.emit("vector", lambda e: e.tensor_copy(ffn_loraA[:, :],
                                                  wstage_f[:, 0:64]),
                waits=["flA"], inc=("V", 1), ev="fcA")
        rings.release(("stage",), "fcA")
        sc.dma("gpsimd", wstage_f[0:64, 0:1024], din["lora_Bp"],
               waits=rings.acquire(("stage",)), ev="flB", sem="stg2")
        sc.emit("vector", lambda e: e.tensor_copy(ffn_loraB[:, :],
                                                  wstage_f[0:64, 0:1024]),
                waits=["flB"], inc=("V", 1), ev="fcB")
        rings.release(("stage",), "fcB")
        srcR = din["rout_t"].rearrange("(k p) r -> p k r", p=128)
        sc.dma("gpsimd", wstage_f[:, 0:64].rearrange("p (k r) -> p k r", r=8),
               srcR, waits=rings.acquire(("stage",)), ev="flR", sem="stg2")
        sc.emit("vector", lambda e: e.tensor_copy(ffn_routw[:, :],
                                                  wstage_f[:, 0:64]),
                waits=["flR"], inc=("V", 1), ev="fcR")
        rings.release(("stage",), "fcR")
        yield
        for k in range(8):
            w = rings.acquire(("fxnt", k)) + ["xfree", "xnTd_14", "xnTd_15"]
            sc.dma("gpsimd", ffn_xnt(k),
                   xnt_h[128 * k:128 * (k + 1), 0:512],
                   waits=w, ev=f"fx_0_{k}", sem=f"fxd{k}")
            if k % 4 == 3:
                yield

    def ffn_gen():
        for n in range(4):
            if n > 0:
                for k in range(8):
                    w = rings.acquire(("fxnt", k))
                    sc.dma("gpsimd", ffn_xnt(k),
                           xnt_h[128 * k:128 * (k + 1), 512 * n:512 * (n + 1)],
                           waits=w, ev=f"fx_{n}_{k}",
                           sem=f"fxd{k}")
            yield
            for j in range(4):
                m = 4 * n + j
                s, sw = take_turn(f"logE_{m}")
                for k in range(8):
                    wt = []
                    if j == 0:
                        wt += [f"fx_{n}_{k}"]
                    if k == 0:
                        wt += list(sw)
                        if m == 0:
                            wt += ["fcR", "rbev"]
                    fn = (lambda jj, kk, ss: lambda e: e.matmul(
                        bank(2 * ss)[:, 0:8],
                        ffn_xnt(kk)[:, 128 * jj:128 * (jj + 1)],
                        ffn_routw[:, 8 * kk:8 * (kk + 1)],
                        start=(kk == 0), stop=(kk == 7)))(j, k, s)
                    sc.emit("tensor", fn, waits=wt,
                            inc=("P", 1) if k == 7 else None,
                            ev=f"logM_{m}" if k == 7 else None)
                fn = (lambda mm, ss: lambda e: e.tensor_tensor(
                    ffn_logits[:, 8 * mm:8 * (mm + 1)], bank(2 * ss)[:, 0:8],
                    vecs[:, 40:48], ALU.add))(m, s)
                sc.emit("vector", fn, waits=[f"logM_{m}"], inc=("V", 1),
                        ev=f"logE_{m}")
                fn = (lambda mm: lambda e: e.reduce_max(
                    st_v[:, mm:mm + 1], ffn_logits[:, 8 * mm:8 * (mm + 1)],
                    axis=AX.X))(m)
                sc.emit("vector", fn, inc=("V", 1))
                fn = (lambda mm: lambda e: e.tensor_scalar(
                    st_s[:, mm:mm + 1], st_v[:, mm:mm + 1], -1.0, None,
                    ALU.mult))(m)
                sc.emit("vector", fn, inc=("V", 1), ev=f"gmx_{m}")
                fn = (lambda mm: lambda e: e.activation(
                    stage2[:, 16:24], ffn_logits[:, 8 * mm:8 * (mm + 1)],
                    AF.Exp, bias=st_s[:, mm:mm + 1],
                    accum_out=st_q[:, mm:mm + 1]))(m)
                sc.emit("scalar", fn, waits=[f"gmx_{m}"], inc=("A", 1),
                        ev=f"gex_{m}")
                fn = (lambda mm: lambda e: e.reciprocal(
                    st_v[:, mm:mm + 1], st_q[:, mm:mm + 1]))(m)
                sc.emit("vector", fn, waits=[f"gex_{m}"], inc=("V", 1))
                fn = (lambda mm: lambda e: e.tensor_scalar(
                    ffn_gat[:, 8 * mm:8 * (mm + 1)], stage2[:, 16:24],
                    st_v[:, mm:mm + 1], None, ALU.mult))(m)
                sc.emit("vector", fn, inc=("V", 1), ev=f"gat_{m}")
                yield
            s, sw = take_turn(f"midE_{n}")
            for k in range(8):
                wt = (list(sw) + ["fcA"]) if k == 0 else []
                fn = (lambda kk, ss: lambda e: e.matmul(
                    bank(2 * ss)[0:8, :], ffn_loraA[:, 8 * kk:8 * (kk + 1)],
                    ffn_xnt(kk)[:, :], start=(kk == 0), stop=(kk == 7)))(k, s)
                sc.emit("tensor", fn, waits=wt,
                        inc=("P", 1) if k == 7 else None,
                        ev=f"midM_{n}" if k == 7 else None)
                if k == 3:
                    yield "cont"
            fn = (lambda nn, ss: lambda e: e.tensor_copy(
                ffn_midt[:, 512 * nn:512 * (nn + 1)],
                bank(2 * ss)[0:8, :]))(n, s)
            sc.emit("vector", fn, waits=[f"midM_{n}"], inc=("V", 1),
                    ev=f"midE_{n}")
            for k in range(8):
                rings.release(("fxnt", k), f"midE_{n}")
            yield
        s, sw = take_turn("gtE")
        for m in range(16):
            wt = []
            if m == 0:
                wt = list(sw) + [f"gat_{mm}" for mm in range(16)]
            b3 = 2 * s + (m // 8)
            fn = (lambda mm, bb: lambda e: e.transpose(
                bank_bf(bb)[0:8, 128 * (mm % 8):128 * ((mm % 8) + 1)],
                ffn_gat[:, 8 * mm:8 * (mm + 1)], ident_bf[:, :]))(m, b3)
            sc.emit("tensor", fn, waits=wt, inc=("P", 1) if m == 15 else None,
                    ev="gtT" if m == 15 else None)
        fn = (lambda ss: lambda e: e.tensor_copy(
            ffn_gt[:, 0:1024], bank_bf(2 * ss)[0:8, :]))(s)
        sc.emit("vector", fn, waits=["gtT"], inc=("V", 1))
        fn = (lambda ss: lambda e: e.tensor_copy(
            ffn_gt[:, 1024:2048], bank_bf(2 * ss + 1)[0:8, :]))(s)
        sc.emit("vector", fn, inc=("V", 1), ev="gtE")
        yield
        for n in range(4):
            s, sw = take_turn(f"gm_{n}")
            fn = (lambda nn, ss: lambda e: e.matmul(
                bank(2 * ss)[0:64, :], emat[:, :],
                ffn_gt[:, 512 * nn:512 * (nn + 1)], start=True,
                stop=True))(n, s)
            sc.emit("tensor", fn, waits=list(sw) + ["gtE"], inc=("P", 1),
                    ev=f"grepM_{n}")
            fn = (lambda nn, ss: lambda e: e.matmul(
                bank(2 * ss + 1)[0:64, :], mmat[:, :],
                ffn_midt[:, 512 * nn:512 * (nn + 1)], start=True,
                stop=True))(n, s)
            sc.emit("tensor", fn, waits=[f"midE_{n}"], inc=("P", 1),
                    ev=f"mrepM_{n}")
            fn = (lambda nn, ss: lambda e: e.tensor_copy(
                ffn_grep[:, :], bank(2 * ss)[0:64, :]))(n, s)
            sc.emit("vector", fn, waits=[f"grepM_{n}"], inc=("V", 1),
                    ev=f"grepE_{n}")
            fn = (lambda nn, ss: lambda e: e.tensor_tensor(
                ffn_gm[:, 512 * nn:512 * (nn + 1)], bank(2 * ss + 1)[0:64, :],
                ffn_grep[:, :], ALU.mult))(n, s)
            sc.emit("vector", fn, waits=[f"mrepM_{n}", f"grepE_{n}"],
                    inc=("V", 1), ev=f"gm_{n}")
            yield
        for n in range(4):
            for k in range(8):
                sc.dma("gpsimd", ffn_xnt(k),
                       xnt_h[128 * k:128 * (k + 1), 512 * n:512 * (n + 1)],
                       waits=rings.acquire(("fxnt", k)), ev=f"fx2_{n}_{k}",
                       sem=f"fxd{k}")
            yield
            for m in range(8):
                ii = n * 8 + m
                s, sw = take_turn(f"u_{n}_{m}")
                wt = list(sw) + [f"gm_{n}"]
                if n == 0 and m == 0:
                    wt += ["fcB"]
                fn = (lambda nn, mm, ss: lambda e: e.matmul(
                    bank(2 * ss + 1), ffn_loraB[:, 128 * mm:128 * (mm + 1)],
                    ffn_gm[:, 512 * nn:512 * (nn + 1)], start=True,
                    stop=True))(n, m, s)
                sc.emit("tensor", fn, waits=wt, inc=("P", 1),
                        ev=f"lraM_{n}_{m}")
                fn = (lambda ss: lambda e: e.tensor_copy(
                    ffn_lsb[:, :], bank(2 * ss + 1)))(s)
                sc.emit("vector", fn, waits=[f"lraM_{n}_{m}"], inc=("V", 1),
                        ev=f"lraE_{n}_{m}")
                for k in range(8):
                    wt = []
                    if m == 0:
                        wt += [f"fx2_{n}_{k}"]
                    if k == 0:
                        wt += ["fwc_7"]
                    fn = (lambda mm, kk, ss: lambda e: e.matmul(
                        bank(2 * ss), ffn_basew[:, 1024 * kk + 128 * mm:
                                                1024 * kk + 128 * (mm + 1)],
                        ffn_xnt(kk)[:, :], start=(kk == 0),
                        stop=(kk == 7)))(m, k, s)
                    sc.emit("tensor", fn, waits=wt,
                            inc=("P", 1) if k == 7 else None,
                            ev=f"basM_{n}_{m}" if k == 7 else None)
                    if k == 3:
                        yield "cont"
                fn = (lambda mm, ii2, ss: lambda e: e.scalar_tensor_tensor(
                    ffn_u(ii2), bank(2 * ss), vecs[:, 16 + mm:17 + mm],
                    ffn_lsb[:, :], ALU.add, ALU.add))(m, ii, s)
                w6 = [f"basM_{n}_{m}", f"lraE_{n}_{m}"] + \
                    rings.acquire(("fu", ii % 2))
                sc.emit("vector", fn, waits=w6, inc=("V", 1), ev=f"u_{n}_{m}")
                w7 = [f"u_{n}_{m}"]
                if ii >= 1:
                    w7.append(f"sg1_{(ii - 1) // 8}_{(ii - 1) % 8}")
                fn = (lambda ii2: lambda e: e.activation(
                    ffn_sig[:, :], ffn_u(ii2), AF.Exp, scale=-1.0))(ii)
                sc.emit("scalar", fn, waits=w7, inc=("A", 1), ev=f"es_{n}_{m}")
                fn = (lambda: lambda e: e.tensor_scalar(
                    stage2[:, 0:512], ffn_sig[:, :], 1.0, None, ALU.add))()
                sc.emit("vector", fn, waits=[f"es_{n}_{m}"], inc=("V", 1),
                        ev=f"sg1_{n}_{m}")
                fn = (lambda: lambda e: e.reciprocal(stage2[:, 512:1024],
                                                     stage2[:, 0:512]))()
                sc.emit("vector", fn, inc=("V", 1))
                fn = (lambda ii2: lambda e: e.tensor_tensor(
                    ffn_gout(ii2), ffn_u(ii2), stage2[:, 512:1024],
                    ALU.mult))(ii)
                w8 = rings.acquire(("gout", ii % 2))
                sc.emit("vector", fn, waits=w8, inc=("V", 1), ev=f"g_{n}_{m}")
                rings.release(("fu", ii % 2), f"g_{n}_{m}")
                sc.dma("gpsimd",
                       g_h[128 * m:128 * (m + 1), 512 * n:512 * (n + 1)],
                       ffn_gout(ii), waits=[f"g_{n}_{m}"], ev=f"gD_{n}_{m}",
                       sem=f"gd{ii % 2}")
                rings.release(("gout", ii % 2), f"gD_{n}_{m}")
                yield
            for k in range(8):
                rings.release(("fxnt", k), f"basM_{n}_{7}")

    def build_gen(g):
        par = g % 2
        if g >= 1:
            for _ in range(40):
                yield
        wuw = []
        if g >= 2:
            wuw = [f"qprj_{g - 2}_3"]
        for wi, wname in enumerate(("w_uk_t", "w_uv_t", "w_uq_t")):
            rearr = din[wname].rearrange("(k p) c -> p k c", p=128)
            srcw = rearr[:, :, 128 * g:128 * (g + 1)]
            w = rings.acquire(("stage",)) + (wuw if wi == 0 else [])
            dstw = stage[:, 0:512].rearrange("p (k c) -> p k c", c=128)
            sc.dma("gpsimd", dstw, srcw, waits=w, ev=f"wul_{g}_{wi}",
                   sem="stg2")
            fn = (lambda wi2: lambda e: e.tensor_copy(
                wu_d[:, 1536 * par + 512 * wi2:1536 * par + 512 * (wi2 + 1)],
                stage[:, 0:512]))(wi)
            sc.emit("vector", fn, waits=[f"wul_{g}_{wi}"], inc=("V", 1),
                    ev=f"wuc_{g}_{wi}")
            rings.release(("stage",), f"wuc_{g}_{wi}")
        yield

        def wu_k(kk):
            return wu_d[:, 1536 * par + 128 * kk:1536 * par + 128 * (kk + 1)]

        def wu_v(kk):
            o = 1536 * par + 512
            return wu_d[:, o + 128 * kk:o + 128 * (kk + 1)]

        def wu_q(kk):
            o = 1536 * par + 1024
            return wu_d[:, o + 128 * kk:o + 128 * (kk + 1)]

        def qbuild():
            for qc2 in range(4):
                tsl2 = qc2 % 4
                w = rings.acquire(("trig", tsl2))
                sc.dma("gpsimd", trig_r[:, 1024 * tsl2:1024 * tsl2 + 512],
                           din["trig_cq"][:, 512 * qc2:512 * (qc2 + 1)], waits=w,
                           ev=f"tqc_{g}_{qc2}", sem=f"trg{tsl2}")
                sc.dma("gpsimd", trig_r[:, 1024 * tsl2 + 512:1024 * (tsl2 + 1)],
                           din["trig_sq"][:, 512 * qc2:512 * (qc2 + 1)],
                           ev=f"tqs_{g}_{qc2}", sem=f"trg{tsl2}")
                s, sw = take_turn(f"qab_{g}_{qc2}")
                for k in range(4):
                        wt = []
                        if k == 0:
                            wt = list(sw)
                            if qc2 == 0:
                                    wt = wt + rings.acquire(("qtd", par)) + \
                                        [f"wuc_{g}_2"]
                            if qc2 == 0 and g <= 1:
                                    wt = wt + ["cqE_3_3"]
                        fn = (lambda c2, kk, ss: lambda e: e.matmul(
                            bank(2 * ss), wu_q(kk),
                            cq_keep[:, 2048 * kk + 512 * c2:2048 * kk + 512 * (c2 + 1)],
                            start=(kk == 0), stop=(kk == 3)))(qc2, k, s)
                        sc.emit("tensor", fn, waits=wt,
                                    inc=("P", 1) if k == 3 else None,
                                    ev=f"qprj_{g}_{qc2}" if k == 3 else None)
                asl = qc2 % 2
                wA = rings.acquire(("ab", asl)) + [f"qprj_{g}_{qc2}",
                                                                   f"tqc_{g}_{qc2}",
                                                                   f"tqs_{g}_{qc2}"]
                fn = (lambda a2, t3, ss: lambda e: e.tensor_tensor(
                        ab_buf[:, 1024 * a2:1024 * a2 + 512], bank(2 * ss),
                        trig_r[:, 1024 * t3:1024 * t3 + 512], ALU.mult))(asl, tsl2, s)
                sc.emit("vector", fn, waits=wA, inc=("V", 1))
                fn = (lambda a2, t3, ss: lambda e: e.tensor_tensor(
                        ab_buf[:, 1024 * a2 + 512:1024 * (a2 + 1)], bank(2 * ss),
                        trig_r[:, 1024 * t3 + 512:1024 * (t3 + 1)],
                        ALU.mult))(asl, tsl2, s)
                sc.emit("vector", fn, inc=("V", 1), ev=f"qab_{g}_{qc2}")
                rings.release(("trig", tsl2), f"qab_{g}_{qc2}")
                s2, sw2 = take_turn(f"qt_{g}_{qc2}")
                fn = (lambda a2, ss: lambda e: e.matmul(
                        bank(2 * ss), mrot[:, :],
                        ab_buf[:, 1024 * a2 + 512:1024 * (a2 + 1)],
                        start=True, stop=True))(asl, s2)
                sc.emit("tensor", fn, waits=list(sw2) + [f"qab_{g}_{qc2}"],
                            inc=("P", 1), ev=f"qrot_{g}_{qc2}")
                fn = (lambda c2, a2, ss: lambda e: e.tensor_tensor(
                        qt_d[:, NQ * par + 512 * c2:NQ * par + 512 * (c2 + 1)],
                        bank(2 * ss), ab_buf[:, 1024 * a2:1024 * a2 + 512],
                        ALU.add))(qc2, asl, s2)
                sc.emit("vector", fn, waits=[f"qrot_{g}_{qc2}"], inc=("V", 1),
                            ev=f"qt_{g}_{qc2}")
                rings.release(("ab", asl), f"qt_{g}_{qc2}")
                yield


        if g != 0:
            for _ in qbuild():
                yield

        war = []
        if g >= 2:
            war = [f"qk_{g - 2}_{NCH * NK - 1}", f"avs_{g - 2}_{NCH - 1}"]

        def chunk_dmas(ch):
            for k in (() if g == 0 else range(4)):
                sl = (4 * ch + k) % 12
                w = rings.acquire(("ckvin", sl))
                if g == 1:
                    w = w + [f"ckvD_{ch}_3"]
                sc.dma("gpsimd", ckv_in[:, 512 * sl:512 * (sl + 1)],
                       ckv_h[128 * k:128 * (k + 1), 512 * ch:512 * (ch + 1)],
                       waits=w, ev=f"kin_{g}_{ch}_{k}", sem=f"kin{sl}")
            tsl2 = ch % 4
            w = rings.acquire(("trig", tsl2))
            sc.dma("gpsimd", trig_r[:, 1024 * tsl2:1024 * tsl2 + 512],
                   din["trig_c"][:, 512 * ch:512 * (ch + 1)], waits=w,
                   ev=f"tcd_{g}_{ch}", sem=f"trg{tsl2}")
            sc.dma("gpsimd", trig_r[:, 1024 * tsl2 + 512:1024 * (tsl2 + 1)],
                   din["trig_s"][:, 512 * ch:512 * (ch + 1)],
                   ev=f"tsd_{g}_{ch}", sem=f"trg{tsl2}")

        chunk_dmas(0)
        yield
        chunk_dmas(1)
        yield
        for ch in range(16):
            tsl = ch % 4
            if ch + 2 < 16:
                chunk_dmas(ch + 2)
            yield
            s, sw = take_turn(f"kt_{g}_{ch}")

            def ckv_rhs(kk, lo, sz, c2=ch):
                if g == 0:
                    return ckv_sb(c2)[:, 512 * kk + lo:512 * kk + lo + sz]
                sl2 = (4 * c2 + kk) % 12
                return ckv_in[:, 512 * sl2 + lo:512 * sl2 + lo + sz]

            for k in range(4):
                wt = []
                if k == 0:
                    wt = list(sw)
                    wt += ([f"ckvE_{ch}_3"] if g == 0 else
                           [f"kin_{g}_{ch}_{k2}" for k2 in range(4)])
                    if ch == 0:
                        wt = wt + [f"wuc_{g}_2"]
                fn = (lambda kk, ss, rhs: lambda e: e.matmul(
                    bank(2 * ss), wu_k(kk), rhs,
                    start=(kk == 0), stop=(kk == 3)))(k, s, ckv_rhs(k, 0, 512))
                sc.emit("tensor", fn, waits=wt,
                        inc=("P", 1) if k == 3 else None,
                        ev=f"kprj_{g}_{ch}" if k == 3 else None)
                if k == 1:
                    yield "cont"
            asl = ch % 2
            wA = rings.acquire(("ab", asl)) + \
                [f"kprj_{g}_{ch}", f"tcd_{g}_{ch}", f"tsd_{g}_{ch}"]
            fn = (lambda a2, t3, ss: lambda e: e.tensor_tensor(
                ab_buf[:, 1024 * a2:1024 * a2 + 512], bank(2 * ss),
                trig_r[:, 1024 * t3:1024 * t3 + 512], ALU.mult))(asl, tsl, s)
            sc.emit("vector", fn, waits=wA, inc=("V", 1))
            fn = (lambda a2, t3, ss: lambda e: e.tensor_tensor(
                ab_buf[:, 1024 * a2 + 512:1024 * (a2 + 1)], bank(2 * ss),
                trig_r[:, 1024 * t3 + 512:1024 * (t3 + 1)],
                ALU.mult))(asl, tsl, s)
            sc.emit("vector", fn, inc=("V", 1), ev=f"kab_{g}_{ch}")
            rings.release(("trig", tsl), f"kab_{g}_{ch}")
            fn = (lambda a2, ss: lambda e: e.matmul(
                bank(2 * ss + 1), mrot[:, :],
                ab_buf[:, 1024 * a2 + 512:1024 * (a2 + 1)],
                start=True, stop=True))(asl, s)
            sc.emit("tensor", fn, waits=[f"kab_{g}_{ch}"], inc=("P", 1),
                    ev=f"krot_{g}_{ch}")
            ksl = ch % 2
            wkt = rings.acquire(("kts", ksl)) + [f"krot_{g}_{ch}"]
            fn = (lambda a2, ss, kl: lambda e: e.tensor_tensor(
                kts[:, 512 * kl:512 * (kl + 1)], bank(2 * ss + 1),
                ab_buf[:, 1024 * a2:1024 * a2 + 512], ALU.add))(asl, s, ksl)
            sc.emit("vector", fn, waits=wkt, inc=("V", 1), ev=f"kt_{g}_{ch}")
            rings.release(("ab", asl), f"kt_{g}_{ch}")
            w = [f"kt_{g}_{ch}"] + (war if ch == 0 else [])
            sc.dma("gpsimd",
                   kt_h[128 * par:128 * (par + 1), 512 * ch:512 * (ch + 1)],
                   kts[:, 512 * ksl:512 * (ksl + 1)], waits=w,
                   ev=f"kto_{g}_{ch}", sem=f"kto{ksl}")
            rings.release(("kts", ksl), f"kto_{g}_{ch}")
            yield
            vsl = ch % 2
            wv0 = rings.acquire(("vss", vsl))
            for half in range(2):
                s, sw = take_turn(f"vse_{g}_{ch}_{half}")
                for mj2 in range(2):
                    mj = 2 * half + mj2
                    kt2 = 4 * ch + mj
                    for k in range(4):
                        wt = []
                        if k == 0:
                            wt = list(sw) if mj2 == 0 else []
                            if half == 0 and mj2 == 0:
                                wt = wt + wv0
                        fn = (lambda kk, bb, rhs: lambda e: e.matmul(
                            ps[:, 512 * bb:512 * bb + 128], rhs,
                            wu_v(kk), start=(kk == 0), stop=(kk == 3)))(
                                k, 2 * s + mj2, ckv_rhs(k, 128 * mj, 128))
                        sc.emit("tensor", fn, waits=wt,
                                inc=("P", 1) if (k == 3 and mj2 == 1) else None,
                                ev=f"vprj_{g}_{ch}_{half}"
                                if (k == 3 and mj2 == 1) else None)
                for mj2 in range(2):
                    mj = 2 * half + mj2
                    fn = (lambda mj3, bb, vl: lambda e: e.tensor_copy(
                        vss[:, 512 * vl + 128 * mj3:512 * vl + 128 * (mj3 + 1)],
                        ps[:, 512 * bb:512 * bb + 128]))(mj, 2 * s + mj2, vsl)
                    sc.emit("vector", fn,
                            waits=[f"vprj_{g}_{ch}_{half}"] if mj2 == 0 else [],
                            inc=("V", 1),
                            ev=f"vse_{g}_{ch}_{half}" if mj2 == 1 else None)
                yield
            VH = NK * 128
            srcA = bass.AP(sbt["vss"], 512 * vsl, [[1024, 128], [128, 4], [1, 64]])
            dstA = bass.AP(vs_h.tensor, (128 * par) * VH + 128 * (4 * ch),
                           [[VH, 128], [128, 4], [1, 64]])
            sc.dma("gpsimd", dstA, srcA, waits=[f"vse_{g}_{ch}_1"],
                   ev=f"vso_{g}_{ch}_A", sem=f"vso{vsl}")
            srcB = bass.AP(sbt["vss"], 512 * vsl + 64, [[1024, 128], [128, 4], [1, 64]])
            dstB = bass.AP(vs_h.tensor,
                           (128 * par) * VH + 128 * (4 * ch) + 64,
                           [[VH, 128], [128, 4], [1, 64]])
            sc.dma("gpsimd", dstB, srcB, waits=[f"vse_{g}_{ch}_1"],
                   ev=f"vso_{g}_{ch}_B", sem=f"vso{vsl}")
            rings.release(("vss", vsl), f"vso_{g}_{ch}_B")
        if g == 0:
            for _ in qbuild():
                yield
    PF = 12

    def emit_stream(g, idx, first_waits):
        par = g % 2
        kk = idx % NK
        sl = idx % 16
        VH = NK * 128
        w = rings.acquire(("vs", sl)) + list(first_waits) + \
            [f"vso_{g}_{kk // 4}_B"]
        if g == 0 and idx < 16:
            w.append("vsr_init")
        dstv = bass.AP(sbt["vs_r"], VST * sl, [[16 * VST, 128], [129, 2], [1, 64]])
        srcv = bass.AP(vs_h.tensor, (128 * par) * VH + 128 * kk,
                       [[VH, 128], [64, 2], [1, 64]])
        sc.dma("sync", dstv, srcv, waits=w, ev=f"vsd_{g}_{idx}",
               sem=f"vs{sl}")

    def emit_ktload(g, kk, first_waits=()):
        par = g % 2
        w = list(first_waits) + [f"kto_{g}_{kk // 4}"]
        if g >= 1:
            w.append(f"qk_{g - 1}_{3 * NK + kk}")
        if kk >= 16:
            w.append(f"qk_{g}_{kk - 16}")
        sc.dma("sync", kt_res[:, 128 * kk:128 * (kk + 1)],
               kt_h[128 * par:128 * (par + 1), 128 * kk:128 * (kk + 1)],
               waits=w, ev=f"ktr_{g}_{kk}", sem=f"ks{kk % 16}")

    def tail_pre_gen():
        for k in range(8):
            w = rings.acquire(("stage",)) + \
                (["gD_3_6", "gD_3_7"] if k == 0 else [])
            sc.dma("gpsimd", tl_wstage[:, 0:1024],
                   din["w_out_t"][128 * k:128 * (k + 1), :], waits=w,
                   ev=f"tlWl_{k}", sem="stg2")
            fn = (lambda kk: lambda e: e.tensor_copy(
                tl_wout[:, 1024 * kk:1024 * (kk + 1)],
                tl_wstage[:, 0:1024]))(k)
            sc.emit("vector", fn, waits=[f"tlWl_{k}"], inc=("V", 1),
                    ev=f"tlW_{k}")
            rings.release(("stage",), f"tlW_{k}")
            yield
        for n in range(2):
            for gg in range(NG - 1):
                w = [f"aod_{gg}_{n}"]
                if gg == 0:
                    w = w + rings.acquire(("tla", n % 2)) + \
                        ["gD_3_6", "gD_3_7"]
                sc.dma("gpsimd",
                       tl_attn_r(n)[:, 512 * gg:512 * (gg + 1)],
                       attn_h[128 * gg:128 * (gg + 1),
                              512 * n:512 * (n + 1)],
                       waits=w, ev=f"tlA_{n}_{gg}", sem=f"tla{n % 2}")
                if gg % 4 == 3:
                    yield
            for k in range(8):
                w = rings.acquire(("tlg", n % 2))
                if k == 0:
                    w = w + ["gD_3_6", "gD_3_7"]
                sc.dma("gpsimd", tl_gr(n)[:, 512 * k:512 * (k + 1)],
                       g_h[128 * k:128 * (k + 1), 512 * n:512 * (n + 1)],
                       waits=w, ev=f"tlG_{n}_{k}", sem=f"tlg{n % 2}")
                if k % 4 == 3:
                    yield
        for tt in range(2):
            w = rings.acquire(("tlx", tt % 2))
            if tt == 1:
                w = w + ["kprj_7_15", "vprj_7_15_1"]
            sc.dma("gpsimd", tlx_buf(tt),
                   din["xq"][128 * tt:128 * (tt + 1), :], waits=w,
                   ev=f"tlx_{tt}", sem=f"xlq{tt % 2}")
            yield

    ph1g = ph1_gen()
    b0 = build_gen(0)

    def drive_b0(n=1):
        for _ in range(n):
            try:
                next(b0)
            except StopIteration:
                return False
        return True

    def ph1_until(target):
        for tok in ph1g:
            if tok == ("E", target):
                return

    drive_b0(1)
    drive_b0(2)
    for ch in range(16):
        ph1_until(ch)
        drive_b0(5)
    drive_b0(4)
    for _ in ph1g:
        pass
    assert not drive_b0()

    pulls = []
    pull_i = [0]
    cont = [None]
    defq = []

    def pull():
        gen = cont[0]
        cont[0] = None
        if gen is None and defq:
            gen = defq[0]
        if gen is None and pulls:
            gen = pulls[pull_i[0] % len(pulls)]
        if gen is not None:
            try:
                if next(gen) == "cont":
                    cont[0] = gen
            except StopIteration:
                if gen in pulls:
                    pulls.remove(gen)
                if defq and gen is defq[0]:
                    defq.pop(0)
        pull_i[0] += 1

    for g in range(NG):
        TT = NCH * NK
        if g == 0:
            pulls.append(ffn_w_gen())
        if g == 1:
            pulls.append(ffn_gen())
        if g + 1 < NG:
            pulls.append(build_gen(g + 1))
        if g == NG - 1:
            pulls.append(tail_pre_gen())

        bld = [f"qt_{g}_3"]
        qk_meta = []

        def emit_qk(idx, g=g, qk_meta=qk_meta, bld=bld):
            c, k = divmod(idx, NK)
            s, sw = take_turn(f"ex_{g}_{idx}")
            qk_meta.append((s, idx))
            wt = list(sw)
            if idx < NK:
                wt.append(f"ktr_{g}_{k}")
            if idx == 0:
                wt = wt + bld
            fn = (lambda cc2, ss, kk2, par2: lambda e: e.matmul(
                bank(2 * ss), kt_res[0:64, 128 * kk2:128 * (kk2 + 1)],
                qt_d[0:64, NQ * par2 + 512 * cc2:NQ * par2 + 512 * (cc2 + 1)],
                start=True, stop=True))(c, s, k, g % 2)
            sc.emit("tensor", fn, waits=wt)
            fn = (lambda cc2, ss, kk2, par2: lambda e: e.matmul(
                bank(2 * ss + 1), kt_res[64:128, 128 * kk2:128 * (kk2 + 1)],
                qt_d[64:128, NQ * par2 + 512 * cc2:NQ * par2 + 512 * (cc2 + 1)],
                start=True, stop=True))(c, s, k, g % 2)
            sc.emit("tensor", fn, inc=("QK", 1), ev=f"qk_{g}_{idx}")

        def emit_exp(idx, g=g, qk_meta=qk_meta):
            s = qk_meta[idx][0]
            p3 = idx % 3
            wt = [f"qk_{g}_{idx}"]
            fn = (lambda ss, pp: lambda e: e.activation(
                probs[:, 1024 * pp:1024 * (pp + 1)],
                ps[:, 1024 * ss:1024 * (ss + 1)], AF.Exp, scale=0.125))(s, p3)
            sc.emit("scalar", fn, waits=wt, inc=("EX", 1), ev=f"ex_{g}_{idx}")
            last_ex_ev[0] = f"ex_{g}_{idx}"

        def emit_av(idx, g=g):
            c, k = divmod(idx, NK)
            p3 = idx % 3
            sl = idx % 16
            wt = [f"ex_{g}_{idx}", f"vsd_{g}_{idx}"]
            if k == 0:
                wt = wt + banks.acquire(6) + banks.acquire(7)
            fn = (lambda sll, pp, kk2: lambda e: e.matmul(
                bank(6)[0:65, :], statA_ap(sll),
                probs[:, 1024 * pp:1024 * pp + 512],
                start=(kk2 == 0), stop=(kk2 == NK - 1)))(sl, p3, k)
            sc.emit("tensor", fn, waits=wt)
            fn = (lambda sll, pp, kk2: lambda e: e.matmul(
                bank(7), statB_ap(sll),
                probs[:, 1024 * pp + 512:1024 * (pp + 1)],
                start=(kk2 == 0), stop=(kk2 == NK - 1)))(sl, p3, k)
            sc.emit("tensor", fn, inc=("AVT", 1))
            ev = f"avs_{g}_{c}" if k == NK - 1 else None
            if ev:
                sc.events[ev] = ("AVT", sc.cnt["AVT"])
            rings.release(("vs", sl), ("#AVT", sc.cnt["AVT"]))
            if k == NK - 1:
                ec = f"avs_{g}_{c}"
                sc.emit("vector", (lambda: lambda e: e.tensor_copy(
                    av_sb[0:65, 0:512], bank(6)[0:65, :]))(), waits=[ec],
                    inc=("V", 1))
                sc.emit("vector", (lambda: lambda e: e.tensor_copy(
                    av_sb[:, 512:1024], bank(7)))(), inc=("V", 1),
                    ev=f"avc_{g}_{c}")
                banks.release(6, f"avc_{g}_{c}")
                banks.release(7, f"avc_{g}_{c}")
                defq.append(epi_gen(c, g))

        def epi_gen(c, g=g):
            sc.emit("scalar", (lambda: lambda e: e.activation(
                rec_sb[64:65, 0:512], av_sb[64:65, 0:512], AF.Ln))(),
                waits=[f"avc_{g}_{c}"], inc=("A", 1))
            sc.emit("scalar", (lambda: lambda e: e.activation(
                rec_sb[64:65, 512:1024], rec_sb[64:65, 0:512], AF.Exp,
                scale=-1.0))(), inc=("A", 1), ev=f"rcpA_{g}_{c}")
            yield
            sc.emit("scalar", (lambda: lambda e: e.activation(
                rec_sb[32:33, 0:512], av_sb[32:33, 512:1024], AF.Ln))(),
                inc=("A", 1))
            sc.emit("scalar", (lambda: lambda e: e.activation(
                rec_sb[32:33, 512:1024], rec_sb[32:33, 0:512], AF.Exp,
                scale=-1.0))(), inc=("A", 1), ev=f"rcp_{g}_{c}")
            yield
            s, sw = take_turn(f"bce_{g}_{c}")
            sc.emit("tensor", (lambda ss: lambda e: e.matmul(
                bank(2 * ss), bmask[64:65, 0:128], rec_sb[64:65, 512:1024],
                start=True, stop=True))(s), waits=list(sw) + [f"rcpA_{g}_{c}"])
            sc.emit("tensor", (lambda ss: lambda e: e.matmul(
                bank(2 * ss + 1), bmask[32:33, 0:128],
                rec_sb[32:33, 512:1024], start=True, stop=True))(s),
                waits=[f"rcp_{g}_{c}"], inc=("P", 1), ev=f"bcm_{g}_{c}")
            sc.emit("vector", (lambda ss: lambda e: e.tensor_copy(
                bc_sb[0:64, :], bank(2 * ss)[0:64, :]))(s),
                waits=[f"bcm_{g}_{c}"], inc=("V", 1))
            sc.emit("vector", (lambda ss: lambda e: e.tensor_copy(
                bc_sb[64:128, :], bank(2 * ss + 1)[64:128, :]))(s),
                inc=("V", 1), ev=f"bce_{g}_{c}")
            yield
            aosl = c % 2
            w7 = rings.acquire(("ao", aosl))
            sc.emit("vector", (lambda a2: lambda e: e.tensor_tensor(
                ao_sb[0:64, 512 * a2:512 * (a2 + 1)], av_sb[0:64, 0:512],
                bc_sb[0:64, :], ALU.mult))(aosl),
                waits=[f"bce_{g}_{c}"] + w7, inc=("V", 1))
            sc.emit("vector", (lambda a2: lambda e: e.tensor_tensor(
                ao_sb[64:128, 512 * a2:512 * (a2 + 1)],
                av_sb[64:128, 512:1024],
                bc_sb[64:128, :], ALU.mult))(aosl),
                inc=("V", 1), ev=f"ep_{g}_{c}")
            sc.dma("gpsimd",
                   attn_h[128 * g:128 * (g + 1), 512 * c:512 * (c + 1)],
                   ao_sb[:, 512 * aosl:512 * (aosl + 1)],
                   waits=[f"ep_{g}_{c}"], ev=f"aod_{g}_{c}", sem=f"ao{aosl}")
            rings.release(("ao", aosl), f"aod_{g}_{c}")

        for kk in range(16):
            emit_ktload(g, kk)
        for pidx in range(PF):
            emit_stream(g, pidx, [])
        emit_qk(0)
        emit_qk(1)
        for idx in range(TT):
            emit_exp(idx)
            pull()
            if idx < NK - 16:
                emit_ktload(g, idx + 16)
            if idx + PF < TT:
                emit_stream(g, idx + PF, [])
            if idx + 2 < TT:
                emit_qk(idx + 2)
            emit_av(idx)
        rings.release(("qtd", g % 2), f"qk_{g}_{TT - 1}")

    while pulls or defq or cont[0] is not None:
        pull()
    for s in range(3):
        if s in slot_free:
            banks.release(2 * s, slot_free[s])
            banks.release(2 * s + 1, slot_free[s])

    for n in range(2):
        sc.dma("gpsimd", tl_attn_r(n)[:, 512 * (NG - 1):512 * NG],
               attn_h[128 * (NG - 1):128 * NG, 512 * n:512 * (n + 1)],
               waits=[f"aod_{NG - 1}_{n}"], ev=f"tlA_{n}_{NG - 1}",
               sem=f"tla{n % 2}")
    for n in range(2, NCH):
        for gg in range(NG):
            w = [f"aod_{gg}_{n}"]
            if n == 2 and gg == 0:
                w = w + [f"qk_{NG - 1}_{NCH * NK - 1}"]
            sc.dma("gpsimd",
                   kt_res[:, 4096 * (n - 2) + 512 * gg:
                          4096 * (n - 2) + 512 * (gg + 1)],
                   attn_h[128 * gg:128 * (gg + 1), 512 * n:512 * (n + 1)],
                   waits=w, ev=f"tlA_{n}_{gg}", sem=f"tla{n % 2}")
    for n in range(NCH):
        if n >= 2:
            for k in range(8):
                w = rings.acquire(("tlg", n % 2))
                sc.dma("gpsimd",
                       tl_gr(n)[:, 512 * k:512 * (k + 1)],
                       g_h[128 * k:128 * (k + 1), 512 * n:512 * (n + 1)],
                       waits=w, ev=f"tlG_{n}_{k}", sem=f"tlg{n % 2}")
        for m in range(8):
            for k in range(8):
                wt = []
                if k == 0:
                    wt = banks.acquire(m % 2)
                    if m <= 1:
                        wt = wt + ["tlW_7"] + \
                            [f"tlA_{n}_{gg}" for gg in range(NG)]
                if n < 2:
                    rhs_a = tl_attn_r(n)[:, 512 * k:512 * (k + 1)]
                else:
                    rhs_a = kt_res[:, 4096 * (n - 2) + 512 * k:
                                   4096 * (n - 2) + 512 * (k + 1)]
                fn = (lambda mm, kk, rr: lambda e: e.matmul(
                    bank(mm % 2), tl_wout[:, 1024 * kk + 128 * mm:
                                          1024 * kk + 128 * (mm + 1)],
                    rr, start=(kk == 0), stop=(kk == 7)))(m, k, rhs_a)
                sc.emit("tensor", fn, waits=wt,
                        inc=("P", 1) if k == 7 else None,
                        ev=f"wo_{n}_{m}" if k == 7 else None)
            w8 = [f"wo_{n}_{m}"]
            if m == 0:
                w8 = w8 + rings.acquire(("sT", n % 2))
            fn = (lambda nn, mm: lambda e: e.tensor_copy(
                tl_sT(nn)[:, 512 * mm:512 * (mm + 1)], bank(mm % 2)))(n, m)
            sc.emit("vector", fn, waits=w8, inc=("V", 1), ev=f"woE_{n}_{m}")
            banks.release(m % 2, f"woE_{n}_{m}")
        for j in range(4):
            tt = 4 * n + j
            for kk in range(8):
                wt = []
                if kk == 0:
                    wt = banks.acquire(2) + banks.acquire(3) + \
                        [f"woE_{n}_{mm}" for mm in range(8)]
                fn = (lambda nn, jj, k2: lambda e: e.transpose(
                    ps[:, 1024 + 128 * k2:1024 + 128 * (k2 + 1)],
                    tl_sT(nn)[:, 512 * k2 + 128 * jj:
                              512 * k2 + 128 * (jj + 1)],
                    ident_f[:, :]))(n, j, kk)
                sc.emit("tensor", fn, waits=wt,
                        inc=("P", 1) if kk == 7 else None,
                        ev=f"wT_{tt}" if kk == 7 else None)
            gb = (4, 5) if tt % 2 == 0 else (6, 7)
            gbase = 2048 if tt % 2 == 0 else 3072
            for kk in range(8):
                wt = []
                if kk == 0:
                    wt = banks.acquire(gb[0]) + banks.acquire(gb[1]) + \
                        [f"tlG_{n}_{k2}" for k2 in range(8)]
                fn = (lambda nn, jj, k2, gp: lambda e: e.transpose(
                    ps[:, gp + 128 * k2:gp + 128 * (k2 + 1)],
                    tl_gr(nn)[:, 512 * k2 + 128 * jj:
                              512 * k2 + 128 * (jj + 1)],
                    ident_f[:, :]))(n, j, kk, gbase)
                sc.emit("tensor", fn, waits=wt,
                        inc=("P", 1) if kk == 7 else None,
                        ev=f"gT_{tt}" if kk == 7 else None)
            if tt >= 2:
                w9 = rings.acquire(("tlx", tt % 2))
                sc.dma("gpsimd", tlx_buf(tt),
                       din["xq"][128 * tt:128 * (tt + 1), :], waits=w9,
                       ev=f"tlx_{tt}", sem=f"xlq{tt % 2}")
            fn = (lambda t2, xb: lambda e: e.tensor_scalar(
                tl_s1[:, :], xb, st_nm1[:, t2:t2 + 1],
                st_r1[:, t2:t2 + 1], ALU.add, ALU.mult))(tt, tlx_buf(tt))
            sc.emit("vector", fn, waits=[f"tlx_{tt}"], inc=("V", 1))
            wsa = [f"wT_{tt}"] + rings.acquire(("tls2", tt % 2))
            if tt == 1:
                wsa = wsa + ["kprj_7_15", "vprj_7_15_1"]
            fn = (lambda sb: lambda e: e.tensor_tensor(
                sb, ps[:, 1024:2048], tl_s1[:, :], ALU.add))(tls2_buf(tt))
            sc.emit("vector", fn, waits=wsa, inc=("V", 1),
                    ev=f"sA_{tt}")
            banks.release(2, f"sA_{tt}")
            banks.release(3, f"sA_{tt}")
            fn = (lambda sb, gp: lambda e: e.tensor_tensor(
                tl_s1[:, :], gp, sb, ALU.mult))(tls2_buf(tt), gT_ps(tt))
            sc.emit("vector", fn, waits=[f"gT_{tt}"], inc=("V", 1),
                    ev=f"sB_{tt}")
            banks.release(gb[0], f"sB_{tt}")
            banks.release(gb[1], f"sB_{tt}")
            fn = (lambda sb, xb: lambda e: e.tensor_tensor(
                sb, tl_s1[:, :], xb, ALU.add))(tls2_buf(tt), tlx_buf(tt))
            sc.emit("vector", fn, inc=("V", 1), ev=f"cmb_{tt}")
            rings.release(("tlx", tt % 2), f"cmb_{tt}")
            sc.dma("sync",
                   out_d[128 * tt:128 * (tt + 1), :], tls2_buf(tt),
                   waits=[f"cmb_{tt}"], ev=f"out_{tt}", sem="outd")
            rings.release(("tls2", tt % 2), f"out_{tt}")
        rings.release(("sT", n % 2), f"wT_{4 * n + 3}")
        rings.release(("tlg", n % 2), f"gT_{4 * n + 3}")

    sc.emit("gpsimd", lambda e: None, waits=["out_15"])

    with nc.Block() as block:
        @block.sync
        def _(eng):
            sc.replay("sync", eng, semobjs)

        @block.gpsimd
        def _(eng):
            sc.replay("gpsimd", eng, semobjs)

        @block.vector
        def _(eng):
            sc.replay("vector", eng, semobjs)

        @block.scalar
        def _(eng):
            sc.replay("scalar", eng, semobjs)

        @block.tensor
        def _(eng):
            sc.replay("tensor", eng, semobjs)

    ctx.close()
    return nc


_NC_CACHE = []


def kernel(**inputs):
    in_maps = _host_prepare(inputs)
    if not _NC_CACHE:
        _NC_CACHE.append(build_nc())
    nc = _NC_CACHE[0]
    res = run_bass_kernel_spmd(nc, in_maps, core_ids=list(range(NCORE)))
    out = np.empty((B, S, D), dtype=np.float32)
    for c in range(NCORE):
        bb = c // 4
        q0 = (c % 4) * NQ
        out[bb, q0:q0 + NQ] = res.results[c]["out"]
    return out
